# version 02
# baseline (speedup 1.0000x reference)
import math
import numpy as np
import ml_dtypes
import concourse.bass as bass
import concourse.mybir as mybir
from concourse.bass_utils import run_bass_kernel_spmd
from contextlib import ExitStack

F32 = mybir.dt.float32
BF16 = mybir.dt.bfloat16
I32 = mybir.dt.int32
AF = mybir.ActivationFunctionType
ALU = mybir.AluOpType

B, S, H = 4, 2048, 4096
NH, HD = 32, 128
INTER = 11008
NTOK = 1024
NKT = 32
NST = 16
SCALE = HD ** -0.5
EPS = 1e-5
NEG = -30000.0
NPART = 8
GRPS = [(0, 2, 256), (2, 2, 256), (4, 2, 256), (6, 2, 256), (8, 2, 256), (10, 1, 96)]
SLAB_ROWS = [128] * 10 + [96]

_DEBUG_TAPS = []
_STOP_AFTER = None


class _StopBuild(Exception):
    pass


def _alibi_slopes(n):
    cp = 2 ** math.floor(math.log2(n))
    base = 2.0 ** (-(2.0 ** (-(math.log2(cp) - 3))))
    sl = [base ** i for i in range(1, cp + 1)]
    if cp != n:
        eb = 2.0 ** (-(2.0 ** (-(math.log2(2 * cp) - 3))))
        ne = min(cp, n - cp)
        sl += [eb ** i for i in range(1, 2 * ne + 1, 2)]
    return np.asarray(sl, dtype=np.float32)


def _perm(n_rows):
    return (8 * np.arange(n_rows)[None, :] + np.arange(8)[:, None]).reshape(-1)


PERM_H = _perm(512)
PERM_I = _perm(1376)
STRIPS_A = [s for s in range(32) if s % 4 in (0, 3)]
STRIPS_B = [s for s in range(32) if s % 4 in (1, 2)]


def att_chunks(j):
    qs = 64 * j
    if qs < 512:
        return [(qs, 512 - qs), (512, 512)]
    return [(qs, 1024 - qs)]



ENGS = ("sy", "gp", "pe", "ve", "ac")


class Op:
    __slots__ = ("eng", "fn", "waits", "is_dma", "marked", "dmakey", "seq", "isbar")

    def __init__(self, eng, fn, is_dma):
        self.eng, self.fn, self.is_dma = eng, fn, is_dma
        self.waits, self.marked, self.dmakey = [], False, None
        self.seq, self.isbar = 0, False


class Sched:
    def __init__(self):
        self.ops = {e: [] for e in ENGS}
        self.writers = {}
        self.readers = {}
        self.seqc = 0

    def barrier(self):
        for e in ENGS:
            o = self.op(e, lambda eng: eng.nop())
            o.isbar = True

    def _edge(self, a, b):
        if a is b or (a.eng == b.eng and not a.is_dma):
            return
        a.marked = True
        b.waits.append(a)

    def op(self, eng, fn, reads=(), writes=(), after=(), is_dma=False):
        o = Op(eng, fn, is_dma)
        if is_dma:
            o.marked = True
            o.dmakey = writes[0] if writes else None
        for a in after:
            if a is not None:
                self._edge(a, o)
        for bk in reads:
            for w in self.writers.get(bk, {}).values():
                self._edge(w, o)
        for bk in writes:
            for w in self.writers.get(bk, {}).values():
                self._edge(w, o)
            for r in self.readers.get(bk, {}).values():
                self._edge(r, o)
        for bk in reads:
            self.readers.setdefault(bk, {})[o.eng] = o
        for bk in writes:
            self.writers[bk] = {o.eng: o}
            self.readers[bk] = {}
        self.seqc += 1
        o.seq = self.seqc
        self.ops[eng].append(o)
        return o

    def dma(self, q, out_ap, in_ap, reads=(), writes=(), after=()):
        return self.op(q, (lambda e, o=out_ap, i=in_ap: e.dma_start(out=o, in_=i)),
                       reads=reads, writes=writes, after=after, is_dma=True)

    def snap(self, name):
        if not hasattr(self, "snaps"):
            self.snaps = {}
        self.snaps[name] = {e: len(self.ops[e]) for e in ENGS}

    def emit(self, nc, block, sems, mksem, prefix=None):
        lens = self.snaps[prefix] if prefix else {e: len(self.ops[e]) for e in ENGS}
        def dsem(o):
            k = o.dmakey
            if k is None or k.startswith(("ysp", "outd", "tap_")):
                return "bulk_" + o.eng
            return "d_" + k

        val = {}
        semname = {}
        tot = {}
        cnt = {}
        events = []
        for e in ENGS:
            for o in self.ops[e][:lens[e]]:
                if o.isbar:
                    o.marked = True
                if not o.marked:
                    continue
                sn = dsem(o) if o.is_dma else e
                semname[id(o)] = sn
                cnt[sn] = cnt.get(sn, 0) + (16 if o.is_dma else 1)
                val[id(o)] = cnt[sn]
                tot.setdefault(e, {})[sn] = cnt[sn]
                events.append((o.seq, sn, cnt[sn]))
        for sn in cnt:
            if sn not in sems:
                sems[sn] = mksem(sn)
        events.sort()
        eseqs = [ev[0] for ev in events]
        import bisect

        def snapshot(seq):
            snap = {}
            for i in range(bisect.bisect_left(eseqs, seq)):
                _, sn, v = events[i]
                snap[sn] = max(snap.get(sn, 0), v)
            return snap

        def mk(ename):
            def body(eng):
                cache = {}
                for o in self.ops[ename][:lens[ename]]:
                    if o.isbar:
                        for sn, v in snapshot(o.seq).items():
                            if cache.get(sn, 0) < v:
                                eng.wait_ge(sems[sn], v)
                                cache[sn] = v
                    for a in o.waits:
                        v = val.get(id(a))
                        if v is None:
                            continue
                        sn = semname[id(a)]
                        if cache.get(sn, 0) < v:
                            eng.wait_ge(sems[sn], v)
                            cache[sn] = v
                    instr = o.fn(eng)
                    if o.marked:
                        instr.then_inc(sems[semname[id(o)]], 16 if o.is_dma else 1)
                if ename in ("sy", "gp"):
                    for sn, v in tot.get(ename, {}).items():
                        if cache.get(sn, 0) < v:
                            eng.wait_ge(sems[sn], v)
                            cache[sn] = v
            return body

        block.sync(mk("sy"))
        block.gpsimd(mk("gp"))
        block.tensor(mk("pe"))
        block.vector(mk("ve"))
        block.scalar(mk("ac"))



def build_program():
    nc = bass.Bass("TRN2", target_bir_lowering=False)
    sc = Sched()
    es_top = ExitStack()
    taps = {}

    def din(name, shape, dtype):
        return nc.dram_tensor(name, shape, dtype, kind="ExternalInput").ap()

    xp_d = din("xp", [H, NTOK], F32)
    xt_d = din("xt", [H, S], F32)
    wq_d = din("wq", [NH, NKT, 128, 128], F32)
    wk_d = din("wk", [NKT, 128, 128], F32)
    wv_d = din("wv", [NKT, 128, 128], F32)
    wo_d = din("wo", [NKT, NKT, 128, 128], F32)
    qg_d = din("qg", [512, INTER], I32)
    qu_d = din("qu", [512, INTER], I32)
    qd_d = din("qd", [1376, H], I32)
    sg_d = din("sg", [32, INTER], F32)
    su_d = din("su", [32, INTER], F32)
    sd_d = din("sd", [86, H], F32)
    zg_d = din("zg", [32, 1376], I32)
    zu_d = din("zu", [32, 1376], I32)
    zd_d = din("zd", [86, 512], I32)
    gg_d = din("gg", [H], I32)
    gu_d = din("gu", [H], I32)
    gd_d = din("gd", [INTER], I32)
    amask_d = din("amask", [128, NST * 64], BF16)
    abias_d = din("abias", [128, NH * NST], F32)
    aqsl_d = din("aqsl", [NH, NTOK], BF16)
    idn_d = din("idn", [128, 128], BF16)
    onesr_d = din("onesr", [1, 128], BF16)
    onesc_d = din("onesc", [128, 1], BF16)
    onesrf_d = din("onesrf", [1, 128], F32)
    out_d = nc.dram_tensor("out", [H, NTOK], F32, kind="ExternalOutput").ap()
    ysp_d = nc.dram_tensor("yspill", [H, NTOK], F32, kind="Internal").ap()

    def mksb(stack, name, shape, dtype):
        t = stack.enter_context(nc.sbuf_tensor(name, shape, dtype))
        allsb[name] = t
        taps[name] = t
        return t

    allsb = {}

    def phase(name):
        for tname in _DEBUG_TAPS.get(name, []) if isinstance(_DEBUG_TAPS, dict) else []:
            t = allsb[tname]
            sh = list(t.shape)
            td = nc.dram_tensor(f"tap_{name}_{tname}", sh, t.dtype,
                                kind="ExternalOutput").ap()
            sc.dma("sy", td, t[:], reads=[tname, tname + "0", tname + "1"])
        sc.snap(name)

    pA = es_top.enter_context(nc.psum_tensor("pA", [128, 1024], F32))
    pB = es_top.enter_context(nc.psum_tensor("pB", [128, 1024], F32))
    pC1 = es_top.enter_context(nc.psum_tensor("pC1", [128, 512], F32))
    pC2 = es_top.enter_context(nc.psum_tensor("pC2", [128, 512], F32))
    pD = es_top.enter_context(nc.psum_tensor("pD", [128, 512], F32))
    pE = es_top.enter_context(nc.psum_tensor("pE", [128, 1024], BF16))
    PC = [pC1, pC2]

    g = ExitStack()
    xP = mksb(g, "xP", [128, NKT * 1024], BF16)
    ones_r = mksb(g, "ones_r", [1, 128], BF16)
    ones_c = mksb(g, "ones_c", [128, 1], BF16)
    ones_rf = mksb(g, "ones_rf", [1, 128], F32)
    idn_sb = mksb(g, "idn_sb", [128, 128], BF16)

    sc.dma("sy", idn_sb[:], idn_d, writes=["idn"])
    sc.dma("sy", ones_r[:], onesr_d, writes=["ones_r"])
    sc.dma("sy", ones_c[:], onesc_d, writes=["ones_c"])
    sc.dma("sy", ones_rf[:], onesrf_d, writes=["ones_rf"])

    att = ExitStack()
    ktb = mksb(att, "ktb", [128, S], BF16)
    vtok = mksb(att, "vtok", [128, S], BF16)
    qhat = mksb(att, "qhat", [128, 1024 * 2], BF16)
    probs = mksb(att, "probs", [128, 512 * 3], BF16)
    wstg = mksb(att, "wstg", [128, 1024 * 2], F32)
    wbf = mksb(att, "wbf", [128, 4096 * 2], BF16)
    sumr = mksb(att, "sumr", [1, 1024], F32)
    sumx = mksb(att, "sumx", [32, 1024], F32)
    rsx = mksb(att, "rsx", [32, 1024], F32)
    rsxb = mksb(att, "rsxb", [32, 1024], BF16)
    rs0 = mksb(att, "rs0", [1, 1024 * 2], BF16)
    aqs = mksb(att, "aqs", [1, 1024 * 2], BF16)
    amask = mksb(att, "amask_sb", [128, NST * 64], BF16)
    abias = mksb(att, "abias_sb", [128, NH * NST], F32)
    p1 = ExitStack()
    xstg = mksb(p1, "xstg", [128, 1024 * 2], F32)
    xtb = mksb(p1, "xtb", [128, 1024 * 2], BF16)
    wkb = mksb(p1, "wkb", [128, 4096], BF16)
    wvb = mksb(p1, "wvb", [128, 4096], BF16)

    sc.dma("sy", amask[:], amask_d, writes=["amask"])
    sc.dma("sy", abias[:], abias_d, writes=["abias"])
    phase("p0")

    for kt in range(NKT):
        sl = kt % 2
        sc.dma("sy", xstg[:, sl * 1024:sl * 1024 + 1024],
               xp_d[kt * 128:kt * 128 + 128, :], writes=[f"xstg{sl}"])
        sc.op("ve", lambda e, kt=kt, sl=sl: e.tensor_copy(
            out=xP[:, kt * 1024:(kt + 1) * 1024],
            in_=xstg[:, sl * 1024:sl * 1024 + 1024]),
            reads=[f"xstg{sl}"], writes=["xP"])

    phase("px")
    for t, (wsrc, wdst, key) in enumerate([(wk_d, wkb, "wkb"), (wv_d, wvb, "wvb")]):
        for q4 in range(4):
            sl = q4 % 2
            sc.dma("sy", xstg[:, sl * 1024:sl * 1024 + 1024].rearrange(
                       "p (t c) -> p t c", c=128),
                   wsrc[q4 * 8:(q4 + 1) * 8].rearrange("t p c -> p t c"),
                   writes=[f"xstg{sl}"])
            sc.op("ve", lambda e, q4=q4, sl=sl, wdst=wdst: e.tensor_copy(
                out=wdst[:, q4 * 1024:(q4 + 1) * 1024],
                in_=xstg[:, sl * 1024:sl * 1024 + 1024]),
                reads=[f"xstg{sl}"], writes=[key])

    phase("pw")
    for sh in range(2):
        for kt in range(NKT):
            sl = kt % 2
            sc.dma("sy", xstg[:, sl * 1024:sl * 1024 + 1024],
                   xt_d[kt * 128:kt * 128 + 128, sh * 1024:sh * 1024 + 1024],
                   writes=[f"xstg{sl}"])
            sc.op("ve", lambda e, sl=sl: e.tensor_copy(
                out=xtb[:, sl * 1024:sl * 1024 + 1024],
                in_=xstg[:, sl * 1024:sl * 1024 + 1024]),
                reads=[f"xstg{sl}"], writes=[f"xtb{sl}"])
            for c in range(2):
                sc.op("pe", lambda e, kt=kt, sl=sl, c=c: e.matmul(
                    pA[:, c * 512:(c + 1) * 512],
                    wkb[:, kt * 128:(kt + 1) * 128],
                    xtb[:, sl * 1024 + c * 512:sl * 1024 + (c + 1) * 512],
                    start=(kt == 0), stop=(kt == NKT - 1)),
                    reads=[f"xtb{sl}", "wkb"], writes=[f"pA{c}"])
            for i in range(8):
                sc.op("pe", lambda e, kt=kt, sl=sl, i=i: e.matmul(
                    pB[:, i * 128:(i + 1) * 128],
                    xtb[:, sl * 1024 + i * 128:sl * 1024 + (i + 1) * 128],
                    wvb[:, kt * 128:(kt + 1) * 128],
                    start=(kt == 0 and i % 4 == 0), stop=(kt == NKT - 1)),
                    reads=[f"xtb{sl}", "wvb"], writes=[f"pB{i // 4}"])
        phase(f"kv{sh}mm")
        sc.op("ac", lambda e, sh=sh: e.activation(
            out=ktb[:, sh * 1024:(sh + 1) * 1024], in_=pA[:], func=AF.Copy),
            reads=["pA0", "pA1"], writes=["ktb"])
        sc.op("ac", lambda e, sh=sh: e.activation(
            out=vtok[:, sh * 1024:(sh + 1) * 1024], in_=pB[:], func=AF.Copy),
            reads=["pB0", "pB1"], writes=["vtok"])
        phase(f"kv{sh}ev")

    phase("p1")
    p1.close()
    avu = mksb(att, "avu", [128, NH * 1024], BF16)
    for h in range(NH):
        wsl = h % 2
        for q4 in range(4):
            ssl = q4 % 2
            sc.dma("sy", wstg[:, ssl * 1024:ssl * 1024 + 1024].rearrange(
                       "p (t c) -> p t c", c=128),
                   wq_d[h, q4 * 8:(q4 + 1) * 8].rearrange("t p c -> p t c"),
                   writes=[f"wstg{ssl}"])
            sc.op("ve", lambda e, q4=q4, ssl=ssl, wsl=wsl: e.tensor_copy(
                out=wbf[:, wsl * 4096 + q4 * 1024:wsl * 4096 + (q4 + 1) * 1024],
                in_=wstg[:, ssl * 1024:ssl * 1024 + 1024]),
                reads=[f"wstg{ssl}"], writes=[f"wbf{wsl}"])
        sc.dma("gp", aqs[0:1, wsl * 1024:wsl * 1024 + 1024], aqsl_d[h:h + 1, :],
               writes=[f"aqs{wsl}"])
        for kt in range(NKT):
            for c in range(2):
                sc.op("pe", lambda e, kt=kt, c=c, wsl=wsl: e.matmul(
                    pA[:, c * 512:(c + 1) * 512],
                    wbf[:, wsl * 4096 + kt * 128:wsl * 4096 + kt * 128 + 128],
                    xP[:, kt * 1024 + c * 512:kt * 1024 + (c + 1) * 512],
                    start=(kt == 0), stop=(kt == NKT - 1)),
                    reads=[f"wbf{wsl}", "xP"], writes=[f"pA{c}"])
        qsl = h % 2
        sc.op("ac", lambda e, qsl=qsl: e.activation(
            out=qhat[:, qsl * 1024:(qsl + 1) * 1024], in_=pA[:], func=AF.Copy,
            scale=SCALE), reads=["pA0", "pA1"], writes=[f"qhat{qsl}"])
        pi = 0
        for j in range(NST):
            chs = att_chunks(j)
            for ci, (off, w) in enumerate(chs):
                spi = pi % 2
                ppi = pi % 3
                skey = f"pC{spi + 1}"
                sc.op("pe", lambda e, j=j, off=off, w=w, spi=spi, qsl=qsl: e.matmul(
                    PC[spi][:, 0:w], ktb[:, j * 128:(j + 1) * 128],
                    qhat[:, qsl * 1024 + off:qsl * 1024 + off + w],
                    start=True, stop=False),
                    reads=["ktb", f"qhat{qsl}"], writes=[skey])
                sc.op("pe", lambda e, off=off, w=w, spi=spi, wsl=wsl: e.matmul(
                    PC[spi][:, 0:w], ones_r[0:1, :],
                    aqs[0:1, wsl * 1024 + off:wsl * 1024 + off + w],
                    start=False, stop=True),
                    reads=["ones_r", f"aqs{wsl}"], writes=[skey])
                if ci == 0:
                    sc.op("ve", lambda e, j=j, spi=spi: e.tensor_tensor(
                        out=PC[spi][:, 0:64], in0=PC[spi][:, 0:64],
                        in1=amask[:, j * 64:j * 64 + 64], op=ALU.add),
                        reads=[skey, "amask"], writes=[skey])
                sc.op("ac", lambda e, j=j, w=w, spi=spi, ppi=ppi, h=h: e.activation(
                    out=probs[:, ppi * 512:ppi * 512 + w], in_=PC[spi][:, 0:w],
                    func=AF.Exp, bias=abias[:, h * 16 + j:h * 16 + j + 1]),
                    reads=[skey, "abias"], writes=[f"probs{ppi}"])
                bank = off // 512
                stop_s = (j == 7 and ci == 0) or (j == 15)
                sc.op("pe", lambda e, off=off, w=w, ppi=ppi, j=j, stop_s=stop_s: e.matmul(
                    pA[0:1, off:off + w], ones_c[:, 0:1],
                    probs[:, ppi * 512:ppi * 512 + w],
                    start=(j == 0), stop=stop_s),
                    reads=["ones_c", f"probs{ppi}"], writes=[f"pA{bank}"])
                sc.op("pe", lambda e, off=off, w=w, ppi=ppi, j=j, stop_s=stop_s: e.matmul(
                    pB[:, off:off + w], vtok[:, j * 128:(j + 1) * 128],
                    probs[:, ppi * 512:ppi * 512 + w],
                    start=(j == 0), stop=stop_s),
                    reads=["vtok", f"probs{ppi}"], writes=[f"pB{bank}"])
                pi += 1
        sc.op("ac", lambda e, h=h: e.activation(
            out=avu[:, h * 1024:(h + 1) * 1024], in_=pB[:], func=AF.Copy),
            reads=["pB0", "pB1"], writes=["avu"])
        sc.op("ac", lambda e: e.activation(out=sumr[:], in_=pA[0:1, :], func=AF.Copy),
              reads=["pA0", "pA1"], writes=["sumr"])
        sc.dma("gp", sumx[h:h + 1, :], sumr[:], reads=["sumr"], writes=["sumx"])

    sc.op("ve", lambda e: e.reciprocal(out=rsx[:], in_=sumx[:]),
          reads=["sumx"], writes=["rsx"])
    sc.op("ve", lambda e: e.tensor_copy(out=rsxb[:], in_=rsx[:]),
          reads=["rsx"], writes=["rsxb"])
    for h in range(NH):
        sl = h % 2
        sc.dma("gp", rs0[0:1, sl * 1024:sl * 1024 + 1024], rsxb[h:h + 1, :],
               reads=["rsxb"], writes=[f"rs0{sl}"])
        for c in range(2):
            sc.op("pe", lambda e, sl=sl, c=c: e.matmul(
                pA[:, c * 512:(c + 1) * 512], ones_r[0:1, :],
                rs0[0:1, sl * 1024 + c * 512:sl * 1024 + (c + 1) * 512],
                start=True, stop=True),
                reads=[f"rs0{sl}", "ones_r"], writes=[f"pA{c}"])
        sc.op("ve", lambda e, h=h: e.tensor_tensor(
            out=avu[:, h * 1024:(h + 1) * 1024],
            in0=avu[:, h * 1024:(h + 1) * 1024], in1=pA[:], op=ALU.mult),
            reads=["pA0", "pA1", "avu"], writes=["avu"])

    phase("p2")
    for nt in range(NKT):
        wsl = nt % 2
        for q4 in range(4):
            ssl = q4 % 2
            sc.dma("sy", wstg[:, ssl * 1024:ssl * 1024 + 1024].rearrange(
                       "p (t c) -> p t c", c=128),
                   wo_d[nt, q4 * 8:(q4 + 1) * 8].rearrange("t p c -> p t c"),
                   writes=[f"wstg{ssl}"])
            sc.op("ve", lambda e, q4=q4, ssl=ssl, wsl=wsl: e.tensor_copy(
                out=wbf[:, wsl * 4096 + q4 * 1024:wsl * 4096 + (q4 + 1) * 1024],
                in_=wstg[:, ssl * 1024:ssl * 1024 + 1024]),
                reads=[f"wstg{ssl}"], writes=[f"wbf{wsl}"])
        pk = "pB" if nt % 2 else "pA"
        ps = pB if nt % 2 else pA
        for dt_ in range(NKT):
            for c in range(2):
                sc.op("pe", lambda e, dt_=dt_, c=c, wsl=wsl, ps=ps: e.matmul(
                    ps[:, c * 512:(c + 1) * 512],
                    wbf[:, wsl * 4096 + dt_ * 128:wsl * 4096 + dt_ * 128 + 128],
                    avu[:, dt_ * 1024 + c * 512:dt_ * 1024 + (c + 1) * 512],
                    start=(dt_ == 0), stop=(dt_ == NKT - 1)),
                    reads=[f"wbf{wsl}", "avu"], writes=[f"{pk}{c}"])
        sc.op("ve", lambda e, nt=nt, ps=ps: e.tensor_tensor(
            out=xP[:, nt * 1024:(nt + 1) * 1024],
            in0=ps[:], in1=xP[:, nt * 1024:(nt + 1) * 1024], op=ALU.add),
            reads=[f"{pk}0", f"{pk}1", "xP"], writes=["xP"])

    phase("p3")
    sc.barrier()
    att.close()
    ln = ExitStack()
    sqb = mksb(ln, "sqb", [128, 1024 * 2], BF16)
    scr = mksb(ln, "scr", [128, 1024 * 2], F32)
    rowm = mksb(ln, "rowm", [1, 1024], F32)
    rowa = mksb(ln, "rowa", [1, 1024], F32)
    rowb = mksb(ln, "rowb", [1, 1024], F32)

    def layernorm(dst_is_out, outst=None):
        for kt in range(NKT):
            sl = kt % 2
            sc.op("ac", lambda e, kt=kt, sl=sl: e.activation(
                out=sqb[:, sl * 1024:(sl + 1) * 1024],
                in_=xP[:, kt * 1024:(kt + 1) * 1024], func=AF.Square),
                reads=["xP"], writes=[f"sqb{sl}"])
            for c in range(2):
                sc.op("pe", lambda e, kt=kt, c=c: e.matmul(
                    pA[0:1, c * 512:(c + 1) * 512], ones_c[:, 0:1],
                    xP[:, kt * 1024 + c * 512:kt * 1024 + (c + 1) * 512],
                    start=(kt == 0), stop=(kt == NKT - 1)),
                    reads=["ones_c", "xP"], writes=[f"pA{c}"])
                sc.op("pe", lambda e, kt=kt, c=c, sl=sl: e.matmul(
                    pB[0:1, c * 512:(c + 1) * 512], ones_c[:, 0:1],
                    sqb[:, sl * 1024 + c * 512:sl * 1024 + (c + 1) * 512],
                    start=(kt == 0), stop=(kt == NKT - 1)),
                    reads=["ones_c", f"sqb{sl}"], writes=[f"pB{c}"])
        sc.op("ve", lambda e: e.tensor_scalar(
            out=rowm[:], in0=pA[0:1, :], scalar1=1.0 / H, scalar2=None, op0=ALU.mult),
            reads=["pA0", "pA1"], writes=["rowm"])
        sc.op("ve", lambda e: e.tensor_tensor(
            out=rowa[:], in0=rowm[:], in1=rowm[:], op=ALU.mult),
            reads=["rowm"], writes=["rowa"])
        sc.op("ve", lambda e: e.scalar_tensor_tensor(
            out=rowb[:], in0=pB[0:1, :], scalar=1.0 / H, in1=rowa[:],
            op0=ALU.mult, op1=ALU.subtract),
            reads=["pB0", "pB1", "rowa"], writes=["rowb"])
        sc.op("ve", lambda e: e.tensor_scalar(
            out=rowb[:], in0=rowb[:], scalar1=EPS, scalar2=None, op0=ALU.add),
            reads=["rowb"], writes=["rowb"])
        sc.op("ac", lambda e: e.activation(out=rowa[:], in_=rowb[:], func=AF.Sqrt),
              reads=["rowb"], writes=["rowa"])
        sc.op("ve", lambda e: e.reciprocal(out=rowb[:], in_=rowa[:]),
              reads=["rowa"], writes=["rowb"])
        sc.op("ve", lambda e: e.tensor_tensor(
            out=rowa[:], in0=rowm[:], in1=rowb[:], op=ALU.mult),
            reads=["rowm", "rowb"], writes=["rowa"])
        for c in range(2):
            sc.op("pe", lambda e, c=c: e.matmul(
                pA[:, c * 512:(c + 1) * 512], ones_rf[0:1, :],
                rowb[0:1, c * 512:(c + 1) * 512], start=True, stop=True),
                reads=["ones_rf", "rowb"], writes=[f"pA{c}"])
            sc.op("pe", lambda e, c=c: e.matmul(
                pB[:, c * 512:(c + 1) * 512], ones_rf[0:1, :],
                rowa[0:1, c * 512:(c + 1) * 512], start=True, stop=True),
                reads=["ones_rf", "rowa"], writes=[f"pB{c}"])
        for kt in range(NKT):
            sl = kt % 2
            sc.op("ve", lambda e, kt=kt, sl=sl: e.tensor_tensor(
                out=scr[:, sl * 1024:(sl + 1) * 1024],
                in0=xP[:, kt * 1024:(kt + 1) * 1024], in1=pA[:], op=ALU.mult),
                reads=["xP", "pA0", "pA1"], writes=[f"scr{sl}"])
            if dst_is_out:
                o = sc.op("ve", lambda e, kt=kt, sl=sl: e.tensor_tensor(
                    out=outst[:, sl * 1024:(sl + 1) * 1024],
                    in0=scr[:, sl * 1024:(sl + 1) * 1024], in1=pB[:], op=ALU.subtract),
                    reads=[f"scr{sl}", "pB0", "pB1"], writes=[f"outst{sl}"])
                sc.dma("sy", out_d[kt * 128:kt * 128 + 128, :],
                       outst[:, sl * 1024:(sl + 1) * 1024],
                       reads=[f"outst{sl}"], writes=[f"outd{kt}"])
            else:
                sc.op("ve", lambda e, kt=kt, sl=sl: e.tensor_tensor(
                    out=xP[:, kt * 1024:(kt + 1) * 1024],
                    in0=scr[:, sl * 1024:(sl + 1) * 1024], in1=pB[:], op=ALU.subtract),
                    reads=[f"scr{sl}", "pB0", "pB1"], writes=["xP"])

    layernorm(False)
    phase("ln1")

    ffn = ExitStack()
    mtg = mksb(ffn, "mtg", [32, NKT * 128], BF16)
    mtu = mksb(ffn, "mtu", [32, NKT * 128], BF16)
    mgm = mksb(ffn, "mgm", [128, NKT * 32], BF16)
    mum = mksb(ffn, "mum", [128, NKT * 32], BF16)
    qzg = mksb(ffn, "qzg", [32, 1376], I32)
    qzu = mksb(ffn, "qzu", [32, 1376], I32)
    qzd = mksb(ffn, "qzd", [86, 512], I32)
    gio32 = mksb(ffn, "gio32", [32, 1], F32)
    gio86 = mksb(ffn, "gio86", [86, 1], F32)
    gior32 = mksb(ffn, "gior32", [128, 32], F32)
    gior86 = mksb(ffn, "gior86", [128, 86], F32)
    gcoli = mksb(ffn, "gcoli", [128, 32], I32)
    gcolf = mksb(ffn, "gcolf", [128, 32], F32)
    xgg = mksb(ffn, "xgg", [32, 1024], BF16)
    xgu = mksb(ffn, "xgu", [32, 1024], BF16)
    hga = mksb(ffn, "hga", [86, 1024], F32)
    hgn = mksb(ffn, "hgn", [86, 1024], BF16)
    sdbf = mksb(ffn, "sdbf", [86, H], BF16)
    zsd = mksb(ffn, "zsd", [86, H], BF16)
    sdf = mksb(ffn, "sdf", [86, 512 * 2], F32)
    hsb = mksb(ffn, "hsb", [128, 11 * 1024], BF16)
    qstg = mksb(ffn, "qstg", [128, 256 * 2], I32)
    t1b = mksb(ffn, "t1b", [128, 256 * 2], I32)
    w1s = mksb(ffn, "w1s", [128, 256 * 2], BF16)
    sgf = mksb(ffn, "sgf", [32, 256 * 2], F32)
    sgb = mksb(ffn, "sgb", [32, 256 * 2], BF16)
    zib = mksb(ffn, "zib", [32, 256 * 2], I32)
    zsb = mksb(ffn, "zsb", [32, 256 * 2], BF16)
    ysin = mksb(ffn, "ysin", [128, 512 * 2], F32)
    ysout = mksb(ffn, "ysout", [128, 512 * 2], F32)
    mtdp = mksb(ffn, "mtdp", [86, 11 * 128], BF16)
    mdp = mksb(ffn, "mdp", [128, 11 * 86], BF16)
    gdbi = mksb(ffn, "gdbi", [1, 128 * 2], I32)
    gdbb = mksb(ffn, "gdbb", [1, 128 * 2], BF16)
    grwi = mksb(ffn, "grwi", [1, 512 * 2], I32)
    grwb = mksb(ffn, "grwb", [1, 512 * 2], BF16)

    sc.dma("sy", qzg[:], zg_d, writes=["qzg"])
    sc.dma("sy", qzu[:], zu_d, writes=["qzu"])
    sc.dma("sy", qzd[:], zd_d, writes=["qzd"])
    sc.op("gp", lambda e: e.iota(gio32[:], [[1, 1]], base=0, channel_multiplier=1,
                                 allow_small_or_imprecise_dtypes=True), writes=["gio32"])
    sc.op("gp", lambda e: e.iota(gio86[:], [[1, 1]], base=0, channel_multiplier=1,
                                 allow_small_or_imprecise_dtypes=True), writes=["gio86"])
    sc.op("gp", lambda e: e.iota(gior32[:], [[1, 32]], base=0, channel_multiplier=0,
                                 allow_small_or_imprecise_dtypes=True), writes=["gior32"])
    sc.op("gp", lambda e: e.iota(gior86[:], [[1, 86]], base=0, channel_multiplier=0,
                                 allow_small_or_imprecise_dtypes=True), writes=["gior86"])

    for name, g_dram, mt, mm in [("g", gg_d, mtg, mgm), ("u", gu_d, mtu, mum)]:
        for ch in range(8):
            sl = ch % 2
            sc.dma("sy", grwi[0:1, sl * 512:sl * 512 + 512],
                   g_dram.rearrange("(o x) -> o x", o=1)[0:1, ch * 512:(ch + 1) * 512],
                   writes=[f"grwi{sl}"])
            sc.op("ve", lambda e, sl=sl: e.tensor_copy(
                out=grwb[0:1, sl * 512:sl * 512 + 512],
                in_=grwi[0:1, sl * 512:sl * 512 + 512]),
                reads=[f"grwi{sl}"], writes=[f"grwb{sl}"])
            sc.op("pe", lambda e, sl=sl: e.matmul(
                pD[0:32, 0:512], ones_r[0:1, 0:32],
                grwb[0:1, sl * 512:sl * 512 + 512], start=True, stop=True),
                reads=[f"grwb{sl}", "ones_r"], writes=["pD"])
            for s4 in range(4):
                kt = ch * 4 + s4
                sc.op("ve", lambda e, kt=kt, s4=s4, mt=mt: e.tensor_scalar(
                    out=mt[:, kt * 128:(kt + 1) * 128],
                    in0=pD[0:32, s4 * 128:(s4 + 1) * 128],
                    scalar1=gio32[:, 0:1], scalar2=None, op0=ALU.is_equal),
                    reads=["pD", "gio32"], writes=[f"mt{name}"])
        for kt in range(NKT):
            sc.dma("sy", gcoli[:, kt:kt + 1],
                   g_dram.rearrange("(a b) -> a b", b=1)[kt * 128:(kt + 1) * 128, :],
                   writes=[f"gcoli{kt}"])
        gcl = sc.op("ve", lambda e: e.tensor_copy(out=gcolf[:], in_=gcoli[:]),
                    reads=[f"gcoli{kt}" for kt in range(NKT)], writes=["gcolf"])
        for kt in range(NKT):
            sc.op("ve", lambda e, kt=kt, mm=mm: e.tensor_scalar(
                out=mm[:, kt * 32:(kt + 1) * 32], in0=gior32[:],
                scalar1=gcolf[:, kt:kt + 1], scalar2=None, op0=ALU.is_equal),
                reads=["gior32", "gcolf"], writes=[f"m{name}m"])

    for ch in range(8):
        sl = ch % 2
        sc.dma("sy", sdf[0:86, sl * 512:sl * 512 + 512], sd_d[:, ch * 512:(ch + 1) * 512],
               writes=[f"sdf{sl}"])
        sc.op("ve", lambda e, ch=ch, sl=sl: e.tensor_copy(
            out=sdbf[:, ch * 512:(ch + 1) * 512], in_=sdf[0:86, sl * 512:sl * 512 + 512]),
            reads=[f"sdf{sl}"], writes=["sdbf"])
        sc.op("ve", lambda e, ch=ch: e.tensor_scalar(
            out=t1b[0:86, 0:512], in0=qzd[:],
            scalar1=4 * ch, scalar2=15,
            op0=ALU.logical_shift_right, op1=ALU.bitwise_and),
            reads=["qzd"], writes=["t1b0", "t1b1"])
        sc.op("ve", lambda e, ch=ch, sl=sl: e.scalar_tensor_tensor(
            out=zsd[:, ch * 512:(ch + 1) * 512], in0=t1b[0:86, 0:512], scalar=1.0,
            in1=sdf[0:86, sl * 512:sl * 512 + 512], op0=ALU.add, op1=ALU.mult),
            reads=["t1b0", "t1b1", f"sdf{sl}"], writes=["zsd"])

    for name, mm, xg in [("g", mgm, xgg), ("u", mum, xgu)]:
        for c in range(2):
            for kt in range(NKT):
                sc.op("pe", lambda e, kt=kt, c=c, mm=mm: e.matmul(
                    pD[0:32, 0:512], mm[:, kt * 32:(kt + 1) * 32],
                    xP[:, kt * 1024 + c * 512:kt * 1024 + (c + 1) * 512],
                    start=(kt == 0), stop=(kt == NKT - 1)),
                    reads=[f"m{name}m", "xP"], writes=["pD"])
            sc.op("ve", lambda e, c=c, xg=xg: e.tensor_scalar(
                out=xg[:, c * 512:(c + 1) * 512], in0=pD[0:32, 0:512],
                scalar1=-1.0, scalar2=None, op0=ALU.mult),
                reads=["pD"], writes=[f"xg{name}"])

    phase("xg")
    ACC = [("pA0", pA, 0), ("pA1", pA, 512), ("pB0", pB, 0), ("pB1", pB, 512)]

    for part in range(NPART):
        d = part
        for b in range(11):
            rows = SLAB_ROWS[b]
            gsl = b % 2
            sc.dma("sy", gdbi[0:1, gsl * 128:gsl * 128 + rows],
                   gd_d.rearrange("(o x) -> o x", o=1)[0:1, d * 1376 + b * 128:d * 1376 + b * 128 + rows],
                   writes=[f"gdbi{gsl}"])
            sc.op("ve", lambda e, gsl=gsl, rows=rows: e.tensor_copy(
                out=gdbb[0:1, gsl * 128:gsl * 128 + rows],
                in_=gdbi[0:1, gsl * 128:gsl * 128 + rows]),
                reads=[f"gdbi{gsl}"], writes=[f"gdbb{gsl}"])
            sc.op("pe", lambda e, b=b, rows=rows, gsl=gsl: e.matmul(
                pD[0:86, 0:rows], ones_r[0:1, 0:86],
                gdbb[0:1, gsl * 128:gsl * 128 + rows], start=True, stop=True),
                reads=[f"gdbb{gsl}", "ones_r"], writes=["pD"])
            sc.op("ve", lambda e, b=b, rows=rows: e.tensor_scalar(
                out=mtdp[:, b * 128:b * 128 + rows], in0=pD[0:86, 0:rows],
                scalar1=gio86[:, 0:1], scalar2=None, op0=ALU.is_equal),
                reads=["pD", "gio86"], writes=["mtdp"])
            sc.dma("sy", gcoli[0:rows, b:b + 1],
                   gd_d.rearrange("(a b) -> a b", b=1)[d * 1376 + b * 128:d * 1376 + b * 128 + rows, :],
                   writes=[f"gcoli{b}"])
        sc.op("ve", lambda e: e.tensor_copy(out=gcolf[:, 0:11], in_=gcoli[:, 0:11]),
              reads=[f"gcoli{b}" for b in range(11)], writes=["gcolf"])
        for b in range(11):
            rows = SLAB_ROWS[b]
            sc.op("ve", lambda e, b=b, rows=rows: e.tensor_scalar(
                out=mdp[0:rows, b * 86:(b + 1) * 86], in0=gior86[0:rows, :],
                scalar1=gcolf[0:rows, b:b + 1], scalar2=None, op0=ALU.is_equal),
                reads=["gior86", "gcolf"], writes=["mdp"])

        for gi, (b0, nsl, width) in enumerate(GRPS):
            gcol = d * 1376 + b0 * 128
            for mat, (q_dram, s_dram, qz, mt, xg, xgk) in enumerate([
                    (qg_d, sg_d, qzg, mtg, xgg, "xgg"),
                    (qu_d, su_d, qzu, mtu, xgu, "xgu")]):
                bsl = (part * 12 + gi * 2 + mat) % 2
                sc.dma("sy", sgf[0:32, bsl * 256:bsl * 256 + width],
                       s_dram[:, gcol:gcol + width], writes=[f"sgf{bsl}"])
                sc.op("ve", lambda e, bsl=bsl, width=width: e.tensor_copy(
                    out=sgb[0:32, bsl * 256:bsl * 256 + width],
                    in_=sgf[0:32, bsl * 256:bsl * 256 + width]),
                    reads=[f"sgf{bsl}"], writes=[f"sgb{bsl}"])
                sc.op("ve", lambda e, qz=qz, d=d, b0=b0, width=width, bsl=bsl: e.tensor_scalar(
                    out=zib[0:32, bsl * 256:bsl * 256 + width],
                    in0=qz[:, b0 * 128:b0 * 128 + width], scalar1=4 * d, scalar2=15,
                    op0=ALU.logical_shift_right, op1=ALU.bitwise_and),
                    reads=["qzg" if mat == 0 else "qzu"], writes=[f"zib{bsl}"])
                sc.op("ve", lambda e, bsl=bsl, width=width: e.scalar_tensor_tensor(
                    out=zsb[0:32, bsl * 256:bsl * 256 + width],
                    in0=zib[0:32, bsl * 256:bsl * 256 + width], scalar=1.0,
                    in1=sgf[0:32, bsl * 256:bsl * 256 + width],
                    op0=ALU.add, op1=ALU.mult),
                    reads=[f"zib{bsl}", f"sgf{bsl}"], writes=[f"zsb{bsl}"])
                for kpb in range(4):
                    qsl = kpb % 2
                    sc.dma("sy", qstg[:, qsl * 256:qsl * 256 + width],
                           q_dram[kpb * 128:kpb * 128 + 128, gcol:gcol + width],
                           writes=[f"qstg{qsl}"])
                    for j in range(8):
                        kt = j * 4 + kpb
                        tsl = kt % 2
                        sesl = kt % 2
                        sc.op("ve", lambda e, j=j, qsl=qsl, tsl=tsl, width=width: e.tensor_scalar(
                            out=t1b[:, tsl * 256:tsl * 256 + width],
                            in0=qstg[:, qsl * 256:qsl * 256 + width],
                            scalar1=4 * j, scalar2=15,
                            op0=ALU.logical_shift_right, op1=ALU.bitwise_and),
                            reads=[f"qstg{qsl}"], writes=[f"t1b{tsl}"])
                        sc.op("pe", lambda e, kt=kt, mt=mt, bsl=bsl, sesl=sesl, width=width: e.matmul(
                            PC[sesl][:, 0:width], mt[:, kt * 128:(kt + 1) * 128],
                            sgb[0:32, bsl * 256:bsl * 256 + width],
                            start=True, stop=True),
                            reads=["mtg" if mat == 0 else "mtu", f"sgb{bsl}"],
                            writes=[f"pC{sesl + 1}"])
                        sc.op("ve", lambda e, tsl=tsl, sesl=sesl, width=width: e.scalar_tensor_tensor(
                            out=w1s[:, tsl * 256:tsl * 256 + width],
                            in0=t1b[:, tsl * 256:tsl * 256 + width], scalar=1.0,
                            in1=PC[sesl][:, 0:width], op0=ALU.mult, op1=ALU.mult),
                            reads=[f"t1b{tsl}", f"pC{sesl + 1}"], writes=[f"w1s{tsl}"])
                        for c in range(2):
                            for s in range(nsl):
                                ak, aps, aoff = ACC[c * 2 + s]
                                rows = SLAB_ROWS[b0 + s]
                                sc.op("pe", lambda e, kt=kt, c=c, s=s, tsl=tsl, rows=rows,
                                      aps=aps, aoff=aoff: e.matmul(
                                    aps[0:rows, aoff:aoff + 512],
                                    w1s[:, tsl * 256 + s * 128:tsl * 256 + s * 128 + rows],
                                    xP[:, kt * 1024 + c * 512:kt * 1024 + (c + 1) * 512],
                                    start=(kt == 0), stop=False),
                                    reads=[f"w1s{tsl}", "xP"], writes=[ak])
                for c in range(2):
                    for s in range(nsl):
                        ak, aps, aoff = ACC[c * 2 + s]
                        rows = SLAB_ROWS[b0 + s]
                        sc.op("pe", lambda e, c=c, s=s, rows=rows, bsl=bsl, aps=aps,
                              aoff=aoff, xg=xg: e.matmul(
                            aps[0:rows, aoff:aoff + 512],
                            zsb[0:32, bsl * 256 + s * 128:bsl * 256 + s * 128 + rows],
                            xg[:, c * 512:(c + 1) * 512],
                            start=False, stop=True),
                            reads=[f"zsb{bsl}", xgk], writes=[ak])
                        hoff = (b0 + s) * 1024 + c * 512
                        if mat == 0:
                            sc.op("ac", lambda e, rows=rows, hoff=hoff, aps=aps, aoff=aoff: e.activation(
                                out=hsb[0:rows, hoff:hoff + 512],
                                in_=aps[0:rows, aoff:aoff + 512], func=AF.Silu),
                                reads=[ak], writes=["hsb"])
                            if part == 0 and gi == 0 and c == 1 and s == nsl - 1:
                                phase("dbg_g0")
                        else:
                            sc.op("ve", lambda e, rows=rows, hoff=hoff, aps=aps, aoff=aoff: e.tensor_tensor(
                                out=hsb[0:rows, hoff:hoff + 512],
                                in0=hsb[0:rows, hoff:hoff + 512],
                                in1=aps[0:rows, aoff:aoff + 512], op=ALU.mult),
                                reads=[ak, "hsb"], writes=["hsb"])

        for c in range(2):
            for b in range(11):
                rows = SLAB_ROWS[b]
                sc.op("pe", lambda e, b=b, c=c, rows=rows: e.matmul(
                    pD[0:86, 0:512], mdp[0:rows, b * 86:(b + 1) * 86],
                    hsb[0:rows, b * 1024 + c * 512:b * 1024 + (c + 1) * 512],
                    start=(b == 0), stop=(b == 10)),
                    reads=["mdp", "hsb"], writes=["pD"])
            if part == 0:
                sc.op("ve", lambda e, c=c: e.tensor_copy(
                    out=hga[:, c * 512:(c + 1) * 512], in_=pD[0:86, 0:512]),
                    reads=["pD"], writes=["hga"])
            else:
                sc.op("ve", lambda e, c=c: e.tensor_tensor(
                    out=hga[:, c * 512:(c + 1) * 512],
                    in0=hga[:, c * 512:(c + 1) * 512], in1=pD[0:86, 0:512], op=ALU.add),
                    reads=["pD", "hga"], writes=["hga"])
        if part == NPART - 1:
            sc.op("ve", lambda e: e.tensor_scalar(
                out=hgn[:], in0=hga[:], scalar1=-1.0, scalar2=None, op0=ALU.mult),
                reads=["hga"], writes=["hgn"])

        for nb in range(16):
            ncol = nb * 256
            for b in range(11):
                rows = SLAB_ROWS[b]
                qsl = b % 2
                sc.dma("sy", qstg[0:rows, qsl * 256:qsl * 256 + 256],
                       qd_d[b * 128:b * 128 + rows, ncol:ncol + 256],
                       writes=[f"qstg{qsl}"])
                tsl = b % 2
                sc.op("ve", lambda e, d=d, rows=rows, qsl=qsl, tsl=tsl: e.tensor_scalar(
                    out=t1b[0:rows, tsl * 256:tsl * 256 + 256],
                    in0=qstg[0:rows, qsl * 256:qsl * 256 + 256],
                    scalar1=4 * d, scalar2=15,
                    op0=ALU.logical_shift_right, op1=ALU.bitwise_and),
                    reads=[f"qstg{qsl}"], writes=[f"t1b{tsl}"])
                sesl = b % 2
                sc.op("pe", lambda e, b=b, rows=rows, sesl=sesl, ncol=ncol: e.matmul(
                    PC[sesl][0:rows, 0:256], mtdp[:, b * 128:b * 128 + rows],
                    sdbf[:, ncol:ncol + 256], start=True, stop=True),
                    reads=["mtdp", "sdbf"], writes=[f"pC{sesl + 1}"])
                sc.op("ve", lambda e, rows=rows, tsl=tsl, sesl=sesl: e.scalar_tensor_tensor(
                    out=w1s[0:rows, tsl * 256:tsl * 256 + 256],
                    in0=t1b[0:rows, tsl * 256:tsl * 256 + 256], scalar=1.0,
                    in1=PC[sesl][0:rows, 0:256], op0=ALU.mult, op1=ALU.mult),
                    reads=[f"t1b{tsl}", f"pC{sesl + 1}"], writes=[f"w1s{tsl}"])
                for nt2 in range(2):
                    for c in range(2):
                        ak, aps, aoff = ACC[nt2 * 2 + c]
                        sc.op("pe", lambda e, b=b, rows=rows, nt2=nt2, c=c, tsl=tsl,
                              aps=aps, aoff=aoff: e.matmul(
                            aps[:, aoff:aoff + 512],
                            w1s[0:rows, tsl * 256 + nt2 * 128:tsl * 256 + nt2 * 128 + 128],
                            hsb[0:rows, b * 1024 + c * 512:b * 1024 + (c + 1) * 512],
                            start=(b == 0), stop=(b == 10 and part != NPART - 1)),
                            reads=[f"w1s{tsl}", "hsb"], writes=[ak])
            for nt2 in range(2):
                for c in range(2):
                    ak, aps, aoff = ACC[nt2 * 2 + c]
                    ntg = nb * 2 + nt2
                    if part == NPART - 1:
                        sc.op("pe", lambda e, ntg=ntg, c=c, aps=aps, aoff=aoff: e.matmul(
                            aps[:, aoff:aoff + 512],
                            zsd[:, ntg * 128:ntg * 128 + 128],
                            hgn[:, c * 512:(c + 1) * 512], start=False, stop=True),
                            reads=["zsd", "hgn"], writes=[ak])
                    usl = (nb * 4 + nt2 * 2 + c) % 2
                    if part > 0:
                        sc.dma("sy", ysin[:, usl * 512:usl * 512 + 512],
                               ysp_d[ntg * 128:ntg * 128 + 128, c * 512:(c + 1) * 512],
                               reads=[f"ysp{ntg}_{c}"], writes=[f"ysin{usl}"])
                    if part == 0:
                        sc.op("ve", lambda e, usl=usl, aps=aps, aoff=aoff: e.tensor_copy(
                            out=ysout[:, usl * 512:usl * 512 + 512],
                            in_=aps[:, aoff:aoff + 512]),
                            reads=[ak], writes=[f"ysout{usl}"])
                        sc.dma("sy", ysp_d[ntg * 128:ntg * 128 + 128, c * 512:(c + 1) * 512],
                               ysout[:, usl * 512:usl * 512 + 512],
                               reads=[f"ysout{usl}"], writes=[f"ysp{ntg}_{c}"])
                    elif part < NPART - 1:
                        sc.op("ve", lambda e, usl=usl, aps=aps, aoff=aoff: e.tensor_tensor(
                            out=ysout[:, usl * 512:usl * 512 + 512],
                            in0=ysin[:, usl * 512:usl * 512 + 512],
                            in1=aps[:, aoff:aoff + 512], op=ALU.add),
                            reads=[ak, f"ysin{usl}"], writes=[f"ysout{usl}"])
                        sc.dma("sy", ysp_d[ntg * 128:ntg * 128 + 128, c * 512:(c + 1) * 512],
                               ysout[:, usl * 512:usl * 512 + 512],
                               reads=[f"ysout{usl}"], writes=[f"ysp{ntg}_{c}"])
                    else:
                        ssl = usl
                        sc.op("ve", lambda e, usl=usl, ssl=ssl, aps=aps, aoff=aoff: e.tensor_tensor(
                            out=scr[:, ssl * 1024:ssl * 1024 + 512],
                            in0=ysin[:, usl * 512:usl * 512 + 512],
                            in1=aps[:, aoff:aoff + 512], op=ALU.add),
                            reads=[ak, f"ysin{usl}"], writes=[f"scr{ssl}"])
                        sc.op("ve", lambda e, ntg=ntg, c=c, ssl=ssl: e.tensor_tensor(
                            out=xP[:, ntg * 1024 + c * 512:ntg * 1024 + (c + 1) * 512],
                            in0=scr[:, ssl * 1024:ssl * 1024 + 512],
                            in1=xP[:, ntg * 1024 + c * 512:ntg * 1024 + (c + 1) * 512],
                            op=ALU.add),
                            reads=[f"scr{ssl}", "xP"], writes=["xP"])

        phase(f"part{part}")
    sc.barrier()
    ffn.close()

    p6 = ExitStack()
    outst = mksb(p6, "outst", [128, 1024 * 2], F32)
    layernorm(True, outst)
    p6.close()
    ln.close()

    sems = {}
    es_sem = ExitStack()
    for e in ENGS:
        sems[e] = es_sem.enter_context(nc.semaphore("s_" + e))
        sems["bulk_" + e] = es_sem.enter_context(nc.semaphore("b_" + e))

    def mksem(name):
        return es_sem.enter_context(nc.semaphore(name[:24]))

    with nc.Block() as block:
        sc.emit(nc, block, sems, mksem, prefix=_STOP_AFTER)
    es_sem.close()
    es_top.close()
    g.close()
    return nc, taps



_PROG = None


def _get_prog():
    global _PROG
    if _PROG is None:
        _PROG = build_program()
    return _PROG


def kernel(L0_attn_hidden_states, L0_attn_Wq, L0_attn_Wkv, L0_attn_Wo,
           L0_ffn_gate_qweight, L0_ffn_gate_scales, L0_ffn_gate_qzeros, L0_ffn_gate_g_idx,
           L0_ffn_up_qweight, L0_ffn_up_scales, L0_ffn_up_qzeros, L0_ffn_up_g_idx,
           L0_ffn_down_qweight, L0_ffn_down_scales, L0_ffn_down_qzeros, L0_ffn_down_g_idx):
    x = np.asarray(L0_attn_hidden_states, dtype=np.float32)
    Wq = np.asarray(L0_attn_Wq, dtype=np.float32)
    Wkv = np.asarray(L0_attn_Wkv, dtype=np.float32)
    Wo = np.asarray(L0_attn_Wo, dtype=np.float32)

    slopes = _alibi_slopes(NH)
    permH, permI = PERM_H, PERM_I

    Wq_p = Wq[permH, :]
    wq = np.ascontiguousarray(
        Wq_p.reshape(NKT, 128, NKT, 128).transpose(2, 0, 1, 3))
    Wkv_p = Wkv[permH, :]
    wk = np.ascontiguousarray(Wkv_p[:, :128].reshape(NKT, 128, 128))
    wv = np.ascontiguousarray(Wkv_p[:, 128:].reshape(NKT, 128, 128))
    wo = np.ascontiguousarray(
        Wo[:, permH].reshape(NKT, 128, NKT, 128).transpose(2, 0, 1, 3))
    qg = np.ascontiguousarray(np.asarray(L0_ffn_gate_qweight, dtype=np.int32)[:, permI])
    qu = np.ascontiguousarray(np.asarray(L0_ffn_up_qweight, dtype=np.int32)[:, permI])
    qd = np.ascontiguousarray(np.asarray(L0_ffn_down_qweight, dtype=np.int32)[:, permH])
    sg = np.ascontiguousarray(np.asarray(L0_ffn_gate_scales, dtype=np.float32)[:, permI])
    su = np.ascontiguousarray(np.asarray(L0_ffn_up_scales, dtype=np.float32)[:, permI])
    sd = np.ascontiguousarray(np.asarray(L0_ffn_down_scales, dtype=np.float32)[:, permH])
    zg = np.ascontiguousarray(np.asarray(L0_ffn_gate_qzeros, dtype=np.int32))
    zu = np.ascontiguousarray(np.asarray(L0_ffn_up_qzeros, dtype=np.int32))
    zd = np.ascontiguousarray(np.asarray(L0_ffn_down_qzeros, dtype=np.int32))
    gg = np.ascontiguousarray(np.asarray(L0_ffn_gate_g_idx, dtype=np.int32)[permH])
    gu = np.ascontiguousarray(np.asarray(L0_ffn_up_g_idx, dtype=np.int32)[permH])
    gd = np.ascontiguousarray(np.asarray(L0_ffn_down_g_idx, dtype=np.int32)[permI])

    idn = np.eye(128, dtype=np.float32).astype(ml_dtypes.bfloat16)
    onesr = np.ones((1, 128), dtype=ml_dtypes.bfloat16)
    onesc = np.ones((128, 1), dtype=ml_dtypes.bfloat16)
    onesrf = np.ones((1, 128), dtype=np.float32)
    abias = np.empty((128, NH * NST), dtype=np.float32)
    p = np.arange(128, dtype=np.float32)
    for h in range(NH):
        for j in range(NST):
            abias[:, h * 16 + j] = slopes[h] * (128 * j + p)

    in_maps = []
    core_cols = []
    for core in range(8):
        b = core // 2
        strips = STRIPS_A if core % 2 == 0 else STRIPS_B
        cols = np.concatenate([np.arange(64 * s, 64 * s + 64) for s in strips])
        core_cols.append((b, cols))
        xt = np.ascontiguousarray(x[b].T[permH, :])
        xp = np.ascontiguousarray(xt[:, cols])
        amask = np.zeros((128, NST * 64), dtype=np.float32)
        for j in range(NST):
            dstrip = [s for s in strips if s in (2 * j, 2 * j + 1)][0]
            kk = 128 * j + p[:, None]
            qq = 64 * dstrip + np.arange(64)[None, :]
            amask[:, j * 64:(j + 1) * 64] = np.where(kk <= qq, 0.0, NEG)
        aqsl = (-slopes[:, None] * cols[None, :].astype(np.float32)).astype(ml_dtypes.bfloat16)
        in_maps.append(dict(
            xp=xp, xt=xt, wq=wq, wk=wk, wv=wv, wo=wo, qg=qg, qu=qu, qd=qd,
            sg=sg, su=su, sd=sd, zg=zg, zu=zu, zd=zd, gg=gg, gu=gu, gd=gd,
            amask=amask.astype(ml_dtypes.bfloat16), abias=abias, aqsl=aqsl,
            idn=idn, onesr=onesr, onesc=onesc, onesrf=onesrf))

    nc, taps = _get_prog()
    res = run_bass_kernel_spmd(nc, in_maps, core_ids=list(range(8)))

    inv = np.empty(H, dtype=np.int64)
    inv[permH] = np.arange(H)
    full = np.empty((B, S, H), dtype=np.float32)
    for core in range(8):
        b, cols = core_cols[core]
        y = np.asarray(res.results[core]["out"])
        full[b, cols, :] = y[inv, :].T
    if _DEBUG_TAPS:
        kernel._last_results = res
    return full


if __name__ == "__main__":
    nc, taps = build_program()
    print("build ok")


# version 03
# speedup vs baseline: 1.5769x; 1.5769x over previous
import math
import numpy as np
import ml_dtypes
import concourse.bass as bass
import concourse.mybir as mybir
from concourse.bass_utils import run_bass_kernel_spmd
from contextlib import ExitStack

F32 = mybir.dt.float32
BF16 = mybir.dt.bfloat16
I32 = mybir.dt.int32
AF = mybir.ActivationFunctionType
ALU = mybir.AluOpType

B, S, H = 4, 2048, 4096
NH, HD = 32, 128
INTER = 11008
NTOK = 1024
NKT = 32
NST = 16
SCALE = HD ** -0.5
EPS = 1e-5
NEG = -30000.0
NPART = 8
GRPS = [(0, 2, 256), (2, 2, 256), (4, 2, 256), (6, 2, 256), (8, 2, 256), (10, 1, 96)]
SLAB_ROWS = [128] * 10 + [96]

_DEBUG_TAPS = []
_STOP_AFTER = None


class _StopBuild(Exception):
    pass


def _alibi_slopes(n):
    cp = 2 ** math.floor(math.log2(n))
    base = 2.0 ** (-(2.0 ** (-(math.log2(cp) - 3))))
    sl = [base ** i for i in range(1, cp + 1)]
    if cp != n:
        eb = 2.0 ** (-(2.0 ** (-(math.log2(2 * cp) - 3))))
        ne = min(cp, n - cp)
        sl += [eb ** i for i in range(1, 2 * ne + 1, 2)]
    return np.asarray(sl, dtype=np.float32)


def _perm(n_rows):
    return (8 * np.arange(n_rows)[None, :] + np.arange(8)[:, None]).reshape(-1)


PERM_H = _perm(512)
PERM_I = _perm(1376)
STRIPS_A = [s for s in range(32) if s % 4 in (0, 3)]
STRIPS_B = [s for s in range(32) if s % 4 in (1, 2)]


def att_chunks(j):
    qs = 64 * j
    if qs < 512:
        return [(qs, 512 - qs), (512, 512)]
    return [(qs, 1024 - qs)]



ENGS = ("sy", "gp", "pe", "ve", "ac")


class Op:
    __slots__ = ("eng", "fn", "waits", "is_dma", "marked", "dmakey", "seq", "isbar")

    def __init__(self, eng, fn, is_dma):
        self.eng, self.fn, self.is_dma = eng, fn, is_dma
        self.waits, self.marked, self.dmakey = [], False, None
        self.seq, self.isbar = 0, False


class Sched:
    def __init__(self):
        self.ops = {e: [] for e in ENGS}
        self.writers = {}
        self.readers = {}
        self.seqc = 0

    def barrier(self):
        for e in ENGS:
            o = self.op(e, lambda eng: eng.nop())
            o.isbar = True

    def _edge(self, a, b):
        if a is b or (a.eng == b.eng and not a.is_dma):
            return
        a.marked = True
        b.waits.append(a)

    def op(self, eng, fn, reads=(), writes=(), after=(), is_dma=False):
        o = Op(eng, fn, is_dma)
        if is_dma:
            o.marked = True
            o.dmakey = writes[0] if writes else None
        for a in after:
            if a is not None:
                self._edge(a, o)
        for bk in reads:
            for w in self.writers.get(bk, {}).values():
                self._edge(w, o)
        for bk in writes:
            for w in self.writers.get(bk, {}).values():
                self._edge(w, o)
            for r in self.readers.get(bk, {}).values():
                self._edge(r, o)
        for bk in reads:
            self.readers.setdefault(bk, {})[o.eng] = o
        for bk in writes:
            self.writers[bk] = {o.eng: o}
            self.readers[bk] = {}
        self.seqc += 1
        o.seq = self.seqc
        self.ops[eng].append(o)
        return o

    def dma(self, q, out_ap, in_ap, reads=(), writes=(), after=()):
        return self.op(q, (lambda e, o=out_ap, i=in_ap: e.dma_start(out=o, in_=i)),
                       reads=reads, writes=writes, after=after, is_dma=True)

    def snap(self, name):
        if not hasattr(self, "snaps"):
            self.snaps = {}
        self.snaps[name] = {e: len(self.ops[e]) for e in ENGS}

    def emit(self, nc, block, sems, mksem, prefix=None):
        lens = self.snaps[prefix] if prefix else {e: len(self.ops[e]) for e in ENGS}
        def dsem(o):
            k = o.dmakey
            if k is None or k.startswith(("ysp", "outd", "tap_")):
                return "bulk_" + o.eng
            return "d_" + k

        val = {}
        semname = {}
        tot = {}
        cnt = {}
        events = []
        for e in ENGS:
            for o in self.ops[e][:lens[e]]:
                if o.isbar:
                    o.marked = True
                if not o.marked:
                    continue
                sn = dsem(o) if o.is_dma else e
                semname[id(o)] = sn
                cnt[sn] = cnt.get(sn, 0) + (16 if o.is_dma else 1)
                val[id(o)] = cnt[sn]
                tot.setdefault(e, {})[sn] = cnt[sn]
                events.append((o.seq, sn, cnt[sn]))
        for sn in cnt:
            if sn not in sems:
                sems[sn] = mksem(sn)
        events.sort()
        eseqs = [ev[0] for ev in events]
        import bisect

        def snapshot(seq):
            snap = {}
            for i in range(bisect.bisect_left(eseqs, seq)):
                _, sn, v = events[i]
                snap[sn] = max(snap.get(sn, 0), v)
            return snap

        def mk(ename):
            def body(eng):
                cache = {}
                for o in self.ops[ename][:lens[ename]]:
                    if o.isbar:
                        for sn, v in snapshot(o.seq).items():
                            if cache.get(sn, 0) < v:
                                eng.wait_ge(sems[sn], v)
                                cache[sn] = v
                    for a in o.waits:
                        v = val.get(id(a))
                        if v is None:
                            continue
                        sn = semname[id(a)]
                        if cache.get(sn, 0) < v:
                            eng.wait_ge(sems[sn], v)
                            cache[sn] = v
                    instr = o.fn(eng)
                    if o.marked:
                        instr.then_inc(sems[semname[id(o)]], 16 if o.is_dma else 1)
                if ename in ("sy", "gp"):
                    for sn, v in tot.get(ename, {}).items():
                        if cache.get(sn, 0) < v:
                            eng.wait_ge(sems[sn], v)
                            cache[sn] = v
            return body

        block.sync(mk("sy"))
        block.gpsimd(mk("gp"))
        block.tensor(mk("pe"))
        block.vector(mk("ve"))
        block.scalar(mk("ac"))



def build_program():
    nc = bass.Bass("TRN2", target_bir_lowering=False)
    sc = Sched()
    es_top = ExitStack()
    taps = {}

    def din(name, shape, dtype):
        return nc.dram_tensor(name, shape, dtype, kind="ExternalInput").ap()

    xp_d = din("xp", [H, NTOK], F32)
    xt_d = din("xt", [H, S], F32)
    wq_d = din("wq", [NH, NKT, 128, 128], F32)
    wk_d = din("wk", [NKT, 128, 128], F32)
    wv_d = din("wv", [NKT, 128, 128], F32)
    wo_d = din("wo", [NKT, NKT, 128, 128], F32)
    qg_d = din("qg", [512, INTER], I32)
    qu_d = din("qu", [512, INTER], I32)
    qd_d = din("qd", [1376, H], I32)
    sg_d = din("sg", [32, INTER], F32)
    su_d = din("su", [32, INTER], F32)
    sd_d = din("sd", [86, H], F32)
    zg_d = din("zg", [32, 1376], I32)
    zu_d = din("zu", [32, 1376], I32)
    zd_d = din("zd", [86, 512], I32)
    gg_d = din("gg", [H], I32)
    gu_d = din("gu", [H], I32)
    gd_d = din("gd", [INTER], I32)
    amask_d = din("amask", [128, NST * 64], BF16)
    abias_d = din("abias", [128, NH * NST], F32)
    aqsl_d = din("aqsl", [NH, NTOK], BF16)
    idn_d = din("idn", [128, 128], BF16)
    onesr_d = din("onesr", [1, 128], BF16)
    onesc_d = din("onesc", [128, 1], BF16)
    onesrf_d = din("onesrf", [1, 128], F32)
    out_d = nc.dram_tensor("out", [H, NTOK], F32, kind="ExternalOutput").ap()
    ysp_d = nc.dram_tensor("yspill", [H, NTOK], F32, kind="Internal").ap()

    def mksb(stack, name, shape, dtype):
        t = stack.enter_context(nc.sbuf_tensor(name, shape, dtype))
        allsb[name] = t
        taps[name] = t
        return t

    allsb = {}

    def phase(name):
        for tname in _DEBUG_TAPS.get(name, []) if isinstance(_DEBUG_TAPS, dict) else []:
            t = allsb[tname]
            sh = list(t.shape)
            td = nc.dram_tensor(f"tap_{name}_{tname}", sh, t.dtype,
                                kind="ExternalOutput").ap()
            sc.dma("sy", td, t[:], reads=[tname, tname + "0", tname + "1"])
        sc.snap(name)

    pA = es_top.enter_context(nc.psum_tensor("pA", [128, 1024], F32))
    pB = es_top.enter_context(nc.psum_tensor("pB", [128, 1024], F32))
    pC1 = es_top.enter_context(nc.psum_tensor("pC1", [128, 512], F32))
    pC2 = es_top.enter_context(nc.psum_tensor("pC2", [128, 512], F32))
    pD = es_top.enter_context(nc.psum_tensor("pD", [128, 512], F32))
    pE = es_top.enter_context(nc.psum_tensor("pE", [128, 1024], BF16))
    PC = [pC1, pC2]

    g = ExitStack()
    xP = mksb(g, "xP", [128, NKT * 1024], BF16)
    ones_r = mksb(g, "ones_r", [1, 128], BF16)
    ones_c = mksb(g, "ones_c", [128, 1], BF16)
    ones_rf = mksb(g, "ones_rf", [1, 128], F32)
    idn_sb = mksb(g, "idn_sb", [128, 128], BF16)

    sc.dma("sy", idn_sb[:], idn_d, writes=["idn"])
    sc.dma("sy", ones_r[:], onesr_d, writes=["ones_r"])
    sc.dma("sy", ones_c[:], onesc_d, writes=["ones_c"])
    sc.dma("sy", ones_rf[:], onesrf_d, writes=["ones_rf"])

    att = ExitStack()
    ktb = mksb(att, "ktb", [128, S], BF16)
    vtok = mksb(att, "vtok", [128, S], BF16)
    qhat = mksb(att, "qhat", [128, 1024 * 2], BF16)
    probs = mksb(att, "probs", [128, 512 * 3], BF16)
    wstg = mksb(att, "wstg", [128, 1024 * 2], F32)
    wbf = mksb(att, "wbf", [128, 4096 * 2], BF16)
    sumr = mksb(att, "sumr", [1, 1024], F32)
    sumx = mksb(att, "sumx", [32, 1024], F32)
    rsx = mksb(att, "rsx", [32, 1024], F32)
    rsxb = mksb(att, "rsxb", [32, 1024], BF16)
    rs0 = mksb(att, "rs0", [1, 1024 * 2], BF16)
    aqs = mksb(att, "aqs", [1, 1024 * 2], BF16)
    amask = mksb(att, "amask_sb", [128, NST * 64], BF16)
    abias = mksb(att, "abias_sb", [128, NH * NST], F32)
    p1 = ExitStack()
    xstg = mksb(p1, "xstg", [128, 1024 * 2], F32)
    xtb = mksb(p1, "xtb", [128, 1024 * 2], BF16)
    wkb = mksb(p1, "wkb", [128, 4096], BF16)
    wvb = mksb(p1, "wvb", [128, 4096], BF16)

    sc.dma("sy", amask[:], amask_d, writes=["amask"])
    sc.dma("sy", abias[:], abias_d, writes=["abias"])
    phase("p0")

    for kt in range(NKT):
        sl = kt % 2
        sc.dma("sy", xstg[:, sl * 1024:sl * 1024 + 1024],
               xp_d[kt * 128:kt * 128 + 128, :], writes=[f"xstg{sl}"])
        sc.op("ve", lambda e, kt=kt, sl=sl: e.tensor_copy(
            out=xP[:, kt * 1024:(kt + 1) * 1024],
            in_=xstg[:, sl * 1024:sl * 1024 + 1024]),
            reads=[f"xstg{sl}"], writes=["xP"])

    phase("px")
    for t, (wsrc, wdst, key) in enumerate([(wk_d, wkb, "wkb"), (wv_d, wvb, "wvb")]):
        for q4 in range(4):
            sl = q4 % 2
            sc.dma("sy", xstg[:, sl * 1024:sl * 1024 + 1024].rearrange(
                       "p (t c) -> p t c", c=128),
                   wsrc[q4 * 8:(q4 + 1) * 8].rearrange("t p c -> p t c"),
                   writes=[f"xstg{sl}"])
            sc.op("ve", lambda e, q4=q4, sl=sl, wdst=wdst: e.tensor_copy(
                out=wdst[:, q4 * 1024:(q4 + 1) * 1024],
                in_=xstg[:, sl * 1024:sl * 1024 + 1024]),
                reads=[f"xstg{sl}"], writes=[key])

    phase("pw")
    for sh in range(2):
        for kt in range(NKT):
            sl = kt % 2
            sc.dma("sy", xstg[:, sl * 1024:sl * 1024 + 1024],
                   xt_d[kt * 128:kt * 128 + 128, sh * 1024:sh * 1024 + 1024],
                   writes=[f"xstg{sl}"])
            sc.op("ve", lambda e, sl=sl: e.tensor_copy(
                out=xtb[:, sl * 1024:sl * 1024 + 1024],
                in_=xstg[:, sl * 1024:sl * 1024 + 1024]),
                reads=[f"xstg{sl}"], writes=[f"xtb{sl}"])
            for c in range(2):
                sc.op("pe", lambda e, kt=kt, sl=sl, c=c: e.matmul(
                    pA[:, c * 512:(c + 1) * 512],
                    wkb[:, kt * 128:(kt + 1) * 128],
                    xtb[:, sl * 1024 + c * 512:sl * 1024 + (c + 1) * 512],
                    start=(kt == 0), stop=(kt == NKT - 1)),
                    reads=[f"xtb{sl}", "wkb"], writes=[f"pA{c}"])
            for i in range(8):
                sc.op("pe", lambda e, kt=kt, sl=sl, i=i: e.matmul(
                    pB[:, i * 128:(i + 1) * 128],
                    xtb[:, sl * 1024 + i * 128:sl * 1024 + (i + 1) * 128],
                    wvb[:, kt * 128:(kt + 1) * 128],
                    start=(kt == 0 and i % 4 == 0), stop=(kt == NKT - 1)),
                    reads=[f"xtb{sl}", "wvb"], writes=[f"pB{i // 4}"])
        phase(f"kv{sh}mm")
        sc.op("ac", lambda e, sh=sh: e.activation(
            out=ktb[:, sh * 1024:(sh + 1) * 1024], in_=pA[:], func=AF.Copy),
            reads=["pA0", "pA1"], writes=["ktb"])
        sc.op("ac", lambda e, sh=sh: e.activation(
            out=vtok[:, sh * 1024:(sh + 1) * 1024], in_=pB[:], func=AF.Copy),
            reads=["pB0", "pB1"], writes=["vtok"])
        phase(f"kv{sh}ev")

    phase("p1")
    p1.close()
    avu = mksb(att, "avu", [128, NH * 1024], BF16)
    for h in range(NH):
        wsl = h % 2
        for q4 in range(4):
            ssl = q4 % 2
            sc.dma("sy", wstg[:, ssl * 1024:ssl * 1024 + 1024].rearrange(
                       "p (t c) -> p t c", c=128),
                   wq_d[h, q4 * 8:(q4 + 1) * 8].rearrange("t p c -> p t c"),
                   writes=[f"wstg{ssl}"])
            sc.op("ve", lambda e, q4=q4, ssl=ssl, wsl=wsl: e.tensor_copy(
                out=wbf[:, wsl * 4096 + q4 * 1024:wsl * 4096 + (q4 + 1) * 1024],
                in_=wstg[:, ssl * 1024:ssl * 1024 + 1024]),
                reads=[f"wstg{ssl}"], writes=[f"wbf{wsl}"])
        sc.dma("gp", aqs[0:1, wsl * 1024:wsl * 1024 + 1024], aqsl_d[h:h + 1, :],
               writes=[f"aqs{wsl}"])
        for kt in range(NKT):
            for c in range(2):
                sc.op("pe", lambda e, kt=kt, c=c, wsl=wsl: e.matmul(
                    pA[:, c * 512:(c + 1) * 512],
                    wbf[:, wsl * 4096 + kt * 128:wsl * 4096 + kt * 128 + 128],
                    xP[:, kt * 1024 + c * 512:kt * 1024 + (c + 1) * 512],
                    start=(kt == 0), stop=(kt == NKT - 1)),
                    reads=[f"wbf{wsl}", "xP"], writes=[f"pA{c}"])
        qsl = h % 2
        sc.op("ac", lambda e, qsl=qsl: e.activation(
            out=qhat[:, qsl * 1024:(qsl + 1) * 1024], in_=pA[:], func=AF.Copy,
            scale=SCALE), reads=["pA0", "pA1"], writes=[f"qhat{qsl}"])
        units = []
        for j in range(NST):
            for ci, (off, w) in enumerate(att_chunks(j)):
                units.append((j, ci, off, w))

        def att_scores(pi):
            j, ci, off, w = units[pi]
            spi = pi % 2
            skey = f"pC{spi + 1}"
            sc.op("pe", lambda e, j=j, off=off, w=w, spi=spi, qsl=qsl: e.matmul(
                PC[spi][:, 0:w], ktb[:, j * 128:(j + 1) * 128],
                qhat[:, qsl * 1024 + off:qsl * 1024 + off + w],
                start=True, stop=False),
                reads=["ktb", f"qhat{qsl}"], writes=[skey])
            sc.op("pe", lambda e, off=off, w=w, spi=spi, wsl=wsl: e.matmul(
                PC[spi][:, 0:w], ones_r[0:1, :],
                aqs[0:1, wsl * 1024 + off:wsl * 1024 + off + w],
                start=False, stop=True),
                reads=["ones_r", f"aqs{wsl}"], writes=[skey])

        att_scores(0)
        for pi in range(len(units)):
            j, ci, off, w = units[pi]
            spi = pi % 2
            ppi = pi % 3
            skey = f"pC{spi + 1}"
            if ci == 0:
                sc.op("ve", lambda e, j=j, spi=spi: e.tensor_tensor(
                    out=PC[spi][:, 0:64], in0=PC[spi][:, 0:64],
                    in1=amask[:, j * 64:j * 64 + 64], op=ALU.add),
                    reads=[skey, "amask"], writes=[skey])
            sc.op("ac", lambda e, j=j, w=w, spi=spi, ppi=ppi, h=h: e.activation(
                out=probs[:, ppi * 512:ppi * 512 + w], in_=PC[spi][:, 0:w],
                func=AF.Exp, bias=abias[:, h * 16 + j:h * 16 + j + 1]),
                reads=[skey, "abias"], writes=[f"probs{ppi}"])
            if pi + 1 < len(units):
                att_scores(pi + 1)
            bank = off // 512
            stop_s = (j == 7 and ci == 0) or (j == 15)
            sc.op("pe", lambda e, off=off, w=w, ppi=ppi, j=j, stop_s=stop_s: e.matmul(
                pA[0:1, off:off + w], ones_c[:, 0:1],
                probs[:, ppi * 512:ppi * 512 + w],
                start=(j == 0), stop=stop_s),
                reads=["ones_c", f"probs{ppi}"], writes=[f"pA{bank}"])
            sc.op("pe", lambda e, off=off, w=w, ppi=ppi, j=j, stop_s=stop_s: e.matmul(
                pB[:, off:off + w], vtok[:, j * 128:(j + 1) * 128],
                probs[:, ppi * 512:ppi * 512 + w],
                start=(j == 0), stop=stop_s),
                reads=["vtok", f"probs{ppi}"], writes=[f"pB{bank}"])
        sc.op("ac", lambda e, h=h: e.activation(
            out=avu[:, h * 1024:(h + 1) * 1024], in_=pB[:], func=AF.Copy),
            reads=["pB0", "pB1"], writes=["avu"])
        sc.op("ac", lambda e: e.activation(out=sumr[:], in_=pA[0:1, :], func=AF.Copy),
              reads=["pA0", "pA1"], writes=["sumr"])
        sc.dma("gp", sumx[h:h + 1, :], sumr[:], reads=["sumr"], writes=["sumx"])

    sc.op("ve", lambda e: e.reciprocal(out=rsx[:], in_=sumx[:]),
          reads=["sumx"], writes=["rsx"])
    sc.op("ve", lambda e: e.tensor_copy(out=rsxb[:], in_=rsx[:]),
          reads=["rsx"], writes=["rsxb"])
    for h in range(NH):
        sl = h % 2
        sc.dma("gp", rs0[0:1, sl * 1024:sl * 1024 + 1024], rsxb[h:h + 1, :],
               reads=["rsxb"], writes=[f"rs0{sl}"])
        for c in range(2):
            sc.op("pe", lambda e, sl=sl, c=c: e.matmul(
                pA[:, c * 512:(c + 1) * 512], ones_r[0:1, :],
                rs0[0:1, sl * 1024 + c * 512:sl * 1024 + (c + 1) * 512],
                start=True, stop=True),
                reads=[f"rs0{sl}", "ones_r"], writes=[f"pA{c}"])
        sc.op("ve", lambda e, h=h: e.tensor_tensor(
            out=avu[:, h * 1024:(h + 1) * 1024],
            in0=avu[:, h * 1024:(h + 1) * 1024], in1=pA[:], op=ALU.mult),
            reads=["pA0", "pA1", "avu"], writes=["avu"])

    phase("p2")
    for nt in range(NKT):
        wsl = nt % 2
        for q4 in range(4):
            ssl = q4 % 2
            sc.dma("sy", wstg[:, ssl * 1024:ssl * 1024 + 1024].rearrange(
                       "p (t c) -> p t c", c=128),
                   wo_d[nt, q4 * 8:(q4 + 1) * 8].rearrange("t p c -> p t c"),
                   writes=[f"wstg{ssl}"])
            sc.op("ve", lambda e, q4=q4, ssl=ssl, wsl=wsl: e.tensor_copy(
                out=wbf[:, wsl * 4096 + q4 * 1024:wsl * 4096 + (q4 + 1) * 1024],
                in_=wstg[:, ssl * 1024:ssl * 1024 + 1024]),
                reads=[f"wstg{ssl}"], writes=[f"wbf{wsl}"])
        pk = "pB" if nt % 2 else "pA"
        ps = pB if nt % 2 else pA
        for dt_ in range(NKT):
            for c in range(2):
                sc.op("pe", lambda e, dt_=dt_, c=c, wsl=wsl, ps=ps: e.matmul(
                    ps[:, c * 512:(c + 1) * 512],
                    wbf[:, wsl * 4096 + dt_ * 128:wsl * 4096 + dt_ * 128 + 128],
                    avu[:, dt_ * 1024 + c * 512:dt_ * 1024 + (c + 1) * 512],
                    start=(dt_ == 0), stop=(dt_ == NKT - 1)),
                    reads=[f"wbf{wsl}", "avu"], writes=[f"{pk}{c}"])
        sc.op("ve", lambda e, nt=nt, ps=ps: e.tensor_tensor(
            out=xP[:, nt * 1024:(nt + 1) * 1024],
            in0=ps[:], in1=xP[:, nt * 1024:(nt + 1) * 1024], op=ALU.add),
            reads=[f"{pk}0", f"{pk}1", "xP"], writes=["xP"])

    phase("p3")
    sc.barrier()
    att.close()
    ln = ExitStack()
    sqb = mksb(ln, "sqb", [128, 1024 * 2], BF16)
    scr = mksb(ln, "scr", [128, 1024 * 2], F32)
    rowm = mksb(ln, "rowm", [1, 1024], F32)
    rowa = mksb(ln, "rowa", [1, 1024], F32)
    rowb = mksb(ln, "rowb", [1, 1024], F32)

    def layernorm(dst_is_out, outst=None):
        for kt in range(NKT):
            sl = kt % 2
            sc.op("ac", lambda e, kt=kt, sl=sl: e.activation(
                out=sqb[:, sl * 1024:(sl + 1) * 1024],
                in_=xP[:, kt * 1024:(kt + 1) * 1024], func=AF.Square),
                reads=["xP"], writes=[f"sqb{sl}"])
            for c in range(2):
                sc.op("pe", lambda e, kt=kt, c=c: e.matmul(
                    pA[0:1, c * 512:(c + 1) * 512], ones_c[:, 0:1],
                    xP[:, kt * 1024 + c * 512:kt * 1024 + (c + 1) * 512],
                    start=(kt == 0), stop=(kt == NKT - 1)),
                    reads=["ones_c", "xP"], writes=[f"pA{c}"])
                sc.op("pe", lambda e, kt=kt, c=c, sl=sl: e.matmul(
                    pB[0:1, c * 512:(c + 1) * 512], ones_c[:, 0:1],
                    sqb[:, sl * 1024 + c * 512:sl * 1024 + (c + 1) * 512],
                    start=(kt == 0), stop=(kt == NKT - 1)),
                    reads=["ones_c", f"sqb{sl}"], writes=[f"pB{c}"])
        sc.op("ve", lambda e: e.tensor_scalar(
            out=rowm[:], in0=pA[0:1, :], scalar1=1.0 / H, scalar2=None, op0=ALU.mult),
            reads=["pA0", "pA1"], writes=["rowm"])
        sc.op("ve", lambda e: e.tensor_tensor(
            out=rowa[:], in0=rowm[:], in1=rowm[:], op=ALU.mult),
            reads=["rowm"], writes=["rowa"])
        sc.op("ve", lambda e: e.scalar_tensor_tensor(
            out=rowb[:], in0=pB[0:1, :], scalar=1.0 / H, in1=rowa[:],
            op0=ALU.mult, op1=ALU.subtract),
            reads=["pB0", "pB1", "rowa"], writes=["rowb"])
        sc.op("ve", lambda e: e.tensor_scalar(
            out=rowb[:], in0=rowb[:], scalar1=EPS, scalar2=None, op0=ALU.add),
            reads=["rowb"], writes=["rowb"])
        sc.op("ac", lambda e: e.activation(out=rowa[:], in_=rowb[:], func=AF.Sqrt),
              reads=["rowb"], writes=["rowa"])
        sc.op("ve", lambda e: e.reciprocal(out=rowb[:], in_=rowa[:]),
              reads=["rowa"], writes=["rowb"])
        sc.op("ve", lambda e: e.tensor_tensor(
            out=rowa[:], in0=rowm[:], in1=rowb[:], op=ALU.mult),
            reads=["rowm", "rowb"], writes=["rowa"])
        for c in range(2):
            sc.op("pe", lambda e, c=c: e.matmul(
                pA[:, c * 512:(c + 1) * 512], ones_rf[0:1, :],
                rowb[0:1, c * 512:(c + 1) * 512], start=True, stop=True),
                reads=["ones_rf", "rowb"], writes=[f"pA{c}"])
            sc.op("pe", lambda e, c=c: e.matmul(
                pB[:, c * 512:(c + 1) * 512], ones_rf[0:1, :],
                rowa[0:1, c * 512:(c + 1) * 512], start=True, stop=True),
                reads=["ones_rf", "rowa"], writes=[f"pB{c}"])
        for kt in range(NKT):
            sl = kt % 2
            sc.op("ve", lambda e, kt=kt, sl=sl: e.tensor_tensor(
                out=scr[:, sl * 1024:(sl + 1) * 1024],
                in0=xP[:, kt * 1024:(kt + 1) * 1024], in1=pA[:], op=ALU.mult),
                reads=["xP", "pA0", "pA1"], writes=[f"scr{sl}"])
            if dst_is_out:
                o = sc.op("ve", lambda e, kt=kt, sl=sl: e.tensor_tensor(
                    out=outst[:, sl * 1024:(sl + 1) * 1024],
                    in0=scr[:, sl * 1024:(sl + 1) * 1024], in1=pB[:], op=ALU.subtract),
                    reads=[f"scr{sl}", "pB0", "pB1"], writes=[f"outst{sl}"])
                sc.dma("sy", out_d[kt * 128:kt * 128 + 128, :],
                       outst[:, sl * 1024:(sl + 1) * 1024],
                       reads=[f"outst{sl}"], writes=[f"outd{kt}"])
            else:
                sc.op("ve", lambda e, kt=kt, sl=sl: e.tensor_tensor(
                    out=xP[:, kt * 1024:(kt + 1) * 1024],
                    in0=scr[:, sl * 1024:(sl + 1) * 1024], in1=pB[:], op=ALU.subtract),
                    reads=[f"scr{sl}", "pB0", "pB1"], writes=["xP"])

    layernorm(False)
    phase("ln1")

    ffn = ExitStack()
    mtg = mksb(ffn, "mtg", [32, NKT * 128], BF16)
    mtu = mksb(ffn, "mtu", [32, NKT * 128], BF16)
    mgm = mksb(ffn, "mgm", [128, NKT * 32], BF16)
    mum = mksb(ffn, "mum", [128, NKT * 32], BF16)
    qzg = mksb(ffn, "qzg", [32, 1376], I32)
    qzu = mksb(ffn, "qzu", [32, 1376], I32)
    qzd = mksb(ffn, "qzd", [86, 512], I32)
    gio32 = mksb(ffn, "gio32", [32, 1], F32)
    gio86 = mksb(ffn, "gio86", [86, 1], F32)
    gior32 = mksb(ffn, "gior32", [128, 32], F32)
    gior86 = mksb(ffn, "gior86", [128, 86], F32)
    gcoli = mksb(ffn, "gcoli", [128, 32], I32)
    gcolf = mksb(ffn, "gcolf", [128, 32], F32)
    xgg = mksb(ffn, "xgg", [32, 1024], BF16)
    xgu = mksb(ffn, "xgu", [32, 1024], BF16)
    hga = mksb(ffn, "hga", [86, 1024], F32)
    hgn = mksb(ffn, "hgn", [86, 1024], BF16)
    sdbf = mksb(ffn, "sdbf", [86, H], BF16)
    zsd = mksb(ffn, "zsd", [86, H], BF16)
    sdf = mksb(ffn, "sdf", [86, 512 * 2], F32)
    hsb = mksb(ffn, "hsb", [128, 11 * 1024], BF16)
    qstg = mksb(ffn, "qstg", [128, 256 * 2], I32)
    t1b = mksb(ffn, "t1b", [128, 256 * 2], I32)
    w1s = mksb(ffn, "w1s", [128, 256 * 2], BF16)
    sgf = mksb(ffn, "sgf", [32, 256 * 2], F32)
    sgb = mksb(ffn, "sgb", [32, 256 * 2], BF16)
    zib = mksb(ffn, "zib", [32, 256 * 2], I32)
    zsb = mksb(ffn, "zsb", [32, 256 * 2], BF16)
    ysin = mksb(ffn, "ysin", [128, 512 * 2], F32)
    ysout = mksb(ffn, "ysout", [128, 512 * 2], F32)
    mtdp = mksb(ffn, "mtdp", [86, 11 * 128], BF16)
    mdp = mksb(ffn, "mdp", [128, 11 * 86], BF16)
    gdbi = mksb(ffn, "gdbi", [1, 128 * 2], I32)
    gdbb = mksb(ffn, "gdbb", [1, 128 * 2], BF16)
    grwi = mksb(ffn, "grwi", [1, 512 * 2], I32)
    grwb = mksb(ffn, "grwb", [1, 512 * 2], BF16)

    sc.dma("sy", qzg[:], zg_d, writes=["qzg"])
    sc.dma("sy", qzu[:], zu_d, writes=["qzu"])
    sc.dma("sy", qzd[:], zd_d, writes=["qzd"])
    sc.op("gp", lambda e: e.iota(gio32[:], [[1, 1]], base=0, channel_multiplier=1,
                                 allow_small_or_imprecise_dtypes=True), writes=["gio32"])
    sc.op("gp", lambda e: e.iota(gio86[:], [[1, 1]], base=0, channel_multiplier=1,
                                 allow_small_or_imprecise_dtypes=True), writes=["gio86"])
    sc.op("gp", lambda e: e.iota(gior32[:], [[1, 32]], base=0, channel_multiplier=0,
                                 allow_small_or_imprecise_dtypes=True), writes=["gior32"])
    sc.op("gp", lambda e: e.iota(gior86[:], [[1, 86]], base=0, channel_multiplier=0,
                                 allow_small_or_imprecise_dtypes=True), writes=["gior86"])

    for name, g_dram, mt, mm in [("g", gg_d, mtg, mgm), ("u", gu_d, mtu, mum)]:
        for ch in range(8):
            sl = ch % 2
            sc.dma("sy", grwi[0:1, sl * 512:sl * 512 + 512],
                   g_dram.rearrange("(o x) -> o x", o=1)[0:1, ch * 512:(ch + 1) * 512],
                   writes=[f"grwi{sl}"])
            sc.op("ve", lambda e, sl=sl: e.tensor_copy(
                out=grwb[0:1, sl * 512:sl * 512 + 512],
                in_=grwi[0:1, sl * 512:sl * 512 + 512]),
                reads=[f"grwi{sl}"], writes=[f"grwb{sl}"])
            sc.op("pe", lambda e, sl=sl: e.matmul(
                pD[0:32, 0:512], ones_r[0:1, 0:32],
                grwb[0:1, sl * 512:sl * 512 + 512], start=True, stop=True),
                reads=[f"grwb{sl}", "ones_r"], writes=["pD"])
            for s4 in range(4):
                kt = ch * 4 + s4
                sc.op("ve", lambda e, kt=kt, s4=s4, mt=mt: e.tensor_scalar(
                    out=mt[:, kt * 128:(kt + 1) * 128],
                    in0=pD[0:32, s4 * 128:(s4 + 1) * 128],
                    scalar1=gio32[:, 0:1], scalar2=None, op0=ALU.is_equal),
                    reads=["pD", "gio32"], writes=[f"mt{name}"])
        for kt in range(NKT):
            sc.dma("sy", gcoli[:, kt:kt + 1],
                   g_dram.rearrange("(a b) -> a b", b=1)[kt * 128:(kt + 1) * 128, :],
                   writes=[f"gcoli{kt}"])
        gcl = sc.op("ve", lambda e: e.tensor_copy(out=gcolf[:], in_=gcoli[:]),
                    reads=[f"gcoli{kt}" for kt in range(NKT)], writes=["gcolf"])
        for kt in range(NKT):
            sc.op("ve", lambda e, kt=kt, mm=mm: e.tensor_scalar(
                out=mm[:, kt * 32:(kt + 1) * 32], in0=gior32[:],
                scalar1=gcolf[:, kt:kt + 1], scalar2=None, op0=ALU.is_equal),
                reads=["gior32", "gcolf"], writes=[f"m{name}m"])

    for ch in range(8):
        sl = ch % 2
        sc.dma("sy", sdf[0:86, sl * 512:sl * 512 + 512], sd_d[:, ch * 512:(ch + 1) * 512],
               writes=[f"sdf{sl}"])
        sc.op("ve", lambda e, ch=ch, sl=sl: e.tensor_copy(
            out=sdbf[:, ch * 512:(ch + 1) * 512], in_=sdf[0:86, sl * 512:sl * 512 + 512]),
            reads=[f"sdf{sl}"], writes=["sdbf"])
        sc.op("ve", lambda e, ch=ch: e.tensor_scalar(
            out=t1b[0:86, 0:512], in0=qzd[:],
            scalar1=4 * ch, scalar2=15,
            op0=ALU.logical_shift_right, op1=ALU.bitwise_and),
            reads=["qzd"], writes=["t1b0", "t1b1"])
        sc.op("ve", lambda e, ch=ch, sl=sl: e.scalar_tensor_tensor(
            out=zsd[:, ch * 512:(ch + 1) * 512], in0=t1b[0:86, 0:512], scalar=1.0,
            in1=sdf[0:86, sl * 512:sl * 512 + 512], op0=ALU.add, op1=ALU.mult),
            reads=["t1b0", "t1b1", f"sdf{sl}"], writes=["zsd"])

    for name, mm, xg in [("g", mgm, xgg), ("u", mum, xgu)]:
        for c in range(2):
            for kt in range(NKT):
                sc.op("pe", lambda e, kt=kt, c=c, mm=mm: e.matmul(
                    pD[0:32, 0:512], mm[:, kt * 32:(kt + 1) * 32],
                    xP[:, kt * 1024 + c * 512:kt * 1024 + (c + 1) * 512],
                    start=(kt == 0), stop=(kt == NKT - 1)),
                    reads=[f"m{name}m", "xP"], writes=["pD"])
            sc.op("ve", lambda e, c=c, xg=xg: e.tensor_scalar(
                out=xg[:, c * 512:(c + 1) * 512], in0=pD[0:32, 0:512],
                scalar1=-1.0, scalar2=None, op0=ALU.mult),
                reads=["pD"], writes=[f"xg{name}"])

    phase("xg")
    ACC = [("pA0", pA, 0), ("pA1", pA, 512), ("pB0", pB, 0), ("pB1", pB, 512)]

    for part in range(NPART):
        d = part
        for b in range(11):
            rows = SLAB_ROWS[b]
            gsl = b % 2
            sc.dma("sy", gdbi[0:1, gsl * 128:gsl * 128 + rows],
                   gd_d.rearrange("(o x) -> o x", o=1)[0:1, d * 1376 + b * 128:d * 1376 + b * 128 + rows],
                   writes=[f"gdbi{gsl}"])
            sc.op("ve", lambda e, gsl=gsl, rows=rows: e.tensor_copy(
                out=gdbb[0:1, gsl * 128:gsl * 128 + rows],
                in_=gdbi[0:1, gsl * 128:gsl * 128 + rows]),
                reads=[f"gdbi{gsl}"], writes=[f"gdbb{gsl}"])
            sc.op("pe", lambda e, b=b, rows=rows, gsl=gsl: e.matmul(
                pD[0:86, 0:rows], ones_r[0:1, 0:86],
                gdbb[0:1, gsl * 128:gsl * 128 + rows], start=True, stop=True),
                reads=[f"gdbb{gsl}", "ones_r"], writes=["pD"])
            sc.op("ve", lambda e, b=b, rows=rows: e.tensor_scalar(
                out=mtdp[:, b * 128:b * 128 + rows], in0=pD[0:86, 0:rows],
                scalar1=gio86[:, 0:1], scalar2=None, op0=ALU.is_equal),
                reads=["pD", "gio86"], writes=["mtdp"])
            sc.dma("sy", gcoli[0:rows, b:b + 1],
                   gd_d.rearrange("(a b) -> a b", b=1)[d * 1376 + b * 128:d * 1376 + b * 128 + rows, :],
                   writes=[f"gcoli{b}"])
        sc.op("ve", lambda e: e.tensor_copy(out=gcolf[:, 0:11], in_=gcoli[:, 0:11]),
              reads=[f"gcoli{b}" for b in range(11)], writes=["gcolf"])
        for b in range(11):
            rows = SLAB_ROWS[b]
            sc.op("ve", lambda e, b=b, rows=rows: e.tensor_scalar(
                out=mdp[0:rows, b * 86:(b + 1) * 86], in0=gior86[0:rows, :],
                scalar1=gcolf[0:rows, b:b + 1], scalar2=None, op0=ALU.is_equal),
                reads=["gior86", "gcolf"], writes=["mdp"])

        for gi, (b0, nsl, width) in enumerate(GRPS):
            gcol = d * 1376 + b0 * 128
            for mat, (q_dram, s_dram, qz, mt, xg, xgk) in enumerate([
                    (qg_d, sg_d, qzg, mtg, xgg, "xgg"),
                    (qu_d, su_d, qzu, mtu, xgu, "xgu")]):
                bsl = (part * 12 + gi * 2 + mat) % 2
                sc.dma("sy", sgf[0:32, bsl * 256:bsl * 256 + width],
                       s_dram[:, gcol:gcol + width], writes=[f"sgf{bsl}"])
                sc.op("ve", lambda e, bsl=bsl, width=width: e.tensor_copy(
                    out=sgb[0:32, bsl * 256:bsl * 256 + width],
                    in_=sgf[0:32, bsl * 256:bsl * 256 + width]),
                    reads=[f"sgf{bsl}"], writes=[f"sgb{bsl}"])
                sc.op("ve", lambda e, qz=qz, d=d, b0=b0, width=width, bsl=bsl: e.tensor_scalar(
                    out=zib[0:32, bsl * 256:bsl * 256 + width],
                    in0=qz[:, b0 * 128:b0 * 128 + width], scalar1=4 * d, scalar2=15,
                    op0=ALU.logical_shift_right, op1=ALU.bitwise_and),
                    reads=["qzg" if mat == 0 else "qzu"], writes=[f"zib{bsl}"])
                sc.op("ve", lambda e, bsl=bsl, width=width: e.scalar_tensor_tensor(
                    out=zsb[0:32, bsl * 256:bsl * 256 + width],
                    in0=zib[0:32, bsl * 256:bsl * 256 + width], scalar=1.0,
                    in1=sgf[0:32, bsl * 256:bsl * 256 + width],
                    op0=ALU.add, op1=ALU.mult),
                    reads=[f"zib{bsl}", f"sgf{bsl}"], writes=[f"zsb{bsl}"])
                mtk = "mtg" if mat == 0 else "mtu"
                kts = [(j * 4 + kpb, kpb, j) for kpb in range(4) for j in range(8)]

                def sexp_mm(i, mt=mt, bsl=bsl, width=width, mtk=mtk):
                    kt = kts[i][0]
                    sesl = i % 2
                    sc.op("pe", lambda e, kt=kt, mt=mt, bsl=bsl, sesl=sesl, width=width: e.matmul(
                        PC[sesl][:, 0:width], mt[:, kt * 128:(kt + 1) * 128],
                        sgb[0:32, bsl * 256:bsl * 256 + width],
                        start=True, stop=True),
                        reads=[mtk, f"sgb{bsl}"], writes=[f"pC{sesl + 1}"])

                sexp_mm(0)
                for i, (kt, kpb, j) in enumerate(kts):
                    qsl = kpb % 2
                    if j == 0:
                        sc.dma("sy", qstg[:, qsl * 256:qsl * 256 + width],
                               q_dram[kpb * 128:kpb * 128 + 128, gcol:gcol + width],
                               writes=[f"qstg{qsl}"])
                    tsl = i % 2
                    sesl = i % 2
                    sc.op("ve", lambda e, j=j, qsl=qsl, tsl=tsl, width=width: e.tensor_scalar(
                        out=t1b[:, tsl * 256:tsl * 256 + width],
                        in0=qstg[:, qsl * 256:qsl * 256 + width],
                        scalar1=4 * j, scalar2=15,
                        op0=ALU.logical_shift_right, op1=ALU.bitwise_and),
                        reads=[f"qstg{qsl}"], writes=[f"t1b{tsl}"])
                    sc.op("ve", lambda e, tsl=tsl, sesl=sesl, width=width: e.scalar_tensor_tensor(
                        out=w1s[:, tsl * 256:tsl * 256 + width],
                        in0=t1b[:, tsl * 256:tsl * 256 + width], scalar=1.0,
                        in1=PC[sesl][:, 0:width], op0=ALU.mult, op1=ALU.mult),
                        reads=[f"t1b{tsl}", f"pC{sesl + 1}"], writes=[f"w1s{tsl}"])
                    if i + 1 < len(kts):
                        sexp_mm(i + 1)
                    for c in range(2):
                        for s in range(nsl):
                            ak, aps, aoff = ACC[c * 2 + s]
                            rows = SLAB_ROWS[b0 + s]
                            sc.op("pe", lambda e, kt=kt, c=c, s=s, tsl=tsl, rows=rows,
                                  aps=aps, aoff=aoff: e.matmul(
                                aps[0:rows, aoff:aoff + 512],
                                w1s[:, tsl * 256 + s * 128:tsl * 256 + s * 128 + rows],
                                xP[:, kt * 1024 + c * 512:kt * 1024 + (c + 1) * 512],
                                start=(kt == 0), stop=False),
                                reads=[f"w1s{tsl}", "xP"], writes=[ak])
                for c in range(2):
                    for s in range(nsl):
                        ak, aps, aoff = ACC[c * 2 + s]
                        rows = SLAB_ROWS[b0 + s]
                        sc.op("pe", lambda e, c=c, s=s, rows=rows, bsl=bsl, aps=aps,
                              aoff=aoff, xg=xg: e.matmul(
                            aps[0:rows, aoff:aoff + 512],
                            zsb[0:32, bsl * 256 + s * 128:bsl * 256 + s * 128 + rows],
                            xg[:, c * 512:(c + 1) * 512],
                            start=False, stop=True),
                            reads=[f"zsb{bsl}", xgk], writes=[ak])
                        hoff = (b0 + s) * 1024 + c * 512
                        if mat == 0:
                            sc.op("ac", lambda e, rows=rows, hoff=hoff, aps=aps, aoff=aoff: e.activation(
                                out=hsb[0:rows, hoff:hoff + 512],
                                in_=aps[0:rows, aoff:aoff + 512], func=AF.Silu),
                                reads=[ak], writes=["hsb"])
                            if part == 0 and gi == 0 and c == 1 and s == nsl - 1:
                                phase("dbg_g0")
                        else:
                            sc.op("ve", lambda e, rows=rows, hoff=hoff, aps=aps, aoff=aoff: e.tensor_tensor(
                                out=hsb[0:rows, hoff:hoff + 512],
                                in0=hsb[0:rows, hoff:hoff + 512],
                                in1=aps[0:rows, aoff:aoff + 512], op=ALU.mult),
                                reads=[ak, "hsb"], writes=["hsb"])

        for c in range(2):
            for b in range(11):
                rows = SLAB_ROWS[b]
                sc.op("pe", lambda e, b=b, c=c, rows=rows: e.matmul(
                    pD[0:86, 0:512], mdp[0:rows, b * 86:(b + 1) * 86],
                    hsb[0:rows, b * 1024 + c * 512:b * 1024 + (c + 1) * 512],
                    start=(b == 0), stop=(b == 10)),
                    reads=["mdp", "hsb"], writes=["pD"])
            if part == 0:
                sc.op("ve", lambda e, c=c: e.tensor_copy(
                    out=hga[:, c * 512:(c + 1) * 512], in_=pD[0:86, 0:512]),
                    reads=["pD"], writes=["hga"])
            else:
                sc.op("ve", lambda e, c=c: e.tensor_tensor(
                    out=hga[:, c * 512:(c + 1) * 512],
                    in0=hga[:, c * 512:(c + 1) * 512], in1=pD[0:86, 0:512], op=ALU.add),
                    reads=["pD", "hga"], writes=["hga"])
        if part == NPART - 1:
            sc.op("ve", lambda e: e.tensor_scalar(
                out=hgn[:], in0=hga[:], scalar1=-1.0, scalar2=None, op0=ALU.mult),
                reads=["hga"], writes=["hgn"])

        for nb in range(16):
            ncol = nb * 256
            def dsexp_mm(b, ncol=ncol):
                rows = SLAB_ROWS[b]
                sesl = b % 2
                sc.op("pe", lambda e, b=b, rows=rows, sesl=sesl, ncol=ncol: e.matmul(
                    PC[sesl][0:rows, 0:256], mtdp[:, b * 128:b * 128 + rows],
                    sdbf[:, ncol:ncol + 256], start=True, stop=True),
                    reads=["mtdp", "sdbf"], writes=[f"pC{sesl + 1}"])

            dsexp_mm(0)
            for b in range(11):
                rows = SLAB_ROWS[b]
                qsl = b % 2
                sc.dma("sy", qstg[0:rows, qsl * 256:qsl * 256 + 256],
                       qd_d[b * 128:b * 128 + rows, ncol:ncol + 256],
                       writes=[f"qstg{qsl}"])
                tsl = b % 2
                sc.op("ve", lambda e, d=d, rows=rows, qsl=qsl, tsl=tsl: e.tensor_scalar(
                    out=t1b[0:rows, tsl * 256:tsl * 256 + 256],
                    in0=qstg[0:rows, qsl * 256:qsl * 256 + 256],
                    scalar1=4 * d, scalar2=15,
                    op0=ALU.logical_shift_right, op1=ALU.bitwise_and),
                    reads=[f"qstg{qsl}"], writes=[f"t1b{tsl}"])
                sesl = b % 2
                sc.op("ve", lambda e, rows=rows, tsl=tsl, sesl=sesl: e.scalar_tensor_tensor(
                    out=w1s[0:rows, tsl * 256:tsl * 256 + 256],
                    in0=t1b[0:rows, tsl * 256:tsl * 256 + 256], scalar=1.0,
                    in1=PC[sesl][0:rows, 0:256], op0=ALU.mult, op1=ALU.mult),
                    reads=[f"t1b{tsl}", f"pC{sesl + 1}"], writes=[f"w1s{tsl}"])
                if b + 1 < 11:
                    dsexp_mm(b + 1)
                for nt2 in range(2):
                    for c in range(2):
                        ak, aps, aoff = ACC[nt2 * 2 + c]
                        sc.op("pe", lambda e, b=b, rows=rows, nt2=nt2, c=c, tsl=tsl,
                              aps=aps, aoff=aoff: e.matmul(
                            aps[:, aoff:aoff + 512],
                            w1s[0:rows, tsl * 256 + nt2 * 128:tsl * 256 + nt2 * 128 + 128],
                            hsb[0:rows, b * 1024 + c * 512:b * 1024 + (c + 1) * 512],
                            start=(b == 0), stop=(b == 10 and part != NPART - 1)),
                            reads=[f"w1s{tsl}", "hsb"], writes=[ak])
            for nt2 in range(2):
                for c in range(2):
                    ak, aps, aoff = ACC[nt2 * 2 + c]
                    ntg = nb * 2 + nt2
                    if part == NPART - 1:
                        sc.op("pe", lambda e, ntg=ntg, c=c, aps=aps, aoff=aoff: e.matmul(
                            aps[:, aoff:aoff + 512],
                            zsd[:, ntg * 128:ntg * 128 + 128],
                            hgn[:, c * 512:(c + 1) * 512], start=False, stop=True),
                            reads=["zsd", "hgn"], writes=[ak])
                    usl = (nb * 4 + nt2 * 2 + c) % 2
                    if part > 0:
                        sc.dma("sy", ysin[:, usl * 512:usl * 512 + 512],
                               ysp_d[ntg * 128:ntg * 128 + 128, c * 512:(c + 1) * 512],
                               reads=[f"ysp{ntg}_{c}"], writes=[f"ysin{usl}"])
                    if part == 0:
                        sc.op("ve", lambda e, usl=usl, aps=aps, aoff=aoff: e.tensor_copy(
                            out=ysout[:, usl * 512:usl * 512 + 512],
                            in_=aps[:, aoff:aoff + 512]),
                            reads=[ak], writes=[f"ysout{usl}"])
                        sc.dma("sy", ysp_d[ntg * 128:ntg * 128 + 128, c * 512:(c + 1) * 512],
                               ysout[:, usl * 512:usl * 512 + 512],
                               reads=[f"ysout{usl}"], writes=[f"ysp{ntg}_{c}"])
                    elif part < NPART - 1:
                        sc.op("ve", lambda e, usl=usl, aps=aps, aoff=aoff: e.tensor_tensor(
                            out=ysout[:, usl * 512:usl * 512 + 512],
                            in0=ysin[:, usl * 512:usl * 512 + 512],
                            in1=aps[:, aoff:aoff + 512], op=ALU.add),
                            reads=[ak, f"ysin{usl}"], writes=[f"ysout{usl}"])
                        sc.dma("sy", ysp_d[ntg * 128:ntg * 128 + 128, c * 512:(c + 1) * 512],
                               ysout[:, usl * 512:usl * 512 + 512],
                               reads=[f"ysout{usl}"], writes=[f"ysp{ntg}_{c}"])
                    else:
                        ssl = usl
                        sc.op("ve", lambda e, usl=usl, ssl=ssl, aps=aps, aoff=aoff: e.tensor_tensor(
                            out=scr[:, ssl * 1024:ssl * 1024 + 512],
                            in0=ysin[:, usl * 512:usl * 512 + 512],
                            in1=aps[:, aoff:aoff + 512], op=ALU.add),
                            reads=[ak, f"ysin{usl}"], writes=[f"scr{ssl}"])
                        sc.op("ve", lambda e, ntg=ntg, c=c, ssl=ssl: e.tensor_tensor(
                            out=xP[:, ntg * 1024 + c * 512:ntg * 1024 + (c + 1) * 512],
                            in0=scr[:, ssl * 1024:ssl * 1024 + 512],
                            in1=xP[:, ntg * 1024 + c * 512:ntg * 1024 + (c + 1) * 512],
                            op=ALU.add),
                            reads=[f"scr{ssl}", "xP"], writes=["xP"])

        phase(f"part{part}")
    sc.barrier()
    ffn.close()

    p6 = ExitStack()
    outst = mksb(p6, "outst", [128, 1024 * 2], F32)
    layernorm(True, outst)
    p6.close()
    ln.close()

    sems = {}
    es_sem = ExitStack()
    for e in ENGS:
        sems[e] = es_sem.enter_context(nc.semaphore("s_" + e))
        sems["bulk_" + e] = es_sem.enter_context(nc.semaphore("b_" + e))

    def mksem(name):
        return es_sem.enter_context(nc.semaphore(name[:24]))

    with nc.Block() as block:
        sc.emit(nc, block, sems, mksem, prefix=_STOP_AFTER)
    es_sem.close()
    es_top.close()
    g.close()
    return nc, taps



_PROG = None


def _get_prog():
    global _PROG
    if _PROG is None:
        _PROG = build_program()
    return _PROG


def kernel(L0_attn_hidden_states, L0_attn_Wq, L0_attn_Wkv, L0_attn_Wo,
           L0_ffn_gate_qweight, L0_ffn_gate_scales, L0_ffn_gate_qzeros, L0_ffn_gate_g_idx,
           L0_ffn_up_qweight, L0_ffn_up_scales, L0_ffn_up_qzeros, L0_ffn_up_g_idx,
           L0_ffn_down_qweight, L0_ffn_down_scales, L0_ffn_down_qzeros, L0_ffn_down_g_idx):
    x = np.asarray(L0_attn_hidden_states, dtype=np.float32)
    Wq = np.asarray(L0_attn_Wq, dtype=np.float32)
    Wkv = np.asarray(L0_attn_Wkv, dtype=np.float32)
    Wo = np.asarray(L0_attn_Wo, dtype=np.float32)

    slopes = _alibi_slopes(NH)
    permH, permI = PERM_H, PERM_I

    Wq_p = Wq[permH, :]
    wq = np.ascontiguousarray(
        Wq_p.reshape(NKT, 128, NKT, 128).transpose(2, 0, 1, 3))
    Wkv_p = Wkv[permH, :]
    wk = np.ascontiguousarray(Wkv_p[:, :128].reshape(NKT, 128, 128))
    wv = np.ascontiguousarray(Wkv_p[:, 128:].reshape(NKT, 128, 128))
    wo = np.ascontiguousarray(
        Wo[:, permH].reshape(NKT, 128, NKT, 128).transpose(2, 0, 1, 3))
    qg = np.ascontiguousarray(np.asarray(L0_ffn_gate_qweight, dtype=np.int32)[:, permI])
    qu = np.ascontiguousarray(np.asarray(L0_ffn_up_qweight, dtype=np.int32)[:, permI])
    qd = np.ascontiguousarray(np.asarray(L0_ffn_down_qweight, dtype=np.int32)[:, permH])
    sg = np.ascontiguousarray(np.asarray(L0_ffn_gate_scales, dtype=np.float32)[:, permI])
    su = np.ascontiguousarray(np.asarray(L0_ffn_up_scales, dtype=np.float32)[:, permI])
    sd = np.ascontiguousarray(np.asarray(L0_ffn_down_scales, dtype=np.float32)[:, permH])
    zg = np.ascontiguousarray(np.asarray(L0_ffn_gate_qzeros, dtype=np.int32))
    zu = np.ascontiguousarray(np.asarray(L0_ffn_up_qzeros, dtype=np.int32))
    zd = np.ascontiguousarray(np.asarray(L0_ffn_down_qzeros, dtype=np.int32))
    gg = np.ascontiguousarray(np.asarray(L0_ffn_gate_g_idx, dtype=np.int32)[permH])
    gu = np.ascontiguousarray(np.asarray(L0_ffn_up_g_idx, dtype=np.int32)[permH])
    gd = np.ascontiguousarray(np.asarray(L0_ffn_down_g_idx, dtype=np.int32)[permI])

    idn = np.eye(128, dtype=np.float32).astype(ml_dtypes.bfloat16)
    onesr = np.ones((1, 128), dtype=ml_dtypes.bfloat16)
    onesc = np.ones((128, 1), dtype=ml_dtypes.bfloat16)
    onesrf = np.ones((1, 128), dtype=np.float32)
    abias = np.empty((128, NH * NST), dtype=np.float32)
    p = np.arange(128, dtype=np.float32)
    for h in range(NH):
        for j in range(NST):
            abias[:, h * 16 + j] = slopes[h] * (128 * j + p)

    in_maps = []
    core_cols = []
    for core in range(8):
        b = core // 2
        strips = STRIPS_A if core % 2 == 0 else STRIPS_B
        cols = np.concatenate([np.arange(64 * s, 64 * s + 64) for s in strips])
        core_cols.append((b, cols))
        xt = np.ascontiguousarray(x[b].T[permH, :])
        xp = np.ascontiguousarray(xt[:, cols])
        amask = np.zeros((128, NST * 64), dtype=np.float32)
        for j in range(NST):
            dstrip = [s for s in strips if s in (2 * j, 2 * j + 1)][0]
            kk = 128 * j + p[:, None]
            qq = 64 * dstrip + np.arange(64)[None, :]
            amask[:, j * 64:(j + 1) * 64] = np.where(kk <= qq, 0.0, NEG)
        aqsl = (-slopes[:, None] * cols[None, :].astype(np.float32)).astype(ml_dtypes.bfloat16)
        in_maps.append(dict(
            xp=xp, xt=xt, wq=wq, wk=wk, wv=wv, wo=wo, qg=qg, qu=qu, qd=qd,
            sg=sg, su=su, sd=sd, zg=zg, zu=zu, zd=zd, gg=gg, gu=gu, gd=gd,
            amask=amask.astype(ml_dtypes.bfloat16), abias=abias, aqsl=aqsl,
            idn=idn, onesr=onesr, onesc=onesc, onesrf=onesrf))

    nc, taps = _get_prog()
    res = run_bass_kernel_spmd(nc, in_maps, core_ids=list(range(8)))

    inv = np.empty(H, dtype=np.int64)
    inv[permH] = np.arange(H)
    full = np.empty((B, S, H), dtype=np.float32)
    for core in range(8):
        b, cols = core_cols[core]
        y = np.asarray(res.results[core]["out"])
        full[b, cols, :] = y[inv, :].T
    if _DEBUG_TAPS:
        kernel._last_results = res
    return full


if __name__ == "__main__":
    nc, taps = build_program()
    print("build ok")


# version 07
# speedup vs baseline: 1.7504x; 1.1100x over previous
import math
import numpy as np
import ml_dtypes
import concourse.bass as bass
import concourse.mybir as mybir
from concourse.bass_utils import run_bass_kernel_spmd
from contextlib import ExitStack

F32 = mybir.dt.float32
BF16 = mybir.dt.bfloat16
I32 = mybir.dt.int32
AF = mybir.ActivationFunctionType
ALU = mybir.AluOpType

B, S, H = 4, 2048, 4096
NH, HD = 32, 128
INTER = 11008
NTOK = 1024
NKT = 32
NST = 16
SCALE = HD ** -0.5
EPS = 1e-5
NEG = -30000.0
NPART = 8
GRPS = [(0, 2, 256), (2, 2, 256), (4, 2, 256), (6, 2, 256), (8, 2, 256), (10, 1, 96)]
SLAB_ROWS = [128] * 10 + [96]

_DEBUG_TAPS = []
_STOP_AFTER = None


class _StopBuild(Exception):
    pass


def _alibi_slopes(n):
    cp = 2 ** math.floor(math.log2(n))
    base = 2.0 ** (-(2.0 ** (-(math.log2(cp) - 3))))
    sl = [base ** i for i in range(1, cp + 1)]
    if cp != n:
        eb = 2.0 ** (-(2.0 ** (-(math.log2(2 * cp) - 3))))
        ne = min(cp, n - cp)
        sl += [eb ** i for i in range(1, 2 * ne + 1, 2)]
    return np.asarray(sl, dtype=np.float32)


def _perm(n_rows):
    return (8 * np.arange(n_rows)[None, :] + np.arange(8)[:, None]).reshape(-1)


PERM_H = _perm(512)
PERM_I = _perm(1376)
STRIPS_A = [s for s in range(32) if s % 4 in (0, 3)]
STRIPS_B = [s for s in range(32) if s % 4 in (1, 2)]


def att_chunks(j):
    qs = 64 * j
    if qs < 512:
        return [(qs, 512 - qs), (512, 512)]
    return [(qs, 1024 - qs)]



ENGS = ("sy", "gp", "pe", "ve", "ac")


class Op:
    __slots__ = ("eng", "fn", "waits", "is_dma", "marked", "dmakey", "seq", "isbar")

    def __init__(self, eng, fn, is_dma):
        self.eng, self.fn, self.is_dma = eng, fn, is_dma
        self.waits, self.marked, self.dmakey = [], False, None
        self.seq, self.isbar = 0, False


class Sched:
    def __init__(self):
        self.ops = {e: [] for e in ENGS}
        self.writers = {}
        self.readers = {}
        self.seqc = 0

    def barrier(self):
        for e in ENGS:
            o = self.op(e, lambda eng: eng.nop())
            o.isbar = True

    def _edge(self, a, b):
        if a is b or (a.eng == b.eng and not a.is_dma):
            return
        a.marked = True
        b.waits.append(a)

    def op(self, eng, fn, reads=(), writes=(), after=(), is_dma=False):
        o = Op(eng, fn, is_dma)
        if is_dma:
            o.marked = True
            o.dmakey = writes[0] if writes else None
        for a in after:
            if a is not None:
                self._edge(a, o)
        for bk in reads:
            for w in self.writers.get(bk, {}).values():
                self._edge(w, o)
        for bk in writes:
            for w in self.writers.get(bk, {}).values():
                self._edge(w, o)
            for r in self.readers.get(bk, {}).values():
                self._edge(r, o)
        for bk in reads:
            self.readers.setdefault(bk, {})[o.eng] = o
        for bk in writes:
            self.writers[bk] = {o.eng: o}
            self.readers[bk] = {}
        self.seqc += 1
        o.seq = self.seqc
        self.ops[eng].append(o)
        return o

    def dma(self, q, out_ap, in_ap, reads=(), writes=(), after=()):
        return self.op(q, (lambda e, o=out_ap, i=in_ap: e.dma_start(out=o, in_=i)),
                       reads=reads, writes=writes, after=after, is_dma=True)

    def snap(self, name):
        if not hasattr(self, "snaps"):
            self.snaps = {}
        self.snaps[name] = {e: len(self.ops[e]) for e in ENGS}

    def emit(self, nc, block, sems, mksem, prefix=None):
        lens = self.snaps[prefix] if prefix else {e: len(self.ops[e]) for e in ENGS}
        def dsem(o):
            k = o.dmakey
            if k is None or k.startswith(("ysp", "outd", "tap_")):
                return "bulk_" + o.eng
            return "d_" + k

        val = {}
        semname = {}
        tot = {}
        cnt = {}
        events = []
        for e in ENGS:
            for o in self.ops[e][:lens[e]]:
                if o.isbar:
                    o.marked = True
                if not o.marked:
                    continue
                sn = dsem(o) if o.is_dma else e
                semname[id(o)] = sn
                cnt[sn] = cnt.get(sn, 0) + (16 if o.is_dma else 1)
                val[id(o)] = cnt[sn]
                tot.setdefault(e, {})[sn] = cnt[sn]
                events.append((o.seq, sn, cnt[sn]))
        for sn in cnt:
            if sn not in sems:
                sems[sn] = mksem(sn)
        events.sort()
        eseqs = [ev[0] for ev in events]
        import bisect

        def snapshot(seq):
            snap = {}
            for i in range(bisect.bisect_left(eseqs, seq)):
                _, sn, v = events[i]
                snap[sn] = max(snap.get(sn, 0), v)
            return snap

        def mk(ename):
            def body(eng):
                cache = {}
                for o in self.ops[ename][:lens[ename]]:
                    if o.isbar:
                        for sn, v in snapshot(o.seq).items():
                            if cache.get(sn, 0) < v:
                                eng.wait_ge(sems[sn], v)
                                cache[sn] = v
                    for a in o.waits:
                        v = val.get(id(a))
                        if v is None:
                            continue
                        sn = semname[id(a)]
                        if cache.get(sn, 0) < v:
                            eng.wait_ge(sems[sn], v)
                            cache[sn] = v
                    instr = o.fn(eng)
                    if o.marked:
                        instr.then_inc(sems[semname[id(o)]], 16 if o.is_dma else 1)
                if ename in ("sy", "gp"):
                    for sn, v in tot.get(ename, {}).items():
                        if cache.get(sn, 0) < v:
                            eng.wait_ge(sems[sn], v)
                            cache[sn] = v
            return body

        block.sync(mk("sy"))
        block.gpsimd(mk("gp"))
        block.tensor(mk("pe"))
        block.vector(mk("ve"))
        block.scalar(mk("ac"))



def build_program():
    nc = bass.Bass("TRN2", target_bir_lowering=False)
    sc = Sched()
    es_top = ExitStack()
    taps = {}

    def din(name, shape, dtype):
        return nc.dram_tensor(name, shape, dtype, kind="ExternalInput").ap()

    xp_d = din("xp", [H, NTOK], F32)
    xt_d = din("xt", [H, S], F32)
    wq_d = din("wq", [NH, NKT, 128, 128], F32)
    wk_d = din("wk", [NKT, 128, 128], F32)
    wv_d = din("wv", [NKT, 128, 128], F32)
    wo_d = din("wo", [NKT, NKT, 128, 128], F32)
    qg_d = din("qg", [512, INTER], I32)
    qu_d = din("qu", [512, INTER], I32)
    qd_d = din("qd", [1376, H], I32)
    sg_d = din("sg", [32, INTER], F32)
    su_d = din("su", [32, INTER], F32)
    sd_d = din("sd", [86, H], F32)
    zg_d = din("zg", [32, 1376], I32)
    zu_d = din("zu", [32, 1376], I32)
    zd_d = din("zd", [86, 512], I32)
    gg_d = din("gg", [H], I32)
    gu_d = din("gu", [H], I32)
    gd_d = din("gd", [INTER], I32)
    amask_d = din("amask", [128, NST * 64], BF16)
    abias_d = din("abias", [128, NH * NST], F32)
    aqsl_d = din("aqsl", [NH, NTOK], BF16)
    idn_d = din("idn", [128, 128], BF16)
    onesr_d = din("onesr", [1, 128], BF16)
    onesc_d = din("onesc", [128, 1], BF16)
    onesrf_d = din("onesrf", [1, 128], F32)
    out_d = nc.dram_tensor("out", [H, NTOK], F32, kind="ExternalOutput").ap()
    ysp_d = nc.dram_tensor("yspill", [H, NTOK], F32, kind="Internal").ap()

    def mksb(stack, name, shape, dtype):
        t = stack.enter_context(nc.sbuf_tensor(name, shape, dtype))
        allsb[name] = t
        taps[name] = t
        return t

    allsb = {}

    def phase(name):
        for tname in _DEBUG_TAPS.get(name, []) if isinstance(_DEBUG_TAPS, dict) else []:
            t = allsb[tname]
            sh = list(t.shape)
            td = nc.dram_tensor(f"tap_{name}_{tname}", sh, t.dtype,
                                kind="ExternalOutput").ap()
            sc.dma("sy", td, t[:], reads=[tname, tname + "0", tname + "1"])
        sc.snap(name)

    pA = es_top.enter_context(nc.psum_tensor("pA", [128, 1024], F32))
    pB = es_top.enter_context(nc.psum_tensor("pB", [128, 1024], F32))
    pC1 = es_top.enter_context(nc.psum_tensor("pC1", [128, 512], F32))
    pC2 = es_top.enter_context(nc.psum_tensor("pC2", [128, 512], F32))
    pD = es_top.enter_context(nc.psum_tensor("pD", [128, 512], F32))
    pE = es_top.enter_context(nc.psum_tensor("pE", [128, 1024], BF16))
    PC = [pC1, pC2]

    g = ExitStack()
    xP = mksb(g, "xP", [128, NKT * 1024], BF16)
    ones_r = mksb(g, "ones_r", [1, 128], BF16)
    ones_c = mksb(g, "ones_c", [128, 1], BF16)
    ones_rf = mksb(g, "ones_rf", [1, 128], F32)
    idn_sb = mksb(g, "idn_sb", [128, 128], BF16)

    sc.dma("sy", idn_sb[:], idn_d, writes=["idn"])
    sc.dma("sy", ones_r[:], onesr_d, writes=["ones_r"])
    sc.dma("sy", ones_c[:], onesc_d, writes=["ones_c"])
    sc.dma("sy", ones_rf[:], onesrf_d, writes=["ones_rf"])

    att = ExitStack()
    ktb = mksb(att, "ktb", [128, S], BF16)
    vtok = mksb(att, "vtok", [128, S], BF16)
    qhat = mksb(att, "qhat", [128, 1024 * 2], BF16)
    probs = mksb(att, "probs", [128, 512 * 3], BF16)
    wstg = mksb(att, "wstg", [128, 1024 * 2], F32)
    wbf = mksb(att, "wbf", [128, 4096 * 2], BF16)
    sumr = mksb(att, "sumr", [1, 1024], F32)
    sumx = mksb(att, "sumx", [32, 1024], F32)
    rsx = mksb(att, "rsx", [32, 1024], F32)
    rsxb = mksb(att, "rsxb", [32, 1024], BF16)
    rs0 = mksb(att, "rs0", [1, 1024 * 2], BF16)
    aqs = mksb(att, "aqs", [1, 1024 * 2], BF16)
    amask = mksb(att, "amask_sb", [128, NST * 64], BF16)
    abias = mksb(att, "abias_sb", [128, NH * NST], F32)
    p1 = ExitStack()
    xstg = mksb(p1, "xstg", [128, 1024 * 2], F32)
    xtb = mksb(p1, "xtb", [128, 1024 * 2], BF16)
    wkb = mksb(p1, "wkb", [128, 4096], BF16)
    wvb = mksb(p1, "wvb", [128, 4096], BF16)

    sc.dma("sy", amask[:], amask_d, writes=["amask"])
    sc.dma("sy", abias[:], abias_d, writes=["abias"])
    phase("p0")

    for kt in range(NKT):
        sl = kt % 2
        sc.dma("sy", xstg[:, sl * 1024:sl * 1024 + 1024],
               xp_d[kt * 128:kt * 128 + 128, :], writes=[f"xstg{sl}"])
        sc.op("ve", lambda e, kt=kt, sl=sl: e.tensor_copy(
            out=xP[:, kt * 1024:(kt + 1) * 1024],
            in_=xstg[:, sl * 1024:sl * 1024 + 1024]),
            reads=[f"xstg{sl}"], writes=["xP"])

    phase("px")
    for t, (wsrc, wdst, key) in enumerate([(wk_d, wkb, "wkb"), (wv_d, wvb, "wvb")]):
        for q4 in range(4):
            sl = q4 % 2
            sc.dma("sy", xstg[:, sl * 1024:sl * 1024 + 1024].rearrange(
                       "p (t c) -> p t c", c=128),
                   wsrc[q4 * 8:(q4 + 1) * 8].rearrange("t p c -> p t c"),
                   writes=[f"xstg{sl}"])
            sc.op("ve", lambda e, q4=q4, sl=sl, wdst=wdst: e.tensor_copy(
                out=wdst[:, q4 * 1024:(q4 + 1) * 1024],
                in_=xstg[:, sl * 1024:sl * 1024 + 1024]),
                reads=[f"xstg{sl}"], writes=[key])

    phase("pw")
    for sh in range(2):
        for kt in range(NKT):
            sl = kt % 2
            sc.dma("sy", xstg[:, sl * 1024:sl * 1024 + 1024],
                   xt_d[kt * 128:kt * 128 + 128, sh * 1024:sh * 1024 + 1024],
                   writes=[f"xstg{sl}"])
            sc.op("ve", lambda e, sl=sl: e.tensor_copy(
                out=xtb[:, sl * 1024:sl * 1024 + 1024],
                in_=xstg[:, sl * 1024:sl * 1024 + 1024]),
                reads=[f"xstg{sl}"], writes=[f"xtb{sl}"])
            for c in range(2):
                sc.op("pe", lambda e, kt=kt, sl=sl, c=c: e.matmul(
                    pA[:, c * 512:(c + 1) * 512],
                    wkb[:, kt * 128:(kt + 1) * 128],
                    xtb[:, sl * 1024 + c * 512:sl * 1024 + (c + 1) * 512],
                    start=(kt == 0), stop=(kt == NKT - 1)),
                    reads=[f"xtb{sl}", "wkb"], writes=[f"pA{c}"])
            for i in range(8):
                sc.op("pe", lambda e, kt=kt, sl=sl, i=i: e.matmul(
                    pB[:, i * 128:(i + 1) * 128],
                    xtb[:, sl * 1024 + i * 128:sl * 1024 + (i + 1) * 128],
                    wvb[:, kt * 128:(kt + 1) * 128],
                    start=(kt == 0 and i % 4 == 0), stop=(kt == NKT - 1)),
                    reads=[f"xtb{sl}", "wvb"], writes=[f"pB{i // 4}"])
        phase(f"kv{sh}mm")
        sc.op("ac", lambda e, sh=sh: e.activation(
            out=ktb[:, sh * 1024:(sh + 1) * 1024], in_=pA[:], func=AF.Copy),
            reads=["pA0", "pA1"], writes=["ktb"])
        sc.op("ac", lambda e, sh=sh: e.activation(
            out=vtok[:, sh * 1024:(sh + 1) * 1024], in_=pB[:], func=AF.Copy),
            reads=["pB0", "pB1"], writes=["vtok"])
        phase(f"kv{sh}ev")

    phase("p1")
    p1.close()
    avu = mksb(att, "avu", [128, NH * 1024], BF16)
    for h in range(NH):
        wsl = h % 2
        for q4 in range(4):
            ssl = q4 % 2
            sc.dma("sy", wstg[:, ssl * 1024:ssl * 1024 + 1024].rearrange(
                       "p (t c) -> p t c", c=128),
                   wq_d[h, q4 * 8:(q4 + 1) * 8].rearrange("t p c -> p t c"),
                   writes=[f"wstg{ssl}"])
            sc.op("ve", lambda e, q4=q4, ssl=ssl, wsl=wsl: e.tensor_copy(
                out=wbf[:, wsl * 4096 + q4 * 1024:wsl * 4096 + (q4 + 1) * 1024],
                in_=wstg[:, ssl * 1024:ssl * 1024 + 1024]),
                reads=[f"wstg{ssl}"], writes=[f"wbf{wsl}"])
        sc.dma("gp", aqs[0:1, wsl * 1024:wsl * 1024 + 1024], aqsl_d[h:h + 1, :],
               writes=[f"aqs{wsl}"])
        for kt in range(NKT):
            for c in range(2):
                sc.op("pe", lambda e, kt=kt, c=c, wsl=wsl: e.matmul(
                    pA[:, c * 512:(c + 1) * 512],
                    wbf[:, wsl * 4096 + kt * 128:wsl * 4096 + kt * 128 + 128],
                    xP[:, kt * 1024 + c * 512:kt * 1024 + (c + 1) * 512],
                    start=(kt == 0), stop=(kt == NKT - 1)),
                    reads=[f"wbf{wsl}", "xP"], writes=[f"pA{c}"])
        qsl = h % 2
        sc.op("ac", lambda e, qsl=qsl: e.activation(
            out=qhat[:, qsl * 1024:(qsl + 1) * 1024], in_=pA[:], func=AF.Copy,
            scale=SCALE), reads=["pA0", "pA1"], writes=[f"qhat{qsl}"])
        units = []
        for j in range(NST):
            for ci, (off, w) in enumerate(att_chunks(j)):
                units.append((j, ci, off, w))

        def att_scores(pi):
            j, ci, off, w = units[pi]
            spi = pi % 2
            skey = f"pC{spi + 1}"
            sc.op("pe", lambda e, j=j, off=off, w=w, spi=spi, qsl=qsl: e.matmul(
                PC[spi][:, 0:w], ktb[:, j * 128:(j + 1) * 128],
                qhat[:, qsl * 1024 + off:qsl * 1024 + off + w],
                start=True, stop=False),
                reads=["ktb", f"qhat{qsl}"], writes=[skey])
            sc.op("pe", lambda e, off=off, w=w, spi=spi, wsl=wsl: e.matmul(
                PC[spi][:, 0:w], ones_r[0:1, :],
                aqs[0:1, wsl * 1024 + off:wsl * 1024 + off + w],
                start=False, stop=True),
                reads=["ones_r", f"aqs{wsl}"], writes=[skey])

        att_scores(0)
        for pi in range(len(units)):
            j, ci, off, w = units[pi]
            spi = pi % 2
            ppi = pi % 3
            skey = f"pC{spi + 1}"
            if ci == 0:
                sc.op("ve", lambda e, j=j, spi=spi: e.tensor_tensor(
                    out=PC[spi][:, 0:64], in0=PC[spi][:, 0:64],
                    in1=amask[:, j * 64:j * 64 + 64], op=ALU.add),
                    reads=[skey, "amask"], writes=[skey])
            sc.op("ac", lambda e, j=j, w=w, spi=spi, ppi=ppi, h=h: e.activation(
                out=probs[:, ppi * 512:ppi * 512 + w], in_=PC[spi][:, 0:w],
                func=AF.Exp, bias=abias[:, h * 16 + j:h * 16 + j + 1]),
                reads=[skey, "abias"], writes=[f"probs{ppi}"])
            if pi + 1 < len(units):
                att_scores(pi + 1)
            bank = off // 512
            stop_s = (j == 7 and ci == 0) or (j == 15)
            sc.op("pe", lambda e, off=off, w=w, ppi=ppi, j=j, stop_s=stop_s: e.matmul(
                pA[0:1, off:off + w], ones_c[:, 0:1],
                probs[:, ppi * 512:ppi * 512 + w],
                start=(j == 0), stop=stop_s),
                reads=["ones_c", f"probs{ppi}"], writes=[f"pA{bank}"])
            sc.op("pe", lambda e, off=off, w=w, ppi=ppi, j=j, stop_s=stop_s: e.matmul(
                pB[:, off:off + w], vtok[:, j * 128:(j + 1) * 128],
                probs[:, ppi * 512:ppi * 512 + w],
                start=(j == 0), stop=stop_s),
                reads=["vtok", f"probs{ppi}"], writes=[f"pB{bank}"])
        sc.op("ac", lambda e, h=h: e.activation(
            out=avu[:, h * 1024:(h + 1) * 1024], in_=pB[:], func=AF.Copy),
            reads=["pB0", "pB1"], writes=["avu"])
        sc.op("ac", lambda e: e.activation(out=sumr[:], in_=pA[0:1, :], func=AF.Copy),
              reads=["pA0", "pA1"], writes=["sumr"])
        sc.dma("gp", sumx[h:h + 1, :], sumr[:], reads=["sumr"], writes=["sumx"])

    sc.op("ve", lambda e: e.reciprocal(out=rsx[:], in_=sumx[:]),
          reads=["sumx"], writes=["rsx"])
    sc.op("ve", lambda e: e.tensor_copy(out=rsxb[:], in_=rsx[:]),
          reads=["rsx"], writes=["rsxb"])
    for h in range(NH):
        sl = h % 2
        sc.dma("gp", rs0[0:1, sl * 1024:sl * 1024 + 1024], rsxb[h:h + 1, :],
               reads=["rsxb"], writes=[f"rs0{sl}"])
        for c in range(2):
            sc.op("pe", lambda e, sl=sl, c=c: e.matmul(
                pA[:, c * 512:(c + 1) * 512], ones_r[0:1, :],
                rs0[0:1, sl * 1024 + c * 512:sl * 1024 + (c + 1) * 512],
                start=True, stop=True),
                reads=[f"rs0{sl}", "ones_r"], writes=[f"pA{c}"])
        sc.op("ve", lambda e, h=h: e.tensor_tensor(
            out=avu[:, h * 1024:(h + 1) * 1024],
            in0=avu[:, h * 1024:(h + 1) * 1024], in1=pA[:], op=ALU.mult),
            reads=["pA0", "pA1", "avu"], writes=["avu"])

    phase("p2")
    for nt in range(NKT):
        wsl = nt % 2
        for q4 in range(4):
            ssl = q4 % 2
            sc.dma("sy", wstg[:, ssl * 1024:ssl * 1024 + 1024].rearrange(
                       "p (t c) -> p t c", c=128),
                   wo_d[nt, q4 * 8:(q4 + 1) * 8].rearrange("t p c -> p t c"),
                   writes=[f"wstg{ssl}"])
            sc.op("ve", lambda e, q4=q4, ssl=ssl, wsl=wsl: e.tensor_copy(
                out=wbf[:, wsl * 4096 + q4 * 1024:wsl * 4096 + (q4 + 1) * 1024],
                in_=wstg[:, ssl * 1024:ssl * 1024 + 1024]),
                reads=[f"wstg{ssl}"], writes=[f"wbf{wsl}"])
        pk = "pB" if nt % 2 else "pA"
        ps = pB if nt % 2 else pA
        for dt_ in range(NKT):
            for c in range(2):
                sc.op("pe", lambda e, dt_=dt_, c=c, wsl=wsl, ps=ps: e.matmul(
                    ps[:, c * 512:(c + 1) * 512],
                    wbf[:, wsl * 4096 + dt_ * 128:wsl * 4096 + dt_ * 128 + 128],
                    avu[:, dt_ * 1024 + c * 512:dt_ * 1024 + (c + 1) * 512],
                    start=(dt_ == 0), stop=(dt_ == NKT - 1)),
                    reads=[f"wbf{wsl}", "avu"], writes=[f"{pk}{c}"])
        sc.op("ve", lambda e, nt=nt, ps=ps: e.tensor_tensor(
            out=xP[:, nt * 1024:(nt + 1) * 1024],
            in0=ps[:], in1=xP[:, nt * 1024:(nt + 1) * 1024], op=ALU.add),
            reads=[f"{pk}0", f"{pk}1", "xP"], writes=["xP"])

    phase("p3")
    sc.barrier()
    att.close()
    ln = ExitStack()
    sqb = mksb(ln, "sqb", [128, 1024 * 2], BF16)
    scr = mksb(ln, "scr", [128, 1024 * 2], F32)
    rowm = mksb(ln, "rowm", [1, 1024], F32)
    rowa = mksb(ln, "rowa", [1, 1024], F32)
    rowb = mksb(ln, "rowb", [1, 1024], F32)

    def layernorm(dst_is_out, outst=None):
        for kt in range(NKT):
            sl = kt % 2
            sc.op("ac", lambda e, kt=kt, sl=sl: e.activation(
                out=sqb[:, sl * 1024:(sl + 1) * 1024],
                in_=xP[:, kt * 1024:(kt + 1) * 1024], func=AF.Square),
                reads=["xP"], writes=[f"sqb{sl}"])
            for c in range(2):
                sc.op("pe", lambda e, kt=kt, c=c: e.matmul(
                    pA[0:1, c * 512:(c + 1) * 512], ones_c[:, 0:1],
                    xP[:, kt * 1024 + c * 512:kt * 1024 + (c + 1) * 512],
                    start=(kt == 0), stop=(kt == NKT - 1)),
                    reads=["ones_c", "xP"], writes=[f"pA{c}"])
                sc.op("pe", lambda e, kt=kt, c=c, sl=sl: e.matmul(
                    pB[0:1, c * 512:(c + 1) * 512], ones_c[:, 0:1],
                    sqb[:, sl * 1024 + c * 512:sl * 1024 + (c + 1) * 512],
                    start=(kt == 0), stop=(kt == NKT - 1)),
                    reads=["ones_c", f"sqb{sl}"], writes=[f"pB{c}"])
        sc.op("ve", lambda e: e.tensor_scalar(
            out=rowm[:], in0=pA[0:1, :], scalar1=1.0 / H, scalar2=None, op0=ALU.mult),
            reads=["pA0", "pA1"], writes=["rowm"])
        sc.op("ve", lambda e: e.tensor_tensor(
            out=rowa[:], in0=rowm[:], in1=rowm[:], op=ALU.mult),
            reads=["rowm"], writes=["rowa"])
        sc.op("ve", lambda e: e.scalar_tensor_tensor(
            out=rowb[:], in0=pB[0:1, :], scalar=1.0 / H, in1=rowa[:],
            op0=ALU.mult, op1=ALU.subtract),
            reads=["pB0", "pB1", "rowa"], writes=["rowb"])
        sc.op("ve", lambda e: e.tensor_scalar(
            out=rowb[:], in0=rowb[:], scalar1=EPS, scalar2=None, op0=ALU.add),
            reads=["rowb"], writes=["rowb"])
        sc.op("ac", lambda e: e.activation(out=rowa[:], in_=rowb[:], func=AF.Sqrt),
              reads=["rowb"], writes=["rowa"])
        sc.op("ve", lambda e: e.reciprocal(out=rowb[:], in_=rowa[:]),
              reads=["rowa"], writes=["rowb"])
        sc.op("ve", lambda e: e.tensor_tensor(
            out=rowa[:], in0=rowm[:], in1=rowb[:], op=ALU.mult),
            reads=["rowm", "rowb"], writes=["rowa"])
        for c in range(2):
            sc.op("pe", lambda e, c=c: e.matmul(
                pA[:, c * 512:(c + 1) * 512], ones_rf[0:1, :],
                rowb[0:1, c * 512:(c + 1) * 512], start=True, stop=True),
                reads=["ones_rf", "rowb"], writes=[f"pA{c}"])
            sc.op("pe", lambda e, c=c: e.matmul(
                pB[:, c * 512:(c + 1) * 512], ones_rf[0:1, :],
                rowa[0:1, c * 512:(c + 1) * 512], start=True, stop=True),
                reads=["ones_rf", "rowa"], writes=[f"pB{c}"])
        for kt in range(NKT):
            sl = kt % 2
            sc.op("ve", lambda e, kt=kt, sl=sl: e.tensor_tensor(
                out=scr[:, sl * 1024:(sl + 1) * 1024],
                in0=xP[:, kt * 1024:(kt + 1) * 1024], in1=pA[:], op=ALU.mult),
                reads=["xP", "pA0", "pA1"], writes=[f"scr{sl}"])
            if dst_is_out:
                o = sc.op("ve", lambda e, kt=kt, sl=sl: e.tensor_tensor(
                    out=outst[:, sl * 1024:(sl + 1) * 1024],
                    in0=scr[:, sl * 1024:(sl + 1) * 1024], in1=pB[:], op=ALU.subtract),
                    reads=[f"scr{sl}", "pB0", "pB1"], writes=[f"outst{sl}"])
                sc.dma("sy", out_d[kt * 128:kt * 128 + 128, :],
                       outst[:, sl * 1024:(sl + 1) * 1024],
                       reads=[f"outst{sl}"], writes=[f"outd{kt}"])
            else:
                sc.op("ve", lambda e, kt=kt, sl=sl: e.tensor_tensor(
                    out=xP[:, kt * 1024:(kt + 1) * 1024],
                    in0=scr[:, sl * 1024:(sl + 1) * 1024], in1=pB[:], op=ALU.subtract),
                    reads=[f"scr{sl}", "pB0", "pB1"], writes=["xP"])

    layernorm(False)
    phase("ln1")

    ffn = ExitStack()
    mtg = mksb(ffn, "mtg", [32, NKT * 128], BF16)
    mtu = mksb(ffn, "mtu", [32, NKT * 128], BF16)
    mgm = mksb(ffn, "mgm", [128, NKT * 32], BF16)
    mum = mksb(ffn, "mum", [128, NKT * 32], BF16)
    qzg = mksb(ffn, "qzg", [32, 1376], I32)
    qzu = mksb(ffn, "qzu", [32, 1376], I32)
    qzd = mksb(ffn, "qzd", [86, 512], I32)
    gio32 = mksb(ffn, "gio32", [32, 1], F32)
    gio86 = mksb(ffn, "gio86", [86, 1], F32)
    gior32 = mksb(ffn, "gior32", [128, 32], F32)
    gior86 = mksb(ffn, "gior86", [128, 86], F32)
    gcoli = mksb(ffn, "gcoli", [128, 32], I32)
    gcolf = mksb(ffn, "gcolf", [128, 32], F32)
    xgg = mksb(ffn, "xgg", [32, 1024], BF16)
    xgu = mksb(ffn, "xgu", [32, 1024], BF16)
    hga = mksb(ffn, "hga", [86, 1024], F32)
    hgn = mksb(ffn, "hgn", [86, 1024], BF16)
    sdbf = mksb(ffn, "sdbf", [86, H], BF16)
    zsd = mksb(ffn, "zsd", [86, H], BF16)
    sdf = mksb(ffn, "sdf", [86, 512], F32)
    hsb = mksb(ffn, "hsb", [128, 11 * 1024], BF16)
    qstg = mksb(ffn, "qstg", [128, 256 * 3], I32)
    t1b = mksb(ffn, "t1b", [128, 256 * 2], I32)
    w1s = mksb(ffn, "w1s", [128, 256 * 2], BF16)
    sgf = mksb(ffn, "sgf", [32, 256 * 2], F32)
    sgb = mksb(ffn, "sgb", [32, 256 * 3], BF16)
    zib = mksb(ffn, "zib", [32, 256], I32)
    zsb = mksb(ffn, "zsb", [32, 256 * 3], BF16)
    ysin = mksb(ffn, "ysin", [128, 512 * 6], F32)
    ysout = mksb(ffn, "ysout", [128, 512], F32)
    mtdp = mksb(ffn, "mtdp", [86, 11 * 128], BF16)
    mdp = mksb(ffn, "mdp", [128, 11 * 86], BF16)
    gdbi = mksb(ffn, "gdbi", [1, 128 * 2], I32)
    gdbb = mksb(ffn, "gdbb", [1, 128 * 2], BF16)
    grwi = mksb(ffn, "grwi", [1, 512], I32)
    grwb = mksb(ffn, "grwb", [1, 512], BF16)

    sc.dma("sy", qzg[:], zg_d, writes=["qzg"])
    sc.dma("sy", qzu[:], zu_d, writes=["qzu"])
    sc.dma("sy", qzd[:], zd_d, writes=["qzd"])
    sc.op("gp", lambda e: e.iota(gio32[:], [[1, 1]], base=0, channel_multiplier=1,
                                 allow_small_or_imprecise_dtypes=True), writes=["gio32"])
    sc.op("gp", lambda e: e.iota(gio86[:], [[1, 1]], base=0, channel_multiplier=1,
                                 allow_small_or_imprecise_dtypes=True), writes=["gio86"])
    sc.op("gp", lambda e: e.iota(gior32[:], [[1, 32]], base=0, channel_multiplier=0,
                                 allow_small_or_imprecise_dtypes=True), writes=["gior32"])
    sc.op("gp", lambda e: e.iota(gior86[:], [[1, 86]], base=0, channel_multiplier=0,
                                 allow_small_or_imprecise_dtypes=True), writes=["gior86"])

    for name, g_dram, mt, mm in [("g", gg_d, mtg, mgm), ("u", gu_d, mtu, mum)]:
        for ch in range(8):
            sc.dma("sy", grwi[0:1, 0:512],
                   g_dram.rearrange("(o x) -> o x", o=1)[0:1, ch * 512:(ch + 1) * 512],
                   writes=["grwi"])
            sc.op("ve", lambda e: e.tensor_copy(
                out=grwb[0:1, 0:512],
                in_=grwi[0:1, 0:512]),
                reads=["grwi"], writes=["grwb"])
            sc.op("pe", lambda e: e.matmul(
                pD[0:32, 0:512], ones_r[0:1, 0:32],
                grwb[0:1, 0:512], start=True, stop=True),
                reads=["grwb", "ones_r"], writes=["pD"])
            for s4 in range(4):
                kt = ch * 4 + s4
                sc.op("ve", lambda e, kt=kt, s4=s4, mt=mt: e.tensor_scalar(
                    out=mt[:, kt * 128:(kt + 1) * 128],
                    in0=pD[0:32, s4 * 128:(s4 + 1) * 128],
                    scalar1=gio32[:, 0:1], scalar2=None, op0=ALU.is_equal),
                    reads=["pD", "gio32"], writes=[f"mt{name}"])
        for kt in range(NKT):
            sc.dma("sy", gcoli[:, kt:kt + 1],
                   g_dram.rearrange("(a b) -> a b", b=1)[kt * 128:(kt + 1) * 128, :],
                   writes=[f"gcoli{kt}"])
        gcl = sc.op("ve", lambda e: e.tensor_copy(out=gcolf[:], in_=gcoli[:]),
                    reads=[f"gcoli{kt}" for kt in range(NKT)], writes=["gcolf"])
        for kt in range(NKT):
            sc.op("ve", lambda e, kt=kt, mm=mm: e.tensor_scalar(
                out=mm[:, kt * 32:(kt + 1) * 32], in0=gior32[:],
                scalar1=gcolf[:, kt:kt + 1], scalar2=None, op0=ALU.is_equal),
                reads=["gior32", "gcolf"], writes=[f"m{name}m"])

    for ch in range(8):
        sc.dma("sy", sdf[0:86, 0:512], sd_d[:, ch * 512:(ch + 1) * 512],
               writes=["sdf"])
        sc.op("ve", lambda e, ch=ch: e.tensor_copy(
            out=sdbf[:, ch * 512:(ch + 1) * 512], in_=sdf[0:86, 0:512]),
            reads=["sdf"], writes=["sdbf"])
        sc.op("ve", lambda e, ch=ch: e.tensor_scalar(
            out=t1b[0:86, 0:512], in0=qzd[:],
            scalar1=4 * ch, scalar2=15,
            op0=ALU.logical_shift_right, op1=ALU.bitwise_and),
            reads=["qzd"], writes=["t1b0", "t1b1"])
        sc.op("ve", lambda e, ch=ch: e.scalar_tensor_tensor(
            out=zsd[:, ch * 512:(ch + 1) * 512], in0=t1b[0:86, 0:512], scalar=1.0,
            in1=sdf[0:86, 0:512], op0=ALU.add, op1=ALU.mult),
            reads=["t1b0", "t1b1", "sdf"], writes=["zsd"])

    for name, mm, xg in [("g", mgm, xgg), ("u", mum, xgu)]:
        for c in range(2):
            for kt in range(NKT):
                sc.op("pe", lambda e, kt=kt, c=c, mm=mm: e.matmul(
                    pD[0:32, 0:512], mm[:, kt * 32:(kt + 1) * 32],
                    xP[:, kt * 1024 + c * 512:kt * 1024 + (c + 1) * 512],
                    start=(kt == 0), stop=(kt == NKT - 1)),
                    reads=[f"m{name}m", "xP"], writes=["pD"])
            sc.op("ve", lambda e, c=c, xg=xg: e.tensor_scalar(
                out=xg[:, c * 512:(c + 1) * 512], in0=pD[0:32, 0:512],
                scalar1=-1.0, scalar2=None, op0=ALU.mult),
                reads=["pD"], writes=[f"xg{name}"])

    phase("xg")
    ACC = [("pA0", pA, 0), ("pA1", pA, 512), ("pB0", pB, 0), ("pB1", pB, 512)]

    for part in range(NPART):
        d = part
        for b in range(11):
            rows = SLAB_ROWS[b]
            gsl = b % 2
            sc.dma("sy", gdbi[0:1, gsl * 128:gsl * 128 + rows],
                   gd_d.rearrange("(o x) -> o x", o=1)[0:1, d * 1376 + b * 128:d * 1376 + b * 128 + rows],
                   writes=[f"gdbi{gsl}"])
            sc.op("ve", lambda e, gsl=gsl, rows=rows: e.tensor_copy(
                out=gdbb[0:1, gsl * 128:gsl * 128 + rows],
                in_=gdbi[0:1, gsl * 128:gsl * 128 + rows]),
                reads=[f"gdbi{gsl}"], writes=[f"gdbb{gsl}"])
            sc.op("pe", lambda e, b=b, rows=rows, gsl=gsl: e.matmul(
                pD[0:86, 0:rows], ones_r[0:1, 0:86],
                gdbb[0:1, gsl * 128:gsl * 128 + rows], start=True, stop=True),
                reads=[f"gdbb{gsl}", "ones_r"], writes=["pD"])
            sc.op("ve", lambda e, b=b, rows=rows: e.tensor_scalar(
                out=mtdp[:, b * 128:b * 128 + rows], in0=pD[0:86, 0:rows],
                scalar1=gio86[:, 0:1], scalar2=None, op0=ALU.is_equal),
                reads=["pD", "gio86"], writes=["mtdp"])
            sc.dma("sy", gcoli[0:rows, b:b + 1],
                   gd_d.rearrange("(a b) -> a b", b=1)[d * 1376 + b * 128:d * 1376 + b * 128 + rows, :],
                   writes=[f"gcoli{b}"])
        sc.op("ve", lambda e: e.tensor_copy(out=gcolf[:, 0:11], in_=gcoli[:, 0:11]),
              reads=[f"gcoli{b}" for b in range(11)], writes=["gcolf"])
        for b in range(11):
            rows = SLAB_ROWS[b]
            sc.op("ve", lambda e, b=b, rows=rows: e.tensor_scalar(
                out=mdp[0:rows, b * 86:(b + 1) * 86], in0=gior86[0:rows, :],
                scalar1=gcolf[0:rows, b:b + 1], scalar2=None, op0=ALU.is_equal),
                reads=["gior86", "gcolf"], writes=["mdp"])

        MATP = [(qg_d, sg_d, qzg, mtg, xgg, "xgg", "qzg", "mtg"),
                (qu_d, su_d, qzu, mtu, xgu, "xgu", "qzu", "mtu")]
        kts = [(j * 4 + kpb, kpb, j) for kpb in range(4) for j in range(8)]

        def gu_prep(blk):
            if blk >= 12:
                return
            gi, mat = blk // 2, blk % 2
            b0, nsl, width = GRPS[gi]
            gcol = d * 1376 + b0 * 128
            q_dram, s_dram, qz, mt, xg, xgk, qzk, mtk = MATP[mat]
            bsl = (part * 12 + blk) % 3
            bslf = (part * 12 + blk) % 2
            sc.dma("sy", sgf[0:32, bslf * 256:bslf * 256 + width],
                   s_dram[:, gcol:gcol + width], writes=[f"sgf{bslf}"])
            sc.op("ve", lambda e, bsl=bsl, bslf=bslf, width=width: e.tensor_copy(
                out=sgb[0:32, bsl * 256:bsl * 256 + width],
                in_=sgf[0:32, bslf * 256:bslf * 256 + width]),
                reads=[f"sgf{bslf}"], writes=[f"sgb{bsl}"])
            sc.op("ve", lambda e, qz=qz, d=d, b0=b0, width=width: e.tensor_scalar(
                out=zib[0:32, 0:width],
                in0=qz[:, b0 * 128:b0 * 128 + width], scalar1=4 * d, scalar2=15,
                op0=ALU.logical_shift_right, op1=ALU.bitwise_and),
                reads=[qzk], writes=["zib"])
            sc.op("ve", lambda e, bsl=bsl, bslf=bslf, width=width: e.scalar_tensor_tensor(
                out=zsb[0:32, bsl * 256:bsl * 256 + width],
                in0=zib[0:32, 0:width], scalar=1.0,
                in1=sgf[0:32, bslf * 256:bslf * 256 + width],
                op0=ALU.add, op1=ALU.mult),
                reads=["zib", f"sgf{bslf}"], writes=[f"zsb{bsl}"])

        def gq_load(L):
            if L >= 48:
                return
            lblk, kpb = divmod(L, 4)
            lgi, lmat = lblk // 2, lblk % 2
            lb0, _, lwidth = GRPS[lgi]
            lgcol = d * 1376 + lb0 * 128
            lq_dram = MATP[lmat][0]
            qsl = L % 3
            sc.dma("sy", qstg[:, qsl * 256:qsl * 256 + lwidth],
                   lq_dram[kpb * 128:kpb * 128 + 128, lgcol:lgcol + lwidth],
                   writes=[f"qstg{qsl}"])

        gu_prep(0)
        gq_load(0)
        gq_load(1)
        for blk in range(12):
            gi, mat = blk // 2, blk % 2
            b0, nsl, width = GRPS[gi]
            gcol = d * 1376 + b0 * 128
            q_dram, s_dram, qz, mt, xg, xgk, qzk, mtk = MATP[mat]
            bsl = (part * 12 + blk) % 3
            if True:
                gu_prep(blk + 1)

                def sexp_mm(i, mt=mt, bsl=bsl, width=width, mtk=mtk):
                    kt = kts[i][0]
                    sesl = i % 2
                    sc.op("pe", lambda e, kt=kt, mt=mt, bsl=bsl, sesl=sesl, width=width: e.matmul(
                        PC[sesl][:, 0:width], mt[:, kt * 128:(kt + 1) * 128],
                        sgb[0:32, bsl * 256:bsl * 256 + width],
                        start=True, stop=True),
                        reads=[mtk, f"sgb{bsl}"], writes=[f"pC{sesl + 1}"])

                sexp_mm(0)
                for i, (kt, kpb, j) in enumerate(kts):
                    qsl = (blk * 4 + kpb) % 3
                    if j == 0:
                        gq_load(blk * 4 + kpb + 2)
                    tsl = i % 2
                    sesl = i % 2
                    sc.op("ve", lambda e, j=j, qsl=qsl, tsl=tsl, width=width: e.tensor_scalar(
                        out=t1b[:, tsl * 256:tsl * 256 + width],
                        in0=qstg[:, qsl * 256:qsl * 256 + width],
                        scalar1=4 * j, scalar2=15,
                        op0=ALU.logical_shift_right, op1=ALU.bitwise_and),
                        reads=[f"qstg{qsl}"], writes=[f"t1b{tsl}"])
                    sc.op("ve", lambda e, tsl=tsl, sesl=sesl, width=width: e.scalar_tensor_tensor(
                        out=w1s[:, tsl * 256:tsl * 256 + width],
                        in0=t1b[:, tsl * 256:tsl * 256 + width], scalar=1.0,
                        in1=PC[sesl][:, 0:width], op0=ALU.mult, op1=ALU.mult),
                        reads=[f"t1b{tsl}", f"pC{sesl + 1}"], writes=[f"w1s{tsl}"])
                    if i + 1 < len(kts):
                        sexp_mm(i + 1)
                    for c in range(2):
                        for s in range(nsl):
                            ak, aps, aoff = ACC[c * 2 + s]
                            rows = SLAB_ROWS[b0 + s]
                            sc.op("pe", lambda e, kt=kt, c=c, s=s, tsl=tsl, rows=rows,
                                  aps=aps, aoff=aoff: e.matmul(
                                aps[0:rows, aoff:aoff + 512],
                                w1s[:, tsl * 256 + s * 128:tsl * 256 + s * 128 + rows],
                                xP[:, kt * 1024 + c * 512:kt * 1024 + (c + 1) * 512],
                                start=(kt == 0), stop=False),
                                reads=[f"w1s{tsl}", "xP"], writes=[ak])
                for c in range(2):
                    for s in range(nsl):
                        ak, aps, aoff = ACC[c * 2 + s]
                        rows = SLAB_ROWS[b0 + s]
                        sc.op("pe", lambda e, c=c, s=s, rows=rows, bsl=bsl, aps=aps,
                              aoff=aoff, xg=xg: e.matmul(
                            aps[0:rows, aoff:aoff + 512],
                            zsb[0:32, bsl * 256 + s * 128:bsl * 256 + s * 128 + rows],
                            xg[:, c * 512:(c + 1) * 512],
                            start=False, stop=True),
                            reads=[f"zsb{bsl}", xgk], writes=[ak])
                        hoff = (b0 + s) * 1024 + c * 512
                        if mat == 0:
                            sc.op("ac", lambda e, rows=rows, hoff=hoff, aps=aps, aoff=aoff: e.activation(
                                out=hsb[0:rows, hoff:hoff + 512],
                                in_=aps[0:rows, aoff:aoff + 512], func=AF.Silu),
                                reads=[ak], writes=["hsb"])
                            if part == 0 and gi == 0 and c == 1 and s == nsl - 1:
                                phase("dbg_g0")
                        else:
                            sc.op("ve", lambda e, rows=rows, hoff=hoff, aps=aps, aoff=aoff: e.tensor_tensor(
                                out=hsb[0:rows, hoff:hoff + 512],
                                in0=hsb[0:rows, hoff:hoff + 512],
                                in1=aps[0:rows, aoff:aoff + 512], op=ALU.mult),
                                reads=[ak, "hsb"], writes=["hsb"])

        for c in range(2):
            for b in range(11):
                rows = SLAB_ROWS[b]
                sc.op("pe", lambda e, b=b, c=c, rows=rows: e.matmul(
                    pD[0:86, 0:512], mdp[0:rows, b * 86:(b + 1) * 86],
                    hsb[0:rows, b * 1024 + c * 512:b * 1024 + (c + 1) * 512],
                    start=(b == 0), stop=(b == 10)),
                    reads=["mdp", "hsb"], writes=["pD"])
            if part == 0:
                sc.op("ve", lambda e, c=c: e.tensor_copy(
                    out=hga[:, c * 512:(c + 1) * 512], in_=pD[0:86, 0:512]),
                    reads=["pD"], writes=["hga"])
            else:
                sc.op("ve", lambda e, c=c: e.tensor_tensor(
                    out=hga[:, c * 512:(c + 1) * 512],
                    in0=hga[:, c * 512:(c + 1) * 512], in1=pD[0:86, 0:512], op=ALU.add),
                    reads=["pD", "hga"], writes=["hga"])
        if part == NPART - 1:
            sc.op("ve", lambda e: e.tensor_scalar(
                out=hgn[:], in0=hga[:], scalar1=-1.0, scalar2=None, op0=ALU.mult),
                reads=["hga"], writes=["hgn"])

        def ys_prefetch(nb_):
            if part == 0 or nb_ >= 16:
                return
            for nt2 in range(2):
                for c in range(2):
                    ntg = nb_ * 2 + nt2
                    usl = (nb_ * 4 + nt2 * 2 + c) % 6
                    sc.dma("gp", ysin[:, usl * 512:usl * 512 + 512],
                           ysp_d[ntg * 128:ntg * 128 + 128, c * 512:(c + 1) * 512],
                           reads=[f"ysp{ntg}_{c}"], writes=[f"ysin{usl}"])

        def dq_load(Di):
            if Di >= 176:
                return
            dnb, db = divmod(Di, 11)
            drows = SLAB_ROWS[db]
            dqsl = Di % 3
            sc.dma("sy", qstg[0:drows, dqsl * 256:dqsl * 256 + 256],
                   qd_d[db * 128:db * 128 + drows, dnb * 256:dnb * 256 + 256],
                   writes=[f"qstg{dqsl}"])

        pending_drain = []
        dq_load(0)
        dq_load(1)
        for nb in range(16):
            ncol = nb * 256
            def dsexp_mm(b, ncol=ncol):
                rows = SLAB_ROWS[b]
                sesl = b % 2
                sc.op("pe", lambda e, b=b, rows=rows, sesl=sesl, ncol=ncol: e.matmul(
                    PC[sesl][0:rows, 0:256], mtdp[:, b * 128:b * 128 + rows],
                    sdbf[:, ncol:ncol + 256], start=True, stop=True),
                    reads=["mtdp", "sdbf"], writes=[f"pC{sesl + 1}"])

            dsexp_mm(0)
            for b in range(11):
                rows = SLAB_ROWS[b]
                Di = nb * 11 + b
                dq_load(Di + 2)
                if b == 0:
                    while pending_drain:
                        pending_drain.pop(0)()
                if b == 4:
                    ys_prefetch(nb)
                qsl = Di % 3
                tsl = b % 2
                sc.op("ve", lambda e, d=d, rows=rows, qsl=qsl, tsl=tsl: e.tensor_scalar(
                    out=t1b[0:rows, tsl * 256:tsl * 256 + 256],
                    in0=qstg[0:rows, qsl * 256:qsl * 256 + 256],
                    scalar1=4 * d, scalar2=15,
                    op0=ALU.logical_shift_right, op1=ALU.bitwise_and),
                    reads=[f"qstg{qsl}"], writes=[f"t1b{tsl}"])
                sesl = b % 2
                sc.op("ve", lambda e, rows=rows, tsl=tsl, sesl=sesl: e.scalar_tensor_tensor(
                    out=w1s[0:rows, tsl * 256:tsl * 256 + 256],
                    in0=t1b[0:rows, tsl * 256:tsl * 256 + 256], scalar=1.0,
                    in1=PC[sesl][0:rows, 0:256], op0=ALU.mult, op1=ALU.mult),
                    reads=[f"t1b{tsl}", f"pC{sesl + 1}"], writes=[f"w1s{tsl}"])
                if b + 1 < 11:
                    dsexp_mm(b + 1)
                for nt2 in range(2):
                    for c in range(2):
                        ak, aps, aoff = ACC[nt2 * 2 + c]
                        sc.op("pe", lambda e, b=b, rows=rows, nt2=nt2, c=c, tsl=tsl,
                              aps=aps, aoff=aoff: e.matmul(
                            aps[:, aoff:aoff + 512],
                            w1s[0:rows, tsl * 256 + nt2 * 128:tsl * 256 + nt2 * 128 + 128],
                            hsb[0:rows, b * 1024 + c * 512:b * 1024 + (c + 1) * 512],
                            start=(b == 0), stop=(b == 10 and part != NPART - 1)),
                            reads=[f"w1s{tsl}", "hsb"], writes=[ak])
            for nt2 in range(2):
                for c in range(2):
                    ak, aps, aoff = ACC[nt2 * 2 + c]
                    ntg = nb * 2 + nt2
                    if part == NPART - 1:
                        sc.op("pe", lambda e, ntg=ntg, c=c, aps=aps, aoff=aoff: e.matmul(
                            aps[:, aoff:aoff + 512],
                            zsd[:, ntg * 128:ntg * 128 + 128],
                            hgn[:, c * 512:(c + 1) * 512], start=False, stop=True),
                            reads=["zsd", "hgn"], writes=[ak])
                    usl = (nb * 4 + nt2 * 2 + c) % 6
                    if part == 0:
                        def drain(ak=ak, aps=aps, aoff=aoff, ntg=ntg, c=c):
                            sc.op("ve", lambda e: e.tensor_copy(
                                out=ysout[:, 0:512],
                                in_=aps[:, aoff:aoff + 512]),
                                reads=[ak], writes=["ysout"])
                            sc.dma("ac", ysp_d[ntg * 128:ntg * 128 + 128, c * 512:(c + 1) * 512],
                                   ysout[:, 0:512],
                                   reads=["ysout"], writes=[f"ysp{ntg}_{c}"])
                    elif part < NPART - 1:
                        def drain(usl=usl, ak=ak, aps=aps, aoff=aoff, ntg=ntg, c=c):
                            sc.op("ve", lambda e: e.tensor_tensor(
                                out=ysout[:, 0:512],
                                in0=ysin[:, usl * 512:usl * 512 + 512],
                                in1=aps[:, aoff:aoff + 512], op=ALU.add),
                                reads=[ak, f"ysin{usl}"], writes=["ysout"])
                            sc.dma("ac", ysp_d[ntg * 128:ntg * 128 + 128, c * 512:(c + 1) * 512],
                                   ysout[:, 0:512],
                                   reads=["ysout"], writes=[f"ysp{ntg}_{c}"])
                    else:
                        def drain(usl=usl, ak=ak, aps=aps, aoff=aoff, ntg=ntg, c=c):
                            ssl = c % 2
                            sc.op("ve", lambda e, ssl=ssl: e.tensor_tensor(
                                out=scr[:, ssl * 1024:ssl * 1024 + 512],
                                in0=ysin[:, usl * 512:usl * 512 + 512],
                                in1=aps[:, aoff:aoff + 512], op=ALU.add),
                                reads=[ak, f"ysin{usl}"], writes=[f"scr{ssl}"])
                            sc.op("ve", lambda e, ntg=ntg, c=c, ssl=ssl: e.tensor_tensor(
                                out=xP[:, ntg * 1024 + c * 512:ntg * 1024 + (c + 1) * 512],
                                in0=scr[:, ssl * 1024:ssl * 1024 + 512],
                                in1=xP[:, ntg * 1024 + c * 512:ntg * 1024 + (c + 1) * 512],
                                op=ALU.add),
                                reads=[f"scr{ssl}", "xP"], writes=["xP"])
                    pending_drain.append(drain)
        while pending_drain:
            pending_drain.pop(0)()

        phase(f"part{part}")
    sc.barrier()
    ffn.close()

    p6 = ExitStack()
    outst = mksb(p6, "outst", [128, 1024 * 2], F32)
    layernorm(True, outst)
    p6.close()
    ln.close()

    sems = {}
    es_sem = ExitStack()
    for e in ENGS:
        sems[e] = es_sem.enter_context(nc.semaphore("s_" + e))
        sems["bulk_" + e] = es_sem.enter_context(nc.semaphore("b_" + e))

    def mksem(name):
        return es_sem.enter_context(nc.semaphore(name[:24]))

    with nc.Block() as block:
        sc.emit(nc, block, sems, mksem, prefix=_STOP_AFTER)
    es_sem.close()
    es_top.close()
    g.close()
    return nc, taps



_PROG = None


def _get_prog():
    global _PROG
    if _PROG is None:
        _PROG = build_program()
    return _PROG


def kernel(L0_attn_hidden_states, L0_attn_Wq, L0_attn_Wkv, L0_attn_Wo,
           L0_ffn_gate_qweight, L0_ffn_gate_scales, L0_ffn_gate_qzeros, L0_ffn_gate_g_idx,
           L0_ffn_up_qweight, L0_ffn_up_scales, L0_ffn_up_qzeros, L0_ffn_up_g_idx,
           L0_ffn_down_qweight, L0_ffn_down_scales, L0_ffn_down_qzeros, L0_ffn_down_g_idx):
    x = np.asarray(L0_attn_hidden_states, dtype=np.float32)
    Wq = np.asarray(L0_attn_Wq, dtype=np.float32)
    Wkv = np.asarray(L0_attn_Wkv, dtype=np.float32)
    Wo = np.asarray(L0_attn_Wo, dtype=np.float32)

    slopes = _alibi_slopes(NH)
    permH, permI = PERM_H, PERM_I

    Wq_p = Wq[permH, :]
    wq = np.ascontiguousarray(
        Wq_p.reshape(NKT, 128, NKT, 128).transpose(2, 0, 1, 3))
    Wkv_p = Wkv[permH, :]
    wk = np.ascontiguousarray(Wkv_p[:, :128].reshape(NKT, 128, 128))
    wv = np.ascontiguousarray(Wkv_p[:, 128:].reshape(NKT, 128, 128))
    wo = np.ascontiguousarray(
        Wo[:, permH].reshape(NKT, 128, NKT, 128).transpose(2, 0, 1, 3))
    qg = np.ascontiguousarray(np.asarray(L0_ffn_gate_qweight, dtype=np.int32)[:, permI])
    qu = np.ascontiguousarray(np.asarray(L0_ffn_up_qweight, dtype=np.int32)[:, permI])
    qd = np.ascontiguousarray(np.asarray(L0_ffn_down_qweight, dtype=np.int32)[:, permH])
    sg = np.ascontiguousarray(np.asarray(L0_ffn_gate_scales, dtype=np.float32)[:, permI])
    su = np.ascontiguousarray(np.asarray(L0_ffn_up_scales, dtype=np.float32)[:, permI])
    sd = np.ascontiguousarray(np.asarray(L0_ffn_down_scales, dtype=np.float32)[:, permH])
    zg = np.ascontiguousarray(np.asarray(L0_ffn_gate_qzeros, dtype=np.int32))
    zu = np.ascontiguousarray(np.asarray(L0_ffn_up_qzeros, dtype=np.int32))
    zd = np.ascontiguousarray(np.asarray(L0_ffn_down_qzeros, dtype=np.int32))
    gg = np.ascontiguousarray(np.asarray(L0_ffn_gate_g_idx, dtype=np.int32)[permH])
    gu = np.ascontiguousarray(np.asarray(L0_ffn_up_g_idx, dtype=np.int32)[permH])
    gd = np.ascontiguousarray(np.asarray(L0_ffn_down_g_idx, dtype=np.int32)[permI])

    idn = np.eye(128, dtype=np.float32).astype(ml_dtypes.bfloat16)
    onesr = np.ones((1, 128), dtype=ml_dtypes.bfloat16)
    onesc = np.ones((128, 1), dtype=ml_dtypes.bfloat16)
    onesrf = np.ones((1, 128), dtype=np.float32)
    abias = np.empty((128, NH * NST), dtype=np.float32)
    p = np.arange(128, dtype=np.float32)
    for h in range(NH):
        for j in range(NST):
            abias[:, h * 16 + j] = slopes[h] * (128 * j + p)

    in_maps = []
    core_cols = []
    for core in range(8):
        b = core // 2
        strips = STRIPS_A if core % 2 == 0 else STRIPS_B
        cols = np.concatenate([np.arange(64 * s, 64 * s + 64) for s in strips])
        core_cols.append((b, cols))
        xt = np.ascontiguousarray(x[b].T[permH, :])
        xp = np.ascontiguousarray(xt[:, cols])
        amask = np.zeros((128, NST * 64), dtype=np.float32)
        for j in range(NST):
            dstrip = [s for s in strips if s in (2 * j, 2 * j + 1)][0]
            kk = 128 * j + p[:, None]
            qq = 64 * dstrip + np.arange(64)[None, :]
            amask[:, j * 64:(j + 1) * 64] = np.where(kk <= qq, 0.0, NEG)
        aqsl = (-slopes[:, None] * cols[None, :].astype(np.float32)).astype(ml_dtypes.bfloat16)
        in_maps.append(dict(
            xp=xp, xt=xt, wq=wq, wk=wk, wv=wv, wo=wo, qg=qg, qu=qu, qd=qd,
            sg=sg, su=su, sd=sd, zg=zg, zu=zu, zd=zd, gg=gg, gu=gu, gd=gd,
            amask=amask.astype(ml_dtypes.bfloat16), abias=abias, aqsl=aqsl,
            idn=idn, onesr=onesr, onesc=onesc, onesrf=onesrf))

    nc, taps = _get_prog()
    res = run_bass_kernel_spmd(nc, in_maps, core_ids=list(range(8)))

    inv = np.empty(H, dtype=np.int64)
    inv[permH] = np.arange(H)
    full = np.empty((B, S, H), dtype=np.float32)
    for core in range(8):
        b, cols = core_cols[core]
        y = np.asarray(res.results[core]["out"])
        full[b, cols, :] = y[inv, :].T
    if _DEBUG_TAPS:
        kernel._last_results = res
    return full


if __name__ == "__main__":
    nc, taps = build_program()
    print("build ok")


# version 08 (final)
# speedup vs baseline: 1.8089x; 1.0334x over previous
import math
import numpy as np
import ml_dtypes
import concourse.bass as bass
import concourse.mybir as mybir
from concourse.bass_utils import run_bass_kernel_spmd
from contextlib import ExitStack

F32 = mybir.dt.float32
BF16 = mybir.dt.bfloat16
I32 = mybir.dt.int32
AF = mybir.ActivationFunctionType
ALU = mybir.AluOpType

B, S, H = 4, 2048, 4096
NH, HD = 32, 128
INTER = 11008
NTOK = 1024
NKT = 32
NST = 16
SCALE = HD ** -0.5
EPS = 1e-5
NEG = -30000.0
NPART = 8
GRPS = [(0, 2, 256), (2, 2, 256), (4, 2, 256), (6, 2, 256), (8, 2, 256), (10, 1, 96)]
SLAB_ROWS = [128] * 10 + [96]

_DEBUG_TAPS = []
_STOP_AFTER = None


class _StopBuild(Exception):
    pass


def _alibi_slopes(n):
    cp = 2 ** math.floor(math.log2(n))
    base = 2.0 ** (-(2.0 ** (-(math.log2(cp) - 3))))
    sl = [base ** i for i in range(1, cp + 1)]
    if cp != n:
        eb = 2.0 ** (-(2.0 ** (-(math.log2(2 * cp) - 3))))
        ne = min(cp, n - cp)
        sl += [eb ** i for i in range(1, 2 * ne + 1, 2)]
    return np.asarray(sl, dtype=np.float32)


def _perm(n_rows):
    return (8 * np.arange(n_rows)[None, :] + np.arange(8)[:, None]).reshape(-1)


PERM_H = _perm(512)
PERM_I = _perm(1376)
STRIPS_A = [s for s in range(32) if s % 4 in (0, 3)]
STRIPS_B = [s for s in range(32) if s % 4 in (1, 2)]


def att_chunks(j):
    qs = 64 * j
    if qs < 512:
        return [(qs, 512 - qs), (512, 512)]
    return [(qs, 1024 - qs)]



ENGS = ("sy", "gp", "pe", "ve", "ac")


class Op:
    __slots__ = ("eng", "fn", "waits", "is_dma", "marked", "dmakey", "seq", "isbar")

    def __init__(self, eng, fn, is_dma):
        self.eng, self.fn, self.is_dma = eng, fn, is_dma
        self.waits, self.marked, self.dmakey = [], False, None
        self.seq, self.isbar = 0, False


class Sched:
    def __init__(self):
        self.ops = {e: [] for e in ENGS}
        self.writers = {}
        self.readers = {}
        self.seqc = 0

    def barrier(self):
        for e in ENGS:
            o = self.op(e, lambda eng: eng.nop())
            o.isbar = True

    def _edge(self, a, b):
        if a is b or (a.eng == b.eng and not a.is_dma):
            return
        a.marked = True
        b.waits.append(a)

    def op(self, eng, fn, reads=(), writes=(), after=(), is_dma=False):
        o = Op(eng, fn, is_dma)
        if is_dma:
            o.marked = True
            o.dmakey = writes[0] if writes else None
        for a in after:
            if a is not None:
                self._edge(a, o)
        for bk in reads:
            for w in self.writers.get(bk, {}).values():
                self._edge(w, o)
        for bk in writes:
            for w in self.writers.get(bk, {}).values():
                self._edge(w, o)
            for r in self.readers.get(bk, {}).values():
                self._edge(r, o)
        for bk in reads:
            self.readers.setdefault(bk, {})[o.eng] = o
        for bk in writes:
            self.writers[bk] = {o.eng: o}
            self.readers[bk] = {}
        self.seqc += 1
        o.seq = self.seqc
        self.ops[eng].append(o)
        return o

    def dma(self, q, out_ap, in_ap, reads=(), writes=(), after=(), accum=False):
        if accum:
            fn = lambda e, o=out_ap, i=in_ap: e.dma_start(out=o, in_=i, accum_op=ALU.add)
        else:
            fn = lambda e, o=out_ap, i=in_ap: e.dma_start(out=o, in_=i)
        return self.op(q, fn, reads=reads, writes=writes, after=after, is_dma=True)

    def snap(self, name):
        if not hasattr(self, "snaps"):
            self.snaps = {}
        self.snaps[name] = {e: len(self.ops[e]) for e in ENGS}

    def emit(self, nc, block, sems, mksem, prefix=None):
        lens = self.snaps[prefix] if prefix else {e: len(self.ops[e]) for e in ENGS}
        def dsem(o):
            k = o.dmakey
            if k is None or k.startswith(("ysp", "outd", "tap_")):
                return "bulk_" + o.eng
            return "d_" + k

        val = {}
        semname = {}
        tot = {}
        cnt = {}
        events = []
        for e in ENGS:
            for o in self.ops[e][:lens[e]]:
                if o.isbar:
                    o.marked = True
                if not o.marked:
                    continue
                sn = dsem(o) if o.is_dma else e
                semname[id(o)] = sn
                cnt[sn] = cnt.get(sn, 0) + (16 if o.is_dma else 1)
                val[id(o)] = cnt[sn]
                tot.setdefault(e, {})[sn] = cnt[sn]
                events.append((o.seq, sn, cnt[sn]))
        for sn in cnt:
            if sn not in sems:
                sems[sn] = mksem(sn)
        events.sort()
        eseqs = [ev[0] for ev in events]
        import bisect

        def snapshot(seq):
            snap = {}
            for i in range(bisect.bisect_left(eseqs, seq)):
                _, sn, v = events[i]
                snap[sn] = max(snap.get(sn, 0), v)
            return snap

        def mk(ename):
            def body(eng):
                cache = {}
                for o in self.ops[ename][:lens[ename]]:
                    if o.isbar:
                        for sn, v in snapshot(o.seq).items():
                            if cache.get(sn, 0) < v:
                                eng.wait_ge(sems[sn], v)
                                cache[sn] = v
                    for a in o.waits:
                        v = val.get(id(a))
                        if v is None:
                            continue
                        sn = semname[id(a)]
                        if cache.get(sn, 0) < v:
                            eng.wait_ge(sems[sn], v)
                            cache[sn] = v
                    instr = o.fn(eng)
                    if o.marked:
                        instr.then_inc(sems[semname[id(o)]], 16 if o.is_dma else 1)
                if ename in ("sy", "gp"):
                    for sn, v in tot.get(ename, {}).items():
                        if cache.get(sn, 0) < v:
                            eng.wait_ge(sems[sn], v)
                            cache[sn] = v
            return body

        block.sync(mk("sy"))
        block.gpsimd(mk("gp"))
        block.tensor(mk("pe"))
        block.vector(mk("ve"))
        block.scalar(mk("ac"))



def build_program():
    nc = bass.Bass("TRN2", target_bir_lowering=False)
    sc = Sched()
    es_top = ExitStack()
    taps = {}

    def din(name, shape, dtype):
        return nc.dram_tensor(name, shape, dtype, kind="ExternalInput").ap()

    xp_d = din("xp", [H, NTOK], F32)
    xt_d = din("xt", [H, S], F32)
    wq_d = din("wq", [NH, NKT, 128, 128], F32)
    wk_d = din("wk", [NKT, 128, 128], F32)
    wv_d = din("wv", [NKT, 128, 128], F32)
    wo_d = din("wo", [NKT, NKT, 128, 128], F32)
    qg_d = din("qg", [512, INTER], I32)
    qu_d = din("qu", [512, INTER], I32)
    qd_d = din("qd", [1376, H], I32)
    sg_d = din("sg", [32, INTER], F32)
    su_d = din("su", [32, INTER], F32)
    sd_d = din("sd", [86, H], F32)
    zg_d = din("zg", [32, 1376], I32)
    zu_d = din("zu", [32, 1376], I32)
    zd_d = din("zd", [86, 512], I32)
    gg_d = din("gg", [H], I32)
    gu_d = din("gu", [H], I32)
    gd_d = din("gd", [INTER], I32)
    amask_d = din("amask", [128, NST * 64], BF16)
    abias_d = din("abias", [128, NH * NST], F32)
    aqsl_d = din("aqsl", [NH, NTOK], BF16)
    idn_d = din("idn", [128, 128], BF16)
    onesr_d = din("onesr", [1, 128], BF16)
    onesc_d = din("onesc", [128, 1], BF16)
    onesrf_d = din("onesrf", [1, 128], F32)
    out_d = nc.dram_tensor("out", [H, NTOK], F32, kind="ExternalOutput").ap()
    ysp_d = nc.dram_tensor("yspill", [H, NTOK], F32, kind="Internal").ap()

    def mksb(stack, name, shape, dtype):
        t = stack.enter_context(nc.sbuf_tensor(name, shape, dtype))
        allsb[name] = t
        taps[name] = t
        return t

    allsb = {}

    def phase(name):
        for tname in _DEBUG_TAPS.get(name, []) if isinstance(_DEBUG_TAPS, dict) else []:
            t = allsb[tname]
            sh = list(t.shape)
            td = nc.dram_tensor(f"tap_{name}_{tname}", sh, t.dtype,
                                kind="ExternalOutput").ap()
            sc.dma("sy", td, t[:], reads=[tname, tname + "0", tname + "1"])
        sc.snap(name)

    pA = es_top.enter_context(nc.psum_tensor("pA", [128, 1024], F32))
    pB = es_top.enter_context(nc.psum_tensor("pB", [128, 1024], F32))
    pC1 = es_top.enter_context(nc.psum_tensor("pC1", [128, 512], F32))
    pC2 = es_top.enter_context(nc.psum_tensor("pC2", [128, 512], F32))
    pD = es_top.enter_context(nc.psum_tensor("pD", [128, 512], F32))
    pE = es_top.enter_context(nc.psum_tensor("pE", [128, 1024], BF16))
    PC = [pC1, pC2]

    g = ExitStack()
    xP = mksb(g, "xP", [128, NKT * 1024], BF16)
    ones_r = mksb(g, "ones_r", [1, 128], BF16)
    ones_c = mksb(g, "ones_c", [128, 1], BF16)
    ones_rf = mksb(g, "ones_rf", [1, 128], F32)
    idn_sb = mksb(g, "idn_sb", [128, 128], BF16)

    sc.dma("sy", idn_sb[:], idn_d, writes=["idn"])
    sc.dma("sy", ones_r[:], onesr_d, writes=["ones_r"])
    sc.dma("sy", ones_c[:], onesc_d, writes=["ones_c"])
    sc.dma("sy", ones_rf[:], onesrf_d, writes=["ones_rf"])

    att = ExitStack()
    ktb = mksb(att, "ktb", [128, S], BF16)
    vtok = mksb(att, "vtok", [128, S], BF16)
    qhat = mksb(att, "qhat", [128, 1024 * 2], BF16)
    probs = mksb(att, "probs", [128, 512 * 3], BF16)
    wstg = mksb(att, "wstg", [128, 1024 * 2], F32)
    wbf = mksb(att, "wbf", [128, 4096 * 2], BF16)
    sumr = mksb(att, "sumr", [1, 1024], F32)
    sumx = mksb(att, "sumx", [32, 1024], F32)
    rsx = mksb(att, "rsx", [32, 1024], F32)
    rsxb = mksb(att, "rsxb", [32, 1024], BF16)
    rs0 = mksb(att, "rs0", [1, 1024 * 2], BF16)
    aqs = mksb(att, "aqs", [1, 1024 * 2], BF16)
    amask = mksb(att, "amask_sb", [128, NST * 64], BF16)
    abias = mksb(att, "abias_sb", [128, NH * NST], F32)
    p1 = ExitStack()
    xstg = mksb(p1, "xstg", [128, 1024 * 2], F32)
    xtb = mksb(p1, "xtb", [128, 1024 * 2], BF16)
    wkb = mksb(p1, "wkb", [128, 4096], BF16)
    wvb = mksb(p1, "wvb", [128, 4096], BF16)

    sc.dma("sy", amask[:], amask_d, writes=["amask"])
    sc.dma("sy", abias[:], abias_d, writes=["abias"])
    phase("p0")

    for kt in range(NKT):
        sl = kt % 2
        sc.dma("sy", xstg[:, sl * 1024:sl * 1024 + 1024],
               xp_d[kt * 128:kt * 128 + 128, :], writes=[f"xstg{sl}"])
        sc.op("ve", lambda e, kt=kt, sl=sl: e.tensor_copy(
            out=xP[:, kt * 1024:(kt + 1) * 1024],
            in_=xstg[:, sl * 1024:sl * 1024 + 1024]),
            reads=[f"xstg{sl}"], writes=["xP"])

    phase("px")
    for t, (wsrc, wdst, key) in enumerate([(wk_d, wkb, "wkb"), (wv_d, wvb, "wvb")]):
        for q4 in range(4):
            sl = q4 % 2
            sc.dma("sy", xstg[:, sl * 1024:sl * 1024 + 1024].rearrange(
                       "p (t c) -> p t c", c=128),
                   wsrc[q4 * 8:(q4 + 1) * 8].rearrange("t p c -> p t c"),
                   writes=[f"xstg{sl}"])
            sc.op("ve", lambda e, q4=q4, sl=sl, wdst=wdst: e.tensor_copy(
                out=wdst[:, q4 * 1024:(q4 + 1) * 1024],
                in_=xstg[:, sl * 1024:sl * 1024 + 1024]),
                reads=[f"xstg{sl}"], writes=[key])

    phase("pw")
    for sh in range(2):
        for kt in range(NKT):
            sl = kt % 2
            sc.dma("sy", xstg[:, sl * 1024:sl * 1024 + 1024],
                   xt_d[kt * 128:kt * 128 + 128, sh * 1024:sh * 1024 + 1024],
                   writes=[f"xstg{sl}"])
            sc.op("ve", lambda e, sl=sl: e.tensor_copy(
                out=xtb[:, sl * 1024:sl * 1024 + 1024],
                in_=xstg[:, sl * 1024:sl * 1024 + 1024]),
                reads=[f"xstg{sl}"], writes=[f"xtb{sl}"])
            for c in range(2):
                sc.op("pe", lambda e, kt=kt, sl=sl, c=c: e.matmul(
                    pA[:, c * 512:(c + 1) * 512],
                    wkb[:, kt * 128:(kt + 1) * 128],
                    xtb[:, sl * 1024 + c * 512:sl * 1024 + (c + 1) * 512],
                    start=(kt == 0), stop=(kt == NKT - 1)),
                    reads=[f"xtb{sl}", "wkb"], writes=[f"pA{c}"])
            for i in range(8):
                sc.op("pe", lambda e, kt=kt, sl=sl, i=i: e.matmul(
                    pB[:, i * 128:(i + 1) * 128],
                    xtb[:, sl * 1024 + i * 128:sl * 1024 + (i + 1) * 128],
                    wvb[:, kt * 128:(kt + 1) * 128],
                    start=(kt == 0 and i % 4 == 0), stop=(kt == NKT - 1)),
                    reads=[f"xtb{sl}", "wvb"], writes=[f"pB{i // 4}"])
        phase(f"kv{sh}mm")
        sc.op("ac", lambda e, sh=sh: e.activation(
            out=ktb[:, sh * 1024:(sh + 1) * 1024], in_=pA[:], func=AF.Copy),
            reads=["pA0", "pA1"], writes=["ktb"])
        sc.op("ac", lambda e, sh=sh: e.activation(
            out=vtok[:, sh * 1024:(sh + 1) * 1024], in_=pB[:], func=AF.Copy),
            reads=["pB0", "pB1"], writes=["vtok"])
        phase(f"kv{sh}ev")

    phase("p1")
    p1.close()
    avu = mksb(att, "avu", [128, NH * 1024], BF16)
    for h in range(NH):
        wsl = h % 2
        for q4 in range(4):
            ssl = q4 % 2
            sc.dma("sy", wstg[:, ssl * 1024:ssl * 1024 + 1024].rearrange(
                       "p (t c) -> p t c", c=128),
                   wq_d[h, q4 * 8:(q4 + 1) * 8].rearrange("t p c -> p t c"),
                   writes=[f"wstg{ssl}"])
            sc.op("ve", lambda e, q4=q4, ssl=ssl, wsl=wsl: e.tensor_copy(
                out=wbf[:, wsl * 4096 + q4 * 1024:wsl * 4096 + (q4 + 1) * 1024],
                in_=wstg[:, ssl * 1024:ssl * 1024 + 1024]),
                reads=[f"wstg{ssl}"], writes=[f"wbf{wsl}"])
        sc.dma("gp", aqs[0:1, wsl * 1024:wsl * 1024 + 1024], aqsl_d[h:h + 1, :],
               writes=[f"aqs{wsl}"])
        for kt in range(NKT):
            for c in range(2):
                sc.op("pe", lambda e, kt=kt, c=c, wsl=wsl: e.matmul(
                    pA[:, c * 512:(c + 1) * 512],
                    wbf[:, wsl * 4096 + kt * 128:wsl * 4096 + kt * 128 + 128],
                    xP[:, kt * 1024 + c * 512:kt * 1024 + (c + 1) * 512],
                    start=(kt == 0), stop=(kt == NKT - 1)),
                    reads=[f"wbf{wsl}", "xP"], writes=[f"pA{c}"])
        qsl = h % 2
        sc.op("ac", lambda e, qsl=qsl: e.activation(
            out=qhat[:, qsl * 1024:(qsl + 1) * 1024], in_=pA[:], func=AF.Copy,
            scale=SCALE), reads=["pA0", "pA1"], writes=[f"qhat{qsl}"])
        units = []
        for j in range(NST):
            for ci, (off, w) in enumerate(att_chunks(j)):
                units.append((j, ci, off, w))

        def att_scores(pi):
            j, ci, off, w = units[pi]
            spi = pi % 2
            skey = f"pC{spi + 1}"
            sc.op("pe", lambda e, j=j, off=off, w=w, spi=spi, qsl=qsl: e.matmul(
                PC[spi][:, 0:w], ktb[:, j * 128:(j + 1) * 128],
                qhat[:, qsl * 1024 + off:qsl * 1024 + off + w],
                start=True, stop=False),
                reads=["ktb", f"qhat{qsl}"], writes=[skey])
            sc.op("pe", lambda e, off=off, w=w, spi=spi, wsl=wsl: e.matmul(
                PC[spi][:, 0:w], ones_r[0:1, :],
                aqs[0:1, wsl * 1024 + off:wsl * 1024 + off + w],
                start=False, stop=True),
                reads=["ones_r", f"aqs{wsl}"], writes=[skey])

        att_scores(0)
        for pi in range(len(units)):
            j, ci, off, w = units[pi]
            spi = pi % 2
            ppi = pi % 3
            skey = f"pC{spi + 1}"
            if ci == 0:
                sc.op("ve", lambda e, j=j, spi=spi: e.tensor_tensor(
                    out=PC[spi][:, 0:64], in0=PC[spi][:, 0:64],
                    in1=amask[:, j * 64:j * 64 + 64], op=ALU.add),
                    reads=[skey, "amask"], writes=[skey])
            sc.op("ac", lambda e, j=j, w=w, spi=spi, ppi=ppi, h=h: e.activation(
                out=probs[:, ppi * 512:ppi * 512 + w], in_=PC[spi][:, 0:w],
                func=AF.Exp, bias=abias[:, h * 16 + j:h * 16 + j + 1]),
                reads=[skey, "abias"], writes=[f"probs{ppi}"])
            if pi + 1 < len(units):
                att_scores(pi + 1)
            bank = off // 512
            stop_s = (j == 7 and ci == 0) or (j == 15)
            sc.op("pe", lambda e, off=off, w=w, ppi=ppi, j=j, stop_s=stop_s: e.matmul(
                pA[0:1, off:off + w], ones_c[:, 0:1],
                probs[:, ppi * 512:ppi * 512 + w],
                start=(j == 0), stop=stop_s),
                reads=["ones_c", f"probs{ppi}"], writes=[f"pA{bank}"])
            sc.op("pe", lambda e, off=off, w=w, ppi=ppi, j=j, stop_s=stop_s: e.matmul(
                pB[:, off:off + w], vtok[:, j * 128:(j + 1) * 128],
                probs[:, ppi * 512:ppi * 512 + w],
                start=(j == 0), stop=stop_s),
                reads=["vtok", f"probs{ppi}"], writes=[f"pB{bank}"])
        sc.op("ac", lambda e, h=h: e.activation(
            out=avu[:, h * 1024:(h + 1) * 1024], in_=pB[:], func=AF.Copy),
            reads=["pB0", "pB1"], writes=["avu"])
        sc.op("ac", lambda e: e.activation(out=sumr[:], in_=pA[0:1, :], func=AF.Copy),
              reads=["pA0", "pA1"], writes=["sumr"])
        sc.dma("gp", sumx[h:h + 1, :], sumr[:], reads=["sumr"], writes=["sumx"])

    sc.op("ve", lambda e: e.reciprocal(out=rsx[:], in_=sumx[:]),
          reads=["sumx"], writes=["rsx"])
    sc.op("ve", lambda e: e.tensor_copy(out=rsxb[:], in_=rsx[:]),
          reads=["rsx"], writes=["rsxb"])
    for h in range(NH):
        sl = h % 2
        sc.dma("gp", rs0[0:1, sl * 1024:sl * 1024 + 1024], rsxb[h:h + 1, :],
               reads=["rsxb"], writes=[f"rs0{sl}"])
        for c in range(2):
            sc.op("pe", lambda e, sl=sl, c=c: e.matmul(
                pA[:, c * 512:(c + 1) * 512], ones_r[0:1, :],
                rs0[0:1, sl * 1024 + c * 512:sl * 1024 + (c + 1) * 512],
                start=True, stop=True),
                reads=[f"rs0{sl}", "ones_r"], writes=[f"pA{c}"])
        sc.op("ve", lambda e, h=h: e.tensor_tensor(
            out=avu[:, h * 1024:(h + 1) * 1024],
            in0=avu[:, h * 1024:(h + 1) * 1024], in1=pA[:], op=ALU.mult),
            reads=["pA0", "pA1", "avu"], writes=["avu"])

    phase("p2")
    for nt in range(NKT):
        wsl = nt % 2
        for q4 in range(4):
            ssl = q4 % 2
            sc.dma("sy", wstg[:, ssl * 1024:ssl * 1024 + 1024].rearrange(
                       "p (t c) -> p t c", c=128),
                   wo_d[nt, q4 * 8:(q4 + 1) * 8].rearrange("t p c -> p t c"),
                   writes=[f"wstg{ssl}"])
            sc.op("ve", lambda e, q4=q4, ssl=ssl, wsl=wsl: e.tensor_copy(
                out=wbf[:, wsl * 4096 + q4 * 1024:wsl * 4096 + (q4 + 1) * 1024],
                in_=wstg[:, ssl * 1024:ssl * 1024 + 1024]),
                reads=[f"wstg{ssl}"], writes=[f"wbf{wsl}"])
        pk = "pB" if nt % 2 else "pA"
        ps = pB if nt % 2 else pA
        for dt_ in range(NKT):
            for c in range(2):
                sc.op("pe", lambda e, dt_=dt_, c=c, wsl=wsl, ps=ps: e.matmul(
                    ps[:, c * 512:(c + 1) * 512],
                    wbf[:, wsl * 4096 + dt_ * 128:wsl * 4096 + dt_ * 128 + 128],
                    avu[:, dt_ * 1024 + c * 512:dt_ * 1024 + (c + 1) * 512],
                    start=(dt_ == 0), stop=(dt_ == NKT - 1)),
                    reads=[f"wbf{wsl}", "avu"], writes=[f"{pk}{c}"])
        sc.op("ve", lambda e, nt=nt, ps=ps: e.tensor_tensor(
            out=xP[:, nt * 1024:(nt + 1) * 1024],
            in0=ps[:], in1=xP[:, nt * 1024:(nt + 1) * 1024], op=ALU.add),
            reads=[f"{pk}0", f"{pk}1", "xP"], writes=["xP"])

    phase("p3")
    sc.barrier()
    att.close()
    ln = ExitStack()
    sqb = mksb(ln, "sqb", [128, 1024 * 2], BF16)
    scr = mksb(ln, "scr", [128, 1024 * 2], F32)
    rowm = mksb(ln, "rowm", [1, 1024], F32)
    rowa = mksb(ln, "rowa", [1, 1024], F32)
    rowb = mksb(ln, "rowb", [1, 1024], F32)

    def layernorm(dst_is_out, outst=None):
        for kt in range(NKT):
            sl = kt % 2
            sc.op("ac", lambda e, kt=kt, sl=sl: e.activation(
                out=sqb[:, sl * 1024:(sl + 1) * 1024],
                in_=xP[:, kt * 1024:(kt + 1) * 1024], func=AF.Square),
                reads=["xP"], writes=[f"sqb{sl}"])
            for c in range(2):
                sc.op("pe", lambda e, kt=kt, c=c: e.matmul(
                    pA[0:1, c * 512:(c + 1) * 512], ones_c[:, 0:1],
                    xP[:, kt * 1024 + c * 512:kt * 1024 + (c + 1) * 512],
                    start=(kt == 0), stop=(kt == NKT - 1)),
                    reads=["ones_c", "xP"], writes=[f"pA{c}"])
                sc.op("pe", lambda e, kt=kt, c=c, sl=sl: e.matmul(
                    pB[0:1, c * 512:(c + 1) * 512], ones_c[:, 0:1],
                    sqb[:, sl * 1024 + c * 512:sl * 1024 + (c + 1) * 512],
                    start=(kt == 0), stop=(kt == NKT - 1)),
                    reads=["ones_c", f"sqb{sl}"], writes=[f"pB{c}"])
        sc.op("ve", lambda e: e.tensor_scalar(
            out=rowm[:], in0=pA[0:1, :], scalar1=1.0 / H, scalar2=None, op0=ALU.mult),
            reads=["pA0", "pA1"], writes=["rowm"])
        sc.op("ve", lambda e: e.tensor_tensor(
            out=rowa[:], in0=rowm[:], in1=rowm[:], op=ALU.mult),
            reads=["rowm"], writes=["rowa"])
        sc.op("ve", lambda e: e.scalar_tensor_tensor(
            out=rowb[:], in0=pB[0:1, :], scalar=1.0 / H, in1=rowa[:],
            op0=ALU.mult, op1=ALU.subtract),
            reads=["pB0", "pB1", "rowa"], writes=["rowb"])
        sc.op("ve", lambda e: e.tensor_scalar(
            out=rowb[:], in0=rowb[:], scalar1=EPS, scalar2=None, op0=ALU.add),
            reads=["rowb"], writes=["rowb"])
        sc.op("ac", lambda e: e.activation(out=rowa[:], in_=rowb[:], func=AF.Sqrt),
              reads=["rowb"], writes=["rowa"])
        sc.op("ve", lambda e: e.reciprocal(out=rowb[:], in_=rowa[:]),
              reads=["rowa"], writes=["rowb"])
        sc.op("ve", lambda e: e.tensor_tensor(
            out=rowa[:], in0=rowm[:], in1=rowb[:], op=ALU.mult),
            reads=["rowm", "rowb"], writes=["rowa"])
        for c in range(2):
            sc.op("pe", lambda e, c=c: e.matmul(
                pA[:, c * 512:(c + 1) * 512], ones_rf[0:1, :],
                rowb[0:1, c * 512:(c + 1) * 512], start=True, stop=True),
                reads=["ones_rf", "rowb"], writes=[f"pA{c}"])
            sc.op("pe", lambda e, c=c: e.matmul(
                pB[:, c * 512:(c + 1) * 512], ones_rf[0:1, :],
                rowa[0:1, c * 512:(c + 1) * 512], start=True, stop=True),
                reads=["ones_rf", "rowa"], writes=[f"pB{c}"])
        for kt in range(NKT):
            sl = kt % 2
            sc.op("ve", lambda e, kt=kt, sl=sl: e.tensor_tensor(
                out=scr[:, sl * 1024:(sl + 1) * 1024],
                in0=xP[:, kt * 1024:(kt + 1) * 1024], in1=pA[:], op=ALU.mult),
                reads=["xP", "pA0", "pA1"], writes=[f"scr{sl}"])
            if dst_is_out:
                o = sc.op("ve", lambda e, kt=kt, sl=sl: e.tensor_tensor(
                    out=outst[:, sl * 1024:(sl + 1) * 1024],
                    in0=scr[:, sl * 1024:(sl + 1) * 1024], in1=pB[:], op=ALU.subtract),
                    reads=[f"scr{sl}", "pB0", "pB1"], writes=[f"outst{sl}"])
                sc.dma("sy", out_d[kt * 128:kt * 128 + 128, :],
                       outst[:, sl * 1024:(sl + 1) * 1024],
                       reads=[f"outst{sl}"], writes=[f"outd{kt}"])
            else:
                sc.op("ve", lambda e, kt=kt, sl=sl: e.tensor_tensor(
                    out=xP[:, kt * 1024:(kt + 1) * 1024],
                    in0=scr[:, sl * 1024:(sl + 1) * 1024], in1=pB[:], op=ALU.subtract),
                    reads=[f"scr{sl}", "pB0", "pB1"], writes=["xP"])

    layernorm(False)
    phase("ln1")

    ffn = ExitStack()
    mtg = mksb(ffn, "mtg", [32, NKT * 128], BF16)
    mtu = mksb(ffn, "mtu", [32, NKT * 128], BF16)
    mgm = mksb(ffn, "mgm", [128, NKT * 32], BF16)
    mum = mksb(ffn, "mum", [128, NKT * 32], BF16)
    qzg = mksb(ffn, "qzg", [32, 1376], I32)
    qzu = mksb(ffn, "qzu", [32, 1376], I32)
    qzd = mksb(ffn, "qzd", [86, 512], I32)
    gio32 = mksb(ffn, "gio32", [32, 1], F32)
    gio86 = mksb(ffn, "gio86", [86, 1], F32)
    gior32 = mksb(ffn, "gior32", [128, 32], F32)
    gior86 = mksb(ffn, "gior86", [128, 86], F32)
    gcoli = mksb(ffn, "gcoli", [128, 32], I32)
    gcolf = mksb(ffn, "gcolf", [128, 32], F32)
    xgg = mksb(ffn, "xgg", [32, 1024], BF16)
    xgu = mksb(ffn, "xgu", [32, 1024], BF16)
    hga = mksb(ffn, "hga", [86, 1024], F32)
    hgn = mksb(ffn, "hgn", [86, 1024], BF16)
    sdbf = mksb(ffn, "sdbf", [86, H], BF16)
    zsd = mksb(ffn, "zsd", [86, H], BF16)
    sdf = mksb(ffn, "sdf", [86, 512], F32)
    hsb = mksb(ffn, "hsb", [128, 11 * 1024], BF16)
    qstg = mksb(ffn, "qstg", [128, 256 * 3], I32)
    t1b = mksb(ffn, "t1b", [128, 256 * 2], I32)
    w1s = mksb(ffn, "w1s", [128, 256 * 2], BF16)
    sgf = mksb(ffn, "sgf", [32, 256 * 2], F32)
    sgb = mksb(ffn, "sgb", [32, 256 * 3], BF16)
    zib = mksb(ffn, "zib", [32, 256], I32)
    zsb = mksb(ffn, "zsb", [32, 256 * 3], BF16)
    ysin = mksb(ffn, "ysin", [128, 512 * 4], F32)
    ysout = mksb(ffn, "ysout", [128, 512 * 2], F32)
    mtdp = mksb(ffn, "mtdp", [86, 11 * 128], BF16)
    mdp = mksb(ffn, "mdp", [128, 11 * 86], BF16)
    gdbi = mksb(ffn, "gdbi", [1, 128 * 2], I32)
    gdbb = mksb(ffn, "gdbb", [1, 128 * 2], BF16)
    grwi = mksb(ffn, "grwi", [1, 512], I32)
    grwb = mksb(ffn, "grwb", [1, 512], BF16)

    sc.dma("sy", qzg[:], zg_d, writes=["qzg"])
    sc.dma("sy", qzu[:], zu_d, writes=["qzu"])
    sc.dma("sy", qzd[:], zd_d, writes=["qzd"])
    sc.op("gp", lambda e: e.iota(gio32[:], [[1, 1]], base=0, channel_multiplier=1,
                                 allow_small_or_imprecise_dtypes=True), writes=["gio32"])
    sc.op("gp", lambda e: e.iota(gio86[:], [[1, 1]], base=0, channel_multiplier=1,
                                 allow_small_or_imprecise_dtypes=True), writes=["gio86"])
    sc.op("gp", lambda e: e.iota(gior32[:], [[1, 32]], base=0, channel_multiplier=0,
                                 allow_small_or_imprecise_dtypes=True), writes=["gior32"])
    sc.op("gp", lambda e: e.iota(gior86[:], [[1, 86]], base=0, channel_multiplier=0,
                                 allow_small_or_imprecise_dtypes=True), writes=["gior86"])

    for name, g_dram, mt, mm in [("g", gg_d, mtg, mgm), ("u", gu_d, mtu, mum)]:
        for ch in range(8):
            sc.dma("sy", grwi[0:1, 0:512],
                   g_dram.rearrange("(o x) -> o x", o=1)[0:1, ch * 512:(ch + 1) * 512],
                   writes=["grwi"])
            sc.op("ve", lambda e: e.tensor_copy(
                out=grwb[0:1, 0:512],
                in_=grwi[0:1, 0:512]),
                reads=["grwi"], writes=["grwb"])
            sc.op("pe", lambda e: e.matmul(
                pD[0:32, 0:512], ones_r[0:1, 0:32],
                grwb[0:1, 0:512], start=True, stop=True),
                reads=["grwb", "ones_r"], writes=["pD"])
            for s4 in range(4):
                kt = ch * 4 + s4
                sc.op("ve", lambda e, kt=kt, s4=s4, mt=mt: e.tensor_scalar(
                    out=mt[:, kt * 128:(kt + 1) * 128],
                    in0=pD[0:32, s4 * 128:(s4 + 1) * 128],
                    scalar1=gio32[:, 0:1], scalar2=None, op0=ALU.is_equal),
                    reads=["pD", "gio32"], writes=[f"mt{name}"])
        for kt in range(NKT):
            sc.dma("sy", gcoli[:, kt:kt + 1],
                   g_dram.rearrange("(a b) -> a b", b=1)[kt * 128:(kt + 1) * 128, :],
                   writes=[f"gcoli{kt}"])
        gcl = sc.op("ve", lambda e: e.tensor_copy(out=gcolf[:], in_=gcoli[:]),
                    reads=[f"gcoli{kt}" for kt in range(NKT)], writes=["gcolf"])
        for kt in range(NKT):
            sc.op("ve", lambda e, kt=kt, mm=mm: e.tensor_scalar(
                out=mm[:, kt * 32:(kt + 1) * 32], in0=gior32[:],
                scalar1=gcolf[:, kt:kt + 1], scalar2=None, op0=ALU.is_equal),
                reads=["gior32", "gcolf"], writes=[f"m{name}m"])

    for ch in range(8):
        sc.dma("sy", sdf[0:86, 0:512], sd_d[:, ch * 512:(ch + 1) * 512],
               writes=["sdf"])
        sc.op("ve", lambda e, ch=ch: e.tensor_copy(
            out=sdbf[:, ch * 512:(ch + 1) * 512], in_=sdf[0:86, 0:512]),
            reads=["sdf"], writes=["sdbf"])
        sc.op("ve", lambda e, ch=ch: e.tensor_scalar(
            out=t1b[0:86, 0:512], in0=qzd[:],
            scalar1=4 * ch, scalar2=15,
            op0=ALU.logical_shift_right, op1=ALU.bitwise_and),
            reads=["qzd"], writes=["t1b0", "t1b1"])
        sc.op("ve", lambda e, ch=ch: e.scalar_tensor_tensor(
            out=zsd[:, ch * 512:(ch + 1) * 512], in0=t1b[0:86, 0:512], scalar=1.0,
            in1=sdf[0:86, 0:512], op0=ALU.add, op1=ALU.mult),
            reads=["t1b0", "t1b1", "sdf"], writes=["zsd"])

    for name, mm, xg in [("g", mgm, xgg), ("u", mum, xgu)]:
        for c in range(2):
            for kt in range(NKT):
                sc.op("pe", lambda e, kt=kt, c=c, mm=mm: e.matmul(
                    pD[0:32, 0:512], mm[:, kt * 32:(kt + 1) * 32],
                    xP[:, kt * 1024 + c * 512:kt * 1024 + (c + 1) * 512],
                    start=(kt == 0), stop=(kt == NKT - 1)),
                    reads=[f"m{name}m", "xP"], writes=["pD"])
            sc.op("ve", lambda e, c=c, xg=xg: e.tensor_scalar(
                out=xg[:, c * 512:(c + 1) * 512], in0=pD[0:32, 0:512],
                scalar1=-1.0, scalar2=None, op0=ALU.mult),
                reads=["pD"], writes=[f"xg{name}"])

    phase("xg")
    ACC = [("pA0", pA, 0), ("pA1", pA, 512), ("pB0", pB, 0), ("pB1", pB, 512)]

    for part in range(NPART):
        d = part
        for b in range(11):
            rows = SLAB_ROWS[b]
            gsl = b % 2
            sc.dma("sy", gdbi[0:1, gsl * 128:gsl * 128 + rows],
                   gd_d.rearrange("(o x) -> o x", o=1)[0:1, d * 1376 + b * 128:d * 1376 + b * 128 + rows],
                   writes=[f"gdbi{gsl}"])
            sc.op("ve", lambda e, gsl=gsl, rows=rows: e.tensor_copy(
                out=gdbb[0:1, gsl * 128:gsl * 128 + rows],
                in_=gdbi[0:1, gsl * 128:gsl * 128 + rows]),
                reads=[f"gdbi{gsl}"], writes=[f"gdbb{gsl}"])
            sc.op("pe", lambda e, b=b, rows=rows, gsl=gsl: e.matmul(
                pD[0:86, 0:rows], ones_r[0:1, 0:86],
                gdbb[0:1, gsl * 128:gsl * 128 + rows], start=True, stop=True),
                reads=[f"gdbb{gsl}", "ones_r"], writes=["pD"])
            sc.op("ve", lambda e, b=b, rows=rows: e.tensor_scalar(
                out=mtdp[:, b * 128:b * 128 + rows], in0=pD[0:86, 0:rows],
                scalar1=gio86[:, 0:1], scalar2=None, op0=ALU.is_equal),
                reads=["pD", "gio86"], writes=["mtdp"])
            sc.dma("sy", gcoli[0:rows, b:b + 1],
                   gd_d.rearrange("(a b) -> a b", b=1)[d * 1376 + b * 128:d * 1376 + b * 128 + rows, :],
                   writes=[f"gcoli{b}"])
        sc.op("ve", lambda e: e.tensor_copy(out=gcolf[:, 0:11], in_=gcoli[:, 0:11]),
              reads=[f"gcoli{b}" for b in range(11)], writes=["gcolf"])
        for b in range(11):
            rows = SLAB_ROWS[b]
            sc.op("ve", lambda e, b=b, rows=rows: e.tensor_scalar(
                out=mdp[0:rows, b * 86:(b + 1) * 86], in0=gior86[0:rows, :],
                scalar1=gcolf[0:rows, b:b + 1], scalar2=None, op0=ALU.is_equal),
                reads=["gior86", "gcolf"], writes=["mdp"])

        MATP = [(qg_d, sg_d, qzg, mtg, xgg, "xgg", "qzg", "mtg"),
                (qu_d, su_d, qzu, mtu, xgu, "xgu", "qzu", "mtu")]
        kts = [(j * 4 + kpb, kpb, j) for kpb in range(4) for j in range(8)]

        def gu_prep(blk):
            if blk >= 12:
                return
            gi, mat = blk // 2, blk % 2
            b0, nsl, width = GRPS[gi]
            gcol = d * 1376 + b0 * 128
            q_dram, s_dram, qz, mt, xg, xgk, qzk, mtk = MATP[mat]
            bsl = (part * 12 + blk) % 3
            bslf = (part * 12 + blk) % 2
            sc.dma("sy", sgf[0:32, bslf * 256:bslf * 256 + width],
                   s_dram[:, gcol:gcol + width], writes=[f"sgf{bslf}"])
            sc.op("ve", lambda e, bsl=bsl, bslf=bslf, width=width: e.tensor_copy(
                out=sgb[0:32, bsl * 256:bsl * 256 + width],
                in_=sgf[0:32, bslf * 256:bslf * 256 + width]),
                reads=[f"sgf{bslf}"], writes=[f"sgb{bsl}"])
            sc.op("ve", lambda e, qz=qz, d=d, b0=b0, width=width: e.tensor_scalar(
                out=zib[0:32, 0:width],
                in0=qz[:, b0 * 128:b0 * 128 + width], scalar1=4 * d, scalar2=15,
                op0=ALU.logical_shift_right, op1=ALU.bitwise_and),
                reads=[qzk], writes=["zib"])
            sc.op("ve", lambda e, bsl=bsl, bslf=bslf, width=width: e.scalar_tensor_tensor(
                out=zsb[0:32, bsl * 256:bsl * 256 + width],
                in0=zib[0:32, 0:width], scalar=1.0,
                in1=sgf[0:32, bslf * 256:bslf * 256 + width],
                op0=ALU.add, op1=ALU.mult),
                reads=["zib", f"sgf{bslf}"], writes=[f"zsb{bsl}"])

        def gq_load(L):
            if L >= 48:
                return
            lblk, kpb = divmod(L, 4)
            lgi, lmat = lblk // 2, lblk % 2
            lb0, _, lwidth = GRPS[lgi]
            lgcol = d * 1376 + lb0 * 128
            lq_dram = MATP[lmat][0]
            qsl = L % 3
            sc.dma("sy", qstg[:, qsl * 256:qsl * 256 + lwidth],
                   lq_dram[kpb * 128:kpb * 128 + 128, lgcol:lgcol + lwidth],
                   writes=[f"qstg{qsl}"])

        gu_prep(0)
        gq_load(0)
        gq_load(1)
        for blk in range(12):
            gi, mat = blk // 2, blk % 2
            b0, nsl, width = GRPS[gi]
            gcol = d * 1376 + b0 * 128
            q_dram, s_dram, qz, mt, xg, xgk, qzk, mtk = MATP[mat]
            bsl = (part * 12 + blk) % 3
            if True:
                gu_prep(blk + 1)

                def sexp_mm(i, mt=mt, bsl=bsl, width=width, mtk=mtk):
                    kt = kts[i][0]
                    sesl = i % 2
                    sc.op("pe", lambda e, kt=kt, mt=mt, bsl=bsl, sesl=sesl, width=width: e.matmul(
                        PC[sesl][:, 0:width], mt[:, kt * 128:(kt + 1) * 128],
                        sgb[0:32, bsl * 256:bsl * 256 + width],
                        start=True, stop=True),
                        reads=[mtk, f"sgb{bsl}"], writes=[f"pC{sesl + 1}"])

                sexp_mm(0)
                for i, (kt, kpb, j) in enumerate(kts):
                    qsl = (blk * 4 + kpb) % 3
                    if j == 0:
                        gq_load(blk * 4 + kpb + 2)
                    tsl = i % 2
                    sesl = i % 2
                    sc.op("ve", lambda e, j=j, qsl=qsl, tsl=tsl, width=width: e.tensor_scalar(
                        out=t1b[:, tsl * 256:tsl * 256 + width],
                        in0=qstg[:, qsl * 256:qsl * 256 + width],
                        scalar1=4 * j, scalar2=15,
                        op0=ALU.logical_shift_right, op1=ALU.bitwise_and),
                        reads=[f"qstg{qsl}"], writes=[f"t1b{tsl}"])
                    sc.op("ve", lambda e, tsl=tsl, sesl=sesl, width=width: e.scalar_tensor_tensor(
                        out=w1s[:, tsl * 256:tsl * 256 + width],
                        in0=t1b[:, tsl * 256:tsl * 256 + width], scalar=1.0,
                        in1=PC[sesl][:, 0:width], op0=ALU.mult, op1=ALU.mult),
                        reads=[f"t1b{tsl}", f"pC{sesl + 1}"], writes=[f"w1s{tsl}"])
                    if i + 1 < len(kts):
                        sexp_mm(i + 1)
                    for c in range(2):
                        for s in range(nsl):
                            ak, aps, aoff = ACC[c * 2 + s]
                            rows = SLAB_ROWS[b0 + s]
                            sc.op("pe", lambda e, kt=kt, c=c, s=s, tsl=tsl, rows=rows,
                                  aps=aps, aoff=aoff: e.matmul(
                                aps[0:rows, aoff:aoff + 512],
                                w1s[:, tsl * 256 + s * 128:tsl * 256 + s * 128 + rows],
                                xP[:, kt * 1024 + c * 512:kt * 1024 + (c + 1) * 512],
                                start=(kt == 0), stop=False),
                                reads=[f"w1s{tsl}", "xP"], writes=[ak])
                for c in range(2):
                    for s in range(nsl):
                        ak, aps, aoff = ACC[c * 2 + s]
                        rows = SLAB_ROWS[b0 + s]
                        sc.op("pe", lambda e, c=c, s=s, rows=rows, bsl=bsl, aps=aps,
                              aoff=aoff, xg=xg: e.matmul(
                            aps[0:rows, aoff:aoff + 512],
                            zsb[0:32, bsl * 256 + s * 128:bsl * 256 + s * 128 + rows],
                            xg[:, c * 512:(c + 1) * 512],
                            start=False, stop=True),
                            reads=[f"zsb{bsl}", xgk], writes=[ak])
                        hoff = (b0 + s) * 1024 + c * 512
                        if mat == 0:
                            sc.op("ac", lambda e, rows=rows, hoff=hoff, aps=aps, aoff=aoff: e.activation(
                                out=hsb[0:rows, hoff:hoff + 512],
                                in_=aps[0:rows, aoff:aoff + 512], func=AF.Silu),
                                reads=[ak], writes=["hsb"])
                            if part == 0 and gi == 0 and c == 1 and s == nsl - 1:
                                phase("dbg_g0")
                        else:
                            sc.op("ve", lambda e, rows=rows, hoff=hoff, aps=aps, aoff=aoff: e.tensor_tensor(
                                out=hsb[0:rows, hoff:hoff + 512],
                                in0=hsb[0:rows, hoff:hoff + 512],
                                in1=aps[0:rows, aoff:aoff + 512], op=ALU.mult),
                                reads=[ak, "hsb"], writes=["hsb"])

        for c in range(2):
            for b in range(11):
                rows = SLAB_ROWS[b]
                sc.op("pe", lambda e, b=b, c=c, rows=rows: e.matmul(
                    pD[0:86, 0:512], mdp[0:rows, b * 86:(b + 1) * 86],
                    hsb[0:rows, b * 1024 + c * 512:b * 1024 + (c + 1) * 512],
                    start=(b == 0), stop=(b == 10)),
                    reads=["mdp", "hsb"], writes=["pD"])
            if part == 0:
                sc.op("ve", lambda e, c=c: e.tensor_copy(
                    out=hga[:, c * 512:(c + 1) * 512], in_=pD[0:86, 0:512]),
                    reads=["pD"], writes=["hga"])
            else:
                sc.op("ve", lambda e, c=c: e.tensor_tensor(
                    out=hga[:, c * 512:(c + 1) * 512],
                    in0=hga[:, c * 512:(c + 1) * 512], in1=pD[0:86, 0:512], op=ALU.add),
                    reads=["pD", "hga"], writes=["hga"])
        if part == NPART - 1:
            sc.op("ve", lambda e: e.tensor_scalar(
                out=hgn[:], in0=hga[:], scalar1=-1.0, scalar2=None, op0=ALU.mult),
                reads=["hga"], writes=["hgn"])

        def ys_prefetch(nb_):
            if part == 0 or nb_ >= 16:
                return
            for nt2 in range(2):
                for c in range(2):
                    ntg = nb_ * 2 + nt2
                    usl = nt2 * 2 + c
                    sc.dma("gp", ysin[:, usl * 512:usl * 512 + 512],
                           ysp_d[ntg * 128:ntg * 128 + 128, c * 512:(c + 1) * 512],
                           reads=[f"ysp{ntg}_{c}"], writes=[f"ysin{usl}"])

        def dq_load(Di):
            if Di >= 176:
                return
            dnb, db = divmod(Di, 11)
            drows = SLAB_ROWS[db]
            dqsl = Di % 3
            sc.dma("sy", qstg[0:drows, dqsl * 256:dqsl * 256 + 256],
                   qd_d[db * 128:db * 128 + drows, dnb * 256:dnb * 256 + 256],
                   writes=[f"qstg{dqsl}"])

        pending_drain = []
        dq_load(0)
        dq_load(1)
        for nb in range(16):
            ncol = nb * 256
            def dsexp_mm(b, ncol=ncol):
                rows = SLAB_ROWS[b]
                sesl = b % 2
                sc.op("pe", lambda e, b=b, rows=rows, sesl=sesl, ncol=ncol: e.matmul(
                    PC[sesl][0:rows, 0:256], mtdp[:, b * 128:b * 128 + rows],
                    sdbf[:, ncol:ncol + 256], start=True, stop=True),
                    reads=["mtdp", "sdbf"], writes=[f"pC{sesl + 1}"])

            dsexp_mm(0)
            for b in range(11):
                rows = SLAB_ROWS[b]
                Di = nb * 11 + b
                dq_load(Di + 2)
                if b == 0:
                    while pending_drain:
                        pending_drain.pop(0)()
                if b == 4:
                    ys_prefetch(nb)
                qsl = Di % 3
                tsl = b % 2
                sc.op("ve", lambda e, d=d, rows=rows, qsl=qsl, tsl=tsl: e.tensor_scalar(
                    out=t1b[0:rows, tsl * 256:tsl * 256 + 256],
                    in0=qstg[0:rows, qsl * 256:qsl * 256 + 256],
                    scalar1=4 * d, scalar2=15,
                    op0=ALU.logical_shift_right, op1=ALU.bitwise_and),
                    reads=[f"qstg{qsl}"], writes=[f"t1b{tsl}"])
                sesl = b % 2
                sc.op("ve", lambda e, rows=rows, tsl=tsl, sesl=sesl: e.scalar_tensor_tensor(
                    out=w1s[0:rows, tsl * 256:tsl * 256 + 256],
                    in0=t1b[0:rows, tsl * 256:tsl * 256 + 256], scalar=1.0,
                    in1=PC[sesl][0:rows, 0:256], op0=ALU.mult, op1=ALU.mult),
                    reads=[f"t1b{tsl}", f"pC{sesl + 1}"], writes=[f"w1s{tsl}"])
                if b + 1 < 11:
                    dsexp_mm(b + 1)
                for nt2 in range(2):
                    for c in range(2):
                        ak, aps, aoff = ACC[nt2 * 2 + c]
                        sc.op("pe", lambda e, b=b, rows=rows, nt2=nt2, c=c, tsl=tsl,
                              aps=aps, aoff=aoff: e.matmul(
                            aps[:, aoff:aoff + 512],
                            w1s[0:rows, tsl * 256 + nt2 * 128:tsl * 256 + nt2 * 128 + 128],
                            hsb[0:rows, b * 1024 + c * 512:b * 1024 + (c + 1) * 512],
                            start=(b == 0), stop=(b == 10 and part != NPART - 1)),
                            reads=[f"w1s{tsl}", "hsb"], writes=[ak])
            for nt2 in range(2):
                for c in range(2):
                    ak, aps, aoff = ACC[nt2 * 2 + c]
                    ntg = nb * 2 + nt2
                    if part == NPART - 1:
                        sc.op("pe", lambda e, ntg=ntg, c=c, aps=aps, aoff=aoff: e.matmul(
                            aps[:, aoff:aoff + 512],
                            zsd[:, ntg * 128:ntg * 128 + 128],
                            hgn[:, c * 512:(c + 1) * 512], start=False, stop=True),
                            reads=["zsd", "hgn"], writes=[ak])
                    usl = nt2 * 2 + c
                    wsl = (nt2 * 2 + c) % 2
                    if part < NPART - 1:
                        def drain(ak=ak, aps=aps, aoff=aoff, ntg=ntg, c=c, wsl=wsl,
                                  usl=usl, first=(part == 0)):
                            sc.op("ac", lambda e: e.activation(
                                out=ysout[:, wsl * 512:wsl * 512 + 512],
                                in_=aps[:, aoff:aoff + 512], func=AF.Copy),
                                reads=[ak], writes=[f"ysout{wsl}"])
                            if not first:
                                sc.op("ve", lambda e: e.tensor_tensor(
                                    out=ysout[:, wsl * 512:wsl * 512 + 512],
                                    in0=ysout[:, wsl * 512:wsl * 512 + 512],
                                    in1=ysin[:, usl * 512:usl * 512 + 512], op=ALU.add),
                                    reads=[f"ysout{wsl}", f"ysin{usl}"], writes=[f"ysout{wsl}"])
                            sc.dma("gp", ysp_d[ntg * 128:ntg * 128 + 128, c * 512:(c + 1) * 512],
                                   ysout[:, wsl * 512:wsl * 512 + 512],
                                   reads=[f"ysout{wsl}"], writes=[f"ysp{ntg}_{c}"])
                    else:
                        def drain(usl=usl, ak=ak, aps=aps, aoff=aoff, ntg=ntg, c=c):
                            ssl = c % 2
                            sc.op("ve", lambda e, ssl=ssl: e.tensor_tensor(
                                out=scr[:, ssl * 1024:ssl * 1024 + 512],
                                in0=ysin[:, usl * 512:usl * 512 + 512],
                                in1=aps[:, aoff:aoff + 512], op=ALU.add),
                                reads=[ak, f"ysin{usl}"], writes=[f"scr{ssl}"])
                            sc.op("ve", lambda e, ntg=ntg, c=c, ssl=ssl: e.tensor_tensor(
                                out=xP[:, ntg * 1024 + c * 512:ntg * 1024 + (c + 1) * 512],
                                in0=scr[:, ssl * 1024:ssl * 1024 + 512],
                                in1=xP[:, ntg * 1024 + c * 512:ntg * 1024 + (c + 1) * 512],
                                op=ALU.add),
                                reads=[f"scr{ssl}", "xP"], writes=["xP"])
                    pending_drain.append(drain)
        while pending_drain:
            pending_drain.pop(0)()

        phase(f"part{part}")
    sc.barrier()
    ffn.close()

    p6 = ExitStack()
    outst = mksb(p6, "outst", [128, 1024 * 2], F32)
    layernorm(True, outst)
    p6.close()
    ln.close()

    sems = {}
    es_sem = ExitStack()
    for e in ENGS:
        sems[e] = es_sem.enter_context(nc.semaphore("s_" + e))
        sems["bulk_" + e] = es_sem.enter_context(nc.semaphore("b_" + e))

    def mksem(name):
        return es_sem.enter_context(nc.semaphore(name[:24]))

    with nc.Block() as block:
        sc.emit(nc, block, sems, mksem, prefix=_STOP_AFTER)
    es_sem.close()
    es_top.close()
    g.close()
    return nc, taps



_PROG = None


def _get_prog():
    global _PROG
    if _PROG is None:
        _PROG = build_program()
    return _PROG


def kernel(L0_attn_hidden_states, L0_attn_Wq, L0_attn_Wkv, L0_attn_Wo,
           L0_ffn_gate_qweight, L0_ffn_gate_scales, L0_ffn_gate_qzeros, L0_ffn_gate_g_idx,
           L0_ffn_up_qweight, L0_ffn_up_scales, L0_ffn_up_qzeros, L0_ffn_up_g_idx,
           L0_ffn_down_qweight, L0_ffn_down_scales, L0_ffn_down_qzeros, L0_ffn_down_g_idx):
    x = np.asarray(L0_attn_hidden_states, dtype=np.float32)
    Wq = np.asarray(L0_attn_Wq, dtype=np.float32)
    Wkv = np.asarray(L0_attn_Wkv, dtype=np.float32)
    Wo = np.asarray(L0_attn_Wo, dtype=np.float32)

    slopes = _alibi_slopes(NH)
    permH, permI = PERM_H, PERM_I

    Wq_p = Wq[permH, :]
    wq = np.ascontiguousarray(
        Wq_p.reshape(NKT, 128, NKT, 128).transpose(2, 0, 1, 3))
    Wkv_p = Wkv[permH, :]
    wk = np.ascontiguousarray(Wkv_p[:, :128].reshape(NKT, 128, 128))
    wv = np.ascontiguousarray(Wkv_p[:, 128:].reshape(NKT, 128, 128))
    wo = np.ascontiguousarray(
        Wo[:, permH].reshape(NKT, 128, NKT, 128).transpose(2, 0, 1, 3))
    qg = np.ascontiguousarray(np.asarray(L0_ffn_gate_qweight, dtype=np.int32)[:, permI])
    qu = np.ascontiguousarray(np.asarray(L0_ffn_up_qweight, dtype=np.int32)[:, permI])
    qd = np.ascontiguousarray(np.asarray(L0_ffn_down_qweight, dtype=np.int32)[:, permH])
    sg = np.ascontiguousarray(np.asarray(L0_ffn_gate_scales, dtype=np.float32)[:, permI])
    su = np.ascontiguousarray(np.asarray(L0_ffn_up_scales, dtype=np.float32)[:, permI])
    sd = np.ascontiguousarray(np.asarray(L0_ffn_down_scales, dtype=np.float32)[:, permH])
    zg = np.ascontiguousarray(np.asarray(L0_ffn_gate_qzeros, dtype=np.int32))
    zu = np.ascontiguousarray(np.asarray(L0_ffn_up_qzeros, dtype=np.int32))
    zd = np.ascontiguousarray(np.asarray(L0_ffn_down_qzeros, dtype=np.int32))
    gg = np.ascontiguousarray(np.asarray(L0_ffn_gate_g_idx, dtype=np.int32)[permH])
    gu = np.ascontiguousarray(np.asarray(L0_ffn_up_g_idx, dtype=np.int32)[permH])
    gd = np.ascontiguousarray(np.asarray(L0_ffn_down_g_idx, dtype=np.int32)[permI])

    idn = np.eye(128, dtype=np.float32).astype(ml_dtypes.bfloat16)
    onesr = np.ones((1, 128), dtype=ml_dtypes.bfloat16)
    onesc = np.ones((128, 1), dtype=ml_dtypes.bfloat16)
    onesrf = np.ones((1, 128), dtype=np.float32)
    abias = np.empty((128, NH * NST), dtype=np.float32)
    p = np.arange(128, dtype=np.float32)
    for h in range(NH):
        for j in range(NST):
            abias[:, h * 16 + j] = slopes[h] * (128 * j + p)

    in_maps = []
    core_cols = []
    for core in range(8):
        b = core // 2
        strips = STRIPS_A if core % 2 == 0 else STRIPS_B
        cols = np.concatenate([np.arange(64 * s, 64 * s + 64) for s in strips])
        core_cols.append((b, cols))
        xt = np.ascontiguousarray(x[b].T[permH, :])
        xp = np.ascontiguousarray(xt[:, cols])
        amask = np.zeros((128, NST * 64), dtype=np.float32)
        for j in range(NST):
            dstrip = [s for s in strips if s in (2 * j, 2 * j + 1)][0]
            kk = 128 * j + p[:, None]
            qq = 64 * dstrip + np.arange(64)[None, :]
            amask[:, j * 64:(j + 1) * 64] = np.where(kk <= qq, 0.0, NEG)
        aqsl = (-slopes[:, None] * cols[None, :].astype(np.float32)).astype(ml_dtypes.bfloat16)
        in_maps.append(dict(
            xp=xp, xt=xt, wq=wq, wk=wk, wv=wv, wo=wo, qg=qg, qu=qu, qd=qd,
            sg=sg, su=su, sd=sd, zg=zg, zu=zu, zd=zd, gg=gg, gu=gu, gd=gd,
            amask=amask.astype(ml_dtypes.bfloat16), abias=abias, aqsl=aqsl,
            idn=idn, onesr=onesr, onesc=onesc, onesrf=onesrf))

    nc, taps = _get_prog()
    res = run_bass_kernel_spmd(nc, in_maps, core_ids=list(range(8)))

    inv = np.empty(H, dtype=np.int64)
    inv[permH] = np.arange(H)
    full = np.empty((B, S, H), dtype=np.float32)
    for core in range(8):
        b, cols = core_cols[core]
        y = np.asarray(res.results[core]["out"])
        full[b, cols, :] = y[inv, :].T
    if _DEBUG_TAPS:
        kernel._last_results = res
    return full


if __name__ == "__main__":
    nc, taps = build_program()
    print("build ok")
